# version 33
# baseline (speedup 1.0000x reference)
import numpy as np
from contextlib import ExitStack

import concourse.bass as bass
import concourse.mybir as mybir
from concourse.bass_utils import run_bass_kernel_spmd

F32 = mybir.dt.float32
BF16 = mybir.dt.bfloat16
ALU = mybir.AluOpType
AF = mybir.ActivationFunctionType
AX = mybir.AxisListType

S, E, INNER = 2048, 1024, 4096
NT = S // 128
NE = E // 128
NI = INNER // 128
NCORES = 8
M32 = 3 * 2.0**22
M64 = 3 * 2.0**51
EPS = 1e-5
SCALE_QK = 0.03125
NEG_BIG = -1e30
FFN_GRP = 4
N_GRP = NT // FFN_GRP


class Buf:
    __slots__ = ("wtok", "rtoks")

    def __init__(self):
        self.wtok = None
        self.rtoks = []


_ENG = ("pe", "dve", "act", "pool", "sp")
_DMA = {
    "sbias": "pool", "sx0": "sp", "sx1": "sp", "sx2": "sp",
    "sx20": "act", "sx21": "act",
    "swr2": "act", "sxn": "act", "sP": "act", "strP": "sp", "strx": "sp",
    "swl0": "sp", "swl1": "sp",
    "swq0": "sp", "swq1": "sp", "swq2": "sp", "swq3": "sp",
    "swl2": "sp", "swl3": "sp",
    "swt0": "sp", "swt1": "sp", "swt2": "sp", "swt3": "sp",
    "sfg0": "sp", "sfg1": "sp", "sfg2": "sp", "sfg3": "sp", "sfd0": "act", "sfd1": "act", "skh": "pool", "sy": "pool",
    "stq1": "sp", "stq2": "sp", "stq3": "sp",
    "sdbg": "pool",
}


class Prog:
    ENGS = _ENG
    QHOST = _DMA

    def __init__(self, nc, stack):
        self.nc = nc
        self.sems = {}
        self.cnt = {}
        for name in self.ENGS + tuple(self.QHOST):
            self.sems[name] = stack.enter_context(nc.semaphore(name))
            self.cnt[name] = 0
        self.water = {e: {c: 0 for c in self.ENGS + tuple(self.QHOST)}
                      for e in self.ENGS}
        self._stack = stack
        self.seg = {e: [] for e in self.ENGS}

    def host(self, chain):
        if chain.startswith("sdbg"):
            return "pool"
        return self.QHOST.get(chain, chain)

    def ensure_chain(self, name, stack):
        if name not in self.sems:
            self.sems[name] = stack.enter_context(self.nc.semaphore(name))
            self.cnt[name] = 0
            for e in self.ENGS:
                self.water[e][name] = 0

    def run(self, chain, fn, reads=(), writes=(), deps=()):
        host = self.host(chain)
        toks = list(deps)
        for b in reads:
            toks.append(b.wtok)
        for b in writes:
            toks.append(b.wtok)
            toks.extend(b.rtoks)
        best = {}
        for t in toks:
            if t is None:
                continue
            c, v = t
            if c == "pe" and chain == "pe":
                continue
            if best.get(c, 0) < v:
                best[c] = v
        for c in sorted(best):
            v = best[c]
            if c in _DMA or c.startswith("sdbg"):
                v = max(v, self.cnt[c])
            if self.water[host][c] < v:
                self.seg[host].append(("w", c, v))
                self.water[host][c] = v
        n = 16 if (chain in self.QHOST or chain.startswith("sdbg")) else 1
        self.cnt[chain] += n
        self.seg[host].append(("i", fn, chain, n))
        tok = (chain, self.cnt[chain])
        for b in reads:
            b.rtoks.append(tok)
        for b in writes:
            b.wtok = tok
            b.rtoks = []
        return tok

    def barrier(self):
        snap = dict(self.cnt)
        for e in self.ENGS:
            for c, v in snap.items():
                if v > self.water[e][c]:
                    self.seg[e].append(("w", c, v))
                    self.water[e][c] = v

    def flush(self):
        nc = self.nc
        segs = {e: lst for e, lst in self.seg.items() if lst}
        self.seg = {e: [] for e in self.ENGS}
        if not segs:
            return

        def replay(entries):
            def body(eng):
                for ent in entries:
                    if ent[0] == "w":
                        eng.wait_ge(self.sems[ent[1]], ent[2])
                    else:
                        _, fn, chain, n = ent
                        fn(eng).then_inc(self.sems[chain], n)
            return body

        with nc.Block() as b:
            for e, lst in segs.items():
                getattr(b, {"pe": "tensor", "dve": "vector", "act": "scalar",
                            "pool": "gpsimd", "sp": "sync"}[e])(replay(lst))


def build_core_program(dbg=()):
    nc = bass.Bass("TRN2", target_bir_lowering=False)
    big = ExitStack()
    with big:
        return _build(nc, big, set(dbg)), nc


def _build(nc, big, dbg):
    p = Prog(nc, big)

    x_d = nc.dram_tensor("x", [S, E], F32, kind="ExternalInput").ap()
    w_d = {}
    for nm, shp in (("wqT", [E, E]), ("wkT", [E, E]), ("wvT", [E, E]),
                    ("woT", [E, E]), ("wgT", [E, INNER]), ("wuT", [E, INNER]),
                    ("wdT", [INNER, E])):
        w_d[nm] = nc.dram_tensor(nm, shp, F32, kind="ExternalInput").ap()
    b_d = {nm: nc.dram_tensor(nm, [E], F32, kind="ExternalInput").ap()
           for nm in ("bq", "bk", "bv", "bo")}
    eye_d = nc.dram_tensor("eye", [128, 128], F32, kind="ExternalInput").ap()
    y_d = nc.dram_tensor("y", [S, E], F32, kind="ExternalOutput").ap()

    scr_xn = nc.dram_tensor("scr_xn", [S, E], BF16, kind="Internal").ap()
    scr_P = nc.dram_tensor("scr_P", [NT, 128, S], BF16, kind="Internal").ap()
    scr_x2 = nc.dram_tensor("scr_x2", [S, E], F32, kind="Internal").ap()
    scr_kh = nc.dram_tensor("scr_kh", [S, INNER], BF16, kind="Internal").ap()
    scr_t = {nm: nc.dram_tensor("scr_t" + nm, shp, BF16, kind="Internal").ap()
             for nm, shp in (("g", [E, INNER]), ("u", [E, INNER]),
                             ("d", [INNER, E]))}
    scr_wq = nc.dram_tensor("scr_wq", [3, 130], F32, kind="Internal").ap()

    dbg_d = {}

    def mkdbg(name):
        cname = "sdbg_" + name.replace("{", "").replace("}", "")
        p.ensure_chain(cname, big)
        return cname

    def dbg_out(name, shape, dt=F32):
        if name in dbg and name not in dbg_d:
            dbg_d[name] = nc.dram_tensor("dbg_" + name, list(shape), dt,
                                         kind="ExternalOutput").ap()
        return dbg_d.get(name)

    sbuf = lambda st, name, shape, dt: st.enter_context(
        nc.sbuf_tensor(name, shape, dt))
    psum = lambda st, name, shape: st.enter_context(
        nc.psum_tensor(name, shape, F32))

    ones_bf = sbuf(big, "ones_bf", [1, 128], BF16)
    eye_bf = sbuf(big, "eye_bf", [128, 128], BF16)
    gfl = sbuf(big, "gfl", [128, 1], F32)
    hfl = sbuf(big, "hfl", [128, 1], F32)
    bqs = sbuf(big, "bqs", [128, NE], F32)
    bks = sbuf(big, "bks", [128, NE], F32)
    bvrow = sbuf(big, "bvrow", [1, E], BF16)
    borow = sbuf(big, "borow", [1, E], BF16)
    x_sb = [sbuf(big, f"x_sb{i}", [128, E], F32) for i in range(2)]
    x2_sb = [sbuf(big, f"x2_sb{i}", [128, E], F32) for i in range(2)]
    bnst = [sbuf(big, f"bnst{i}", [128, 12], F32) for i in range(2)]
    bnm = [sbuf(big, f"bnm{i}", [128, 4], F32) for i in range(2)]
    qinv_x = sbuf(big, "qinv_x", [128, NT], F32)
    qinv_h = sbuf(big, "qinv_h", [128, NT], F32)
    Wbuf = [sbuf(big, f"Wbuf{i}", [128, 8 * 1024], BF16) for i in range(2)]
    epsc = sbuf(big, "epsc", [128, 1], F32)
    negc = sbuf(big, "negc", [128, 1], F32)
    smalls = sbuf(big, "smalls", [128, 16], F32)
    cthr = sbuf(big, "cthr", [128, 3], F32)
    ncthr = sbuf(big, "ncthr", [128, 3], F32)
    wmeanb = sbuf(big, "wmeanb", [128, 3], F32)

    B = {}

    def buf(name):
        if name not in B:
            B[name] = Buf()
        return B[name]

    p.run("dve", lambda e: e.memset(ones_bf[:], 1.0), writes=[buf("ones_bf")])
    p.run("dve", lambda e: e.memset(gfl[:], 2.0**-30), writes=[buf("gfl")])
    p.run("dve", lambda e: e.memset(epsc[:], EPS), writes=[buf("epsc")])
    p.run("dve", lambda e: e.memset(negc[:], -1.0), writes=[buf("negc")])
    p.run("dve", lambda e: e.tensor_scalar(out=gfl[:], in0=gfl[:], scalar1=1.0,
                                           scalar2=1.0, op0=ALU.add,
                                           op1=ALU.subtract),
          reads=[buf("gfl")], writes=[buf("gfl")])
    p.run("dve", lambda e: e.tensor_scalar(out=gfl[:], in0=gfl[:],
                                           scalar1=2.0**30, scalar2=None,
                                           op0=ALU.mult),
          reads=[buf("gfl")], writes=[buf("gfl")])
    p.run("dve", lambda e: e.tensor_scalar(out=hfl[:], in0=gfl[:], scalar1=-1.0,
                                           scalar2=1.0, op0=ALU.mult,
                                           op1=ALU.add),
          reads=[buf("gfl")], writes=[buf("hfl")])
    for ec in range(NE):
        p.run("sbias", lambda e, ec=ec: e.dma_start(
            out=bqs[:, ec:ec + 1], in_=b_d["bq"][ec * 128:(ec + 1) * 128, None]),
            writes=[buf(f"bqs{ec}")])
        p.run("sbias", lambda e, ec=ec: e.dma_start(
            out=bks[:, ec:ec + 1], in_=b_d["bk"][ec * 128:(ec + 1) * 128, None]),
            writes=[buf(f"bks{ec}")])
    p.run("sbias", lambda e: e.dma_start(out=eye_bf[:, :], in_=eye_d[:, :]),
          writes=[buf("eye_bf")])
    p.run("sbias", lambda e: e.dma_start(out=bvrow[0:1, :], in_=b_d["bv"][None, :]),
          writes=[buf("bvrow")])
    tok_bias = p.run("sbias", lambda e: e.dma_start(out=borow[0:1, :],
                                                    in_=b_d["bo"][None, :]),
                     writes=[buf("borow")])
    for _nm in [f"bqs{_ec}" for _ec in range(NE)] + \
            [f"bks{_ec}" for _ec in range(NE)] + ["bvrow", "borow", "eye_bf"]:
        buf(_nm).wtok = tok_bias

    def emit_ln(src_buf, src_ap, i):
        bs, bm = bnst[i], bnm[i]
        p.run("dve", lambda e: e.bn_stats(out=bs[:, 0:6], in_=src_ap[:, 0:512]),
              reads=[src_buf], writes=[buf(f"bnst{i}")])
        p.run("dve", lambda e: e.bn_stats(out=bs[:, 6:12], in_=src_ap[:, 512:1024]),
              reads=[src_buf], writes=[buf(f"bnst{i}")])
        p.run("dve", lambda e: e.bn_aggr(out=bm[:, 0:2], in_=bs[:, 0:12]),
              reads=[buf(f"bnst{i}")], writes=[buf(f"bnm{i}")])
        p.run("act", lambda e: e.activation(out=bm[:, 2:3], in_=bm[:, 1:2],
                                            func=AF.Sqrt, bias=epsc[:, 0:1],
                                            scale=1.0),
              reads=[buf(f"bnm{i}"), buf("epsc")], writes=[buf(f"bnm{i}")])
        p.run("dve", lambda e: e.reciprocal(out=bm[:, 3:4], in_=bm[:, 2:3]),
              reads=[buf(f"bnm{i}")], writes=[buf(f"bnm{i}")])
        return bm

    def emit_round_neg(src_buf, src_ap, sneg_ap, sneg_buf, out_buf, out_ap,
                       tmpa_buf, tmpa_ap, tmpr_buf, tmpr_ap):
        p.run("dve", lambda e: e.tensor_scalar(
            out=tmpa_ap, in0=src_ap, scalar1=sneg_ap, scalar2=M32,
            op0=ALU.mult, op1=ALU.add),
            reads=[src_buf, sneg_buf], writes=[tmpa_buf])
        p.run("dve", lambda e: e.tensor_scalar(
            out=tmpr_ap, in0=tmpa_ap, scalar1=M32, scalar2=hfl[:, 0:1],
            op0=ALU.subtract, op1=ALU.mult),
            reads=[tmpa_buf, buf("hfl")], writes=[tmpr_buf])
        p.run("dve", lambda e: e.tensor_scalar(
            out=tmpa_ap, in0=tmpa_ap, scalar1=M64 - M32, scalar2=None,
            op0=ALU.add),
            reads=[tmpa_buf], writes=[tmpa_buf])
        p.run("dve", lambda e: e.scalar_tensor_tensor(
            out=out_ap, in0=tmpa_ap, scalar=M64, in1=tmpr_ap,
            op0=ALU.subtract, op1=ALU.add),
            reads=[tmpa_buf, tmpr_buf], writes=[out_buf])

    segWQ = ExitStack()
    Wstg = [sbuf(segWQ, f"Wstg{i}", [128, 512], F32) for i in range(3)]
    Wmk = [sbuf(segWQ, f"Wmk{i}", [128, 512], BF16) for i in range(2)]
    wabs = [sbuf(segWQ, f"wabs{i}", [128, 512], F32) for i in range(2)]
    wacc = sbuf(segWQ, "wacc", [128, 512], F32)
    wrow = sbuf(segWQ, "wrow", [1, 128], F32)
    wbc = sbuf(segWQ, "wbc", [128, 1], F32)

    wq_thunks = []
    _wq_lseq = [0]

    def _wq_build():
        for wi, (nm, dram, rows, cols) in enumerate(
                (("g", w_d["wgT"], E, INNER), ("u", w_d["wuT"], E, INNER),
                 ("d", w_d["wdT"], INNER, E))):
            slices = [(ic, sub) for ic in range(rows // 128)
                      for sub in range(cols // 512)]
            nelem = float(rows * cols)
            wq_thunks.append((0, lambda: p.run(
                "dve", lambda e: e.memset(wacc[:], 0.0), writes=[buf("wacc")])))

            def loadA(j, ic, sub, jj, dram=dram):
                stg = buf(f"Wstg{jj}")
                p.run(f"swq{jj}", lambda e, ic=ic, sub=sub, jj=jj, dram=dram: e.dma_start(
                    out=Wstg[jj][:], in_=dram[ic * 128:(ic + 1) * 128,
                                              sub * 512:(sub + 1) * 512]),
                    writes=[stg])

            def passA(j, ic, sub, jj, dram=dram):
                ja = j % 2
                stg = buf(f"Wstg{jj}")
                p.run("dve", lambda e, jj=jj, ja=ja: e.scalar_tensor_tensor(
                    out=wabs[ja][:], in0=Wstg[jj][:], scalar=-1.0,
                    in1=Wstg[jj][:], op0=ALU.mult, op1=ALU.max),
                    reads=[stg], writes=[buf(f"wabs{ja}")])
                p.run("pool", lambda e, ja=ja: e.tensor_tensor(
                    out=wacc[:], in0=wacc[:], in1=wabs[ja][:], op=ALU.add),
                    reads=[buf(f"wabs{ja}"), buf("wacc")], writes=[buf("wacc")])

            for j, (ic, sub) in enumerate(slices):
                sl = _wq_lseq[0] % 3
                _wq_lseq[0] += 1
                wq_thunks.append((1, lambda j=j, ic=ic, sub=sub, sl=sl, f=loadA: f(j, ic, sub, sl),
                                  lambda j=j, ic=ic, sub=sub, sl=sl, f=passA: f(j, ic, sub, sl)))

            def mid(wi=wi, nelem=nelem):
                w = 512
                while w > 1:
                    h = w // 2
                    p.run("dve", lambda e, h=h, w=w: e.tensor_tensor(
                        out=wacc[:, 0:h], in0=wacc[:, 0:h], in1=wacc[:, h:w],
                        op=ALU.add),
                        reads=[buf("wacc")], writes=[buf("wacc")])
                    w = h
                p.run("swq0", lambda e: e.dma_start(
                    out=scr_wq[wi, :128, None], in_=wacc[:, 0:1]),
                    reads=[buf("wacc")], writes=[buf("scr_wq")])
                p.run("swq0", lambda e: e.dma_start(
                    out=wrow[0:1, :], in_=scr_wq[wi, None, :128]),
                    reads=[buf("scr_wq")], writes=[buf("wrow")])
                w = 128
                while w > 1:
                    h = w // 2
                    p.run("dve", lambda e, h=h, w=w: e.tensor_tensor(
                        out=wrow[0:1, 0:h], in0=wrow[0:1, 0:h], in1=wrow[0:1, h:w],
                        op=ALU.add),
                        reads=[buf("wrow")], writes=[buf("wrow")])
                    w = h
                p.run("swq0", lambda e: e.dma_start(
                    out=scr_wq[wi, 128:129, None], in_=wrow[0:1, 0:1]),
                    reads=[buf("wrow")], writes=[buf("scr_wq")])
                p.run("swq0", lambda e: e.dma_start(
                    out=wbc[:, 0:1],
                    in_=scr_wq[wi, 128:129, None].to_broadcast([128, 1])),
                    reads=[buf("scr_wq")], writes=[buf("wbc")])
                p.run("pool", lambda e: e.tensor_scalar(
                    out=wmeanb[:, wi:wi + 1], in0=wbc[:, 0:1], scalar1=1.0 / nelem,
                    scalar2=None, op0=ALU.mult),
                    reads=[buf("wbc")], writes=[buf("wmeanb")])
                p.run("pool", lambda e: e.tensor_scalar(
                    out=cthr[:, wi:wi + 1], in0=wbc[:, 0:1], scalar1=0.5 / nelem,
                    scalar2=None, op0=ALU.mult),
                    reads=[buf("wbc")], writes=[buf("cthr")])
                p.run("pool", lambda e: e.tensor_scalar(
                    out=ncthr[:, wi:wi + 1], in0=cthr[:, wi:wi + 1], scalar1=-1.0,
                    scalar2=None, op0=ALU.mult),
                    reads=[buf("cthr")], writes=[buf("ncthr")])

            wq_thunks.append((0, mid))

            def loadB(j, ic, sub, jj, dram=dram):
                stg = buf(f"Wstg{jj}")
                p.run(f"swq{jj}", lambda e, ic=ic, sub=sub, jj=jj, dram=dram: e.dma_start(
                    out=Wstg[jj][:], in_=dram[ic * 128:(ic + 1) * 128,
                                              sub * 512:(sub + 1) * 512]),
                    writes=[stg])

            def passB(j, ic, sub, jj, dram=dram, nm=nm, wi=wi):
                js = j % 2
                ja = j % 2
                stg = buf(f"Wstg{jj}")
                p.run("pool", lambda e, jj=jj, ja=ja, wi=wi: e.tensor_scalar(
                    out=wabs[ja][:], in0=Wstg[jj][:],
                    scalar1=cthr[:, wi:wi + 1], scalar2=None, op0=ALU.is_gt),
                    reads=[stg, buf("cthr")], writes=[buf(f"wabs{ja}")])
                p.run("pool", lambda e, jj=jj, js=js, wi=wi: e.tensor_scalar(
                    out=Wmk[js][:], in0=Wstg[jj][:],
                    scalar1=ncthr[:, wi:wi + 1], scalar2=None, op0=ALU.is_lt),
                    reads=[stg, buf("ncthr")], writes=[buf(f"Wmk{js}")])
                p.run("dve", lambda e, js=js, ja=ja: e.tensor_tensor(
                    out=Wmk[js][:], in0=Wmk[js][:], in1=wabs[ja][:],
                    op=ALU.subtract),
                    reads=[buf(f"Wmk{js}"), buf(f"wabs{ja}")],
                    writes=[buf(f"Wmk{js}")])
                p.run(f"swt{js}", lambda e, ic=ic, sub=sub, js=js, nm=nm: e.dma_start(
                    out=scr_t[nm][ic * 128:(ic + 1) * 128,
                                  sub * 512:(sub + 1) * 512], in_=Wmk[js][:]),
                    reads=[buf(f"Wmk{js}")], writes=[buf("scr_t" + nm)])

            for j, (ic, sub) in enumerate(slices):
                sl = _wq_lseq[0] % 3
                _wq_lseq[0] += 1
                wq_thunks.append((1, lambda j=j, ic=ic, sub=sub, sl=sl, f=loadB: f(j, ic, sub, sl),
                                  lambda j=j, ic=ic, sub=sub, sl=sl, f=passB: f(j, ic, sub, sl)))

    _wq_build()
    wq_load_pos = [0]
    wq_comp_pos = [0]

    def _wq_advance_loads(target):
        while wq_load_pos[0] < min(target, len(wq_thunks)):
            ent = wq_thunks[wq_load_pos[0]]
            if ent[0] == 1:
                ent[1]()
            wq_load_pos[0] += 1

    def drain_wq(k):
        while k > 0 and wq_comp_pos[0] < len(wq_thunks):
            i = wq_comp_pos[0]
            _wq_advance_loads(i + 3)
            ent = wq_thunks[i]
            if ent[0] == 1:
                ent[2]()
            else:
                ent[1]()
                wq_load_pos[0] = max(wq_load_pos[0], i + 1)
            wq_comp_pos[0] += 1
            k -= 1

    with ExitStack() as segB:
        Q_fm = sbuf(segB, "Q_fm", [128, NE * S], BF16)
        K_fm = sbuf(segB, "K_fm", [128, NE * S], BF16)
        V_tm = sbuf(segB, "V_tm", [128, NT * E], BF16)
        psA = psum(segB, "psA", [128, 512])
        psB = psum(segB, "psB", [128, 512])
        segB1 = ExitStack()
        xn1_fm = sbuf(segB1, "xn1_fm", [128, NE * S], BF16)
        xn1c = [sbuf(segB1, "xn1c0", [128, E], BF16)]
        psX = [segB1.enter_context(nc.psum_tensor(f"psX{i}", [128, 128], BF16))
               for i in range(2)]
        segWL = ExitStack()
        Wst32 = sbuf(segWL, "Wst32", [128, 2048], F32)

        wb0, wb1 = buf("Wbuf0"), buf("Wbuf1")

        wl_thunks = []
        _wl_seq = [0]

        def _wl_stage(hh):
            return Wst32[:, hh * 512:(hh + 1) * 512], buf(f"Wst32h{hh}")

        def _wl_build(dram, wb, wbuf, thunks=None):
            if thunks is None:
                thunks = wl_thunks
            for hq in range(16):
                h, q = hq // 2, hq % 2
                hh = _wl_seq[0] % 4
                _wl_seq[0] += 1

                def fn(h=h, q=q, hh=hh, dram=dram, wb=wb, wbuf=wbuf):
                    sap, sh = _wl_stage(hh)
                    p.run(f"swl{hh}", lambda e, h=h, q=q, sap=sap, dram=dram: e.dma_start(
                        out=sap,
                        in_=dram[h * 128:(h + 1) * 128, q * 512:(q + 1) * 512]),
                        writes=[sh])
                    p.run("dve", lambda e, h=h, q=q, sap=sap, wbuf=wbuf: e.tensor_copy(
                        out=wbuf[:, h * 1024 + q * 512: h * 1024 + q * 512 + 512],
                        in_=sap),
                        reads=[sh], writes=[wb])
                thunks.append(fn)

        _wl_build(w_d["wqT"], wb0, Wbuf[0])
        _wl_build(w_d["wkT"], wb1, Wbuf[1])

        def proj_fm(wbuf, wb, bias_sb, bias_pref, scale, out_sb, out_buf, t4):
            for oc in range(NE):
                bank, bk_ = (psA, buf("psA")) if oc % 2 == 0 \
                    else (psB, buf("psB"))
                for ic in range(NE):
                    tok = p.run("pe", lambda e, oc=oc, t4=t4, ic=ic, bank=bank,
                                wbuf=wbuf: e.matmul(
                        bank[:, :],
                        wbuf[:, ic * E + oc * 128: ic * E + oc * 128 + 128],
                        xn1_fm[:, ic * S + t4 * 512: ic * S + t4 * 512 + 512],
                        start=(ic == 0), stop=(ic == NE - 1)),
                        reads=[wb, buf("xn1_fm")],
                        writes=([bk_] if ic == 0 else []))
                    if ic == NE - 1:
                        bk_.wtok = tok
                if scale is None:
                    p.run("dve", lambda e, oc=oc, t4=t4, bank=bank: e.tensor_scalar(
                        out=out_sb[:, oc * S + t4 * 512: oc * S + t4 * 512 + 512],
                        in0=bank[:, :], scalar1=bias_sb[:, oc:oc + 1],
                        scalar2=None, op0=ALU.add),
                        reads=[bk_, buf(f"{bias_pref}{oc}")], writes=[out_buf])
                else:
                    p.run("dve", lambda e, oc=oc, t4=t4, bank=bank: e.tensor_scalar(
                        out=out_sb[:, oc * S + t4 * 512: oc * S + t4 * 512 + 512],
                        in0=bank[:, :], scalar1=bias_sb[:, oc:oc + 1],
                        scalar2=scale, op0=ALU.add, op1=ALU.mult),
                        reads=[bk_, buf(f"{bias_pref}{oc}")], writes=[out_buf])

        def issue_x(c):
            i = c % 2
            p.run(f"sx{i}", lambda e, c=c, i=i: e.dma_start(
                out=x_sb[i][:], in_=x_d[c * 128:(c + 1) * 128, :]),
                writes=[buf(f"x_sb{i}")])

        issue_x(0)
        issue_x(1)
        for c in range(NT):
            i = c % 2
            ib = c % 2
            xb, xnb = buf(f"x_sb{i}"), buf("xn1c0")
            bm = emit_ln(xb, x_sb[i][:], ib)
            p.run("dve", lambda e, i=i, ib=ib: e.tensor_scalar(
                out=xn1c[0][:], in0=x_sb[i][:], scalar1=bnm[ib][:, 0:1],
                scalar2=bnm[ib][:, 3:4], op0=ALU.subtract, op1=ALU.mult),
                reads=[xb, buf(f"bnm{ib}")], writes=[xnb])
            if c + 2 < NT:
                issue_x(c + 2)
            for ic in range(NE):
                tb = buf(f"psX{ic % 2}")
                tok = p.run("pe", lambda e, ic=ic: e.transpose(
                    psX[ic % 2][:, :], xn1c[0][:, ic * 128:(ic + 1) * 128],
                    eye_bf[:, :]),
                    reads=[xnb, buf("eye_bf")], writes=[tb])
                tb.wtok = tok
                p.run("act", lambda e, ic=ic, c=c: e.copy(
                    out=xn1_fm[:, ic * S + c * 128: ic * S + c * 128 + 128],
                    in_=psX[ic % 2][:, :]),
                    reads=[tb], writes=[buf("xn1_fm")])
            for _ in range(4):
                if wl_thunks:
                    wl_thunks.pop(0)()
            drain_wq(6)
            if c % 4 == 3:
                t4 = c // 4
                proj_fm(Wbuf[0], wb0, bqs, "bqs", SCALE_QK, Q_fm,
                        buf("Q_fm"), t4)
                if t4 >= 1:
                    proj_fm(Wbuf[1], wb1, bks, "bks", None, K_fm,
                            buf("K_fm"), t4 - 1)
        proj_fm(Wbuf[1], wb1, bks, "bks", None, K_fm, buf("K_fm"), 3)
        drain_wq(6)

        _wl_build(w_d["wvT"], wb0, Wbuf[0])
        for fn in wl_thunks:
            fn()
        wl_thunks = []
        wo_thunks = []
        _wl_build(w_d["woT"], wb1, Wbuf[1], thunks=wo_thunks)

        for c in range(NT):
            for oh in range(2):
                bank, bk_ = (psA, buf("psA")) if (c * 2 + oh) % 2 == 0 else (psB, buf("psB"))
                for ic in range(NE):
                    tok = p.run("pe", lambda e, c=c, oh=oh, ic=ic, bank=bank: e.matmul(
                        bank[:, :],
                        xn1_fm[:, ic * S + c * 128: ic * S + c * 128 + 128],
                        Wbuf[0][:, ic * E + oh * 512: ic * E + oh * 512 + 512],
                        start=(ic == 0), stop=False),
                        reads=[wb0, buf("xn1_fm")],
                        writes=([bk_] if ic == 0 else []))
                tok = p.run("pe", lambda e, oh=oh, bank=bank: e.matmul(
                    bank[:, :], ones_bf[0:1, :],
                    bvrow[0:1, oh * 512:(oh + 1) * 512], start=False, stop=True),
                    reads=[buf("ones_bf"), buf("bvrow")])
                bk_.wtok = tok
                p.run("act", lambda e, c=c, oh=oh, bank=bank: e.copy(
                    out=V_tm[:, c * E + oh * 512: c * E + oh * 512 + 512],
                    in_=bank[:, :]), reads=[bk_], writes=[buf("V_tm")])
            if wo_thunks:
                wo_thunks.pop(0)()
            drain_wq(4)
        p.flush()
        segWL.close()
        segB1.close()
        p.barrier()

        if dbg_out("q_fm", [128, NE * S], BF16) is not None:
            p.run("sdbg", lambda e: e.dma_start(out=dbg_d["q_fm"][:, :], in_=Q_fm[:]),
                  reads=[buf("Q_fm")])
        if dbg_out("v_tm", [128, NT * E], BF16) is not None:
            p.run("sdbg", lambda e: e.dma_start(out=dbg_d["v_tm"][:, :], in_=V_tm[:]),
                  reads=[buf("V_tm")])
        p.flush()

        with ExitStack() as segC:
            S_sb = [sbuf(segC, f"S_sb{i}", [128, S], BF16) for i in range(3)]
            P_sb = [sbuf(segC, f"P_sb{i}", [128, S], BF16) for i in range(2)]
            PT_g = sbuf(segC, "PT_g", [128, 16 * 256], BF16)
            attn_g = sbuf(segC, "attn_g", [128, NE * 256], BF16)
            xn2_sb = sbuf(segC, "xn2_sb", [128, E], F32)
            t8 = sbuf(segC, "t8", [128, 32], BF16)
            qr = sbuf(segC, "qr", [128, E], BF16)
            kx = sbuf(segC, "kx", [128, E], BF16)
            psC = [psum(segC, f"psC{i}", [128, 512]) for i in range(2)]
            psD = psum(segC, "psD", [128, 512])
            psE = psum(segC, "psE", [128, 512])
            psT = [st.enter_context(nc.psum_tensor(f"psT{i}", [128, 128], BF16))
                   for i, st in ((0, segC), (1, segC))]

            def sc_banks(tg):
                return ([(psA, buf("psA")), (psB, buf("psB"))] if tg % 2 == 0
                        else [(psC[0], buf("psC0")), (psC[1], buf("psC1"))])

            def emit_scores(tg):
                q0 = tg * 128
                pp = tg % 3
                banks = sc_banks(tg)
                for w in range(2):
                    for kh in range(2):
                        kc4 = w * 2 + kh
                        bank, bk_ = banks[kh]
                        for ic in range(NE):
                            tok = p.run("pe", lambda e, kc4=kc4, ic=ic, q0=q0, bank=bank: e.matmul(
                                bank[:, :],
                                Q_fm[:, ic * S + q0: ic * S + q0 + 128],
                                K_fm[:, ic * S + kc4 * 512: ic * S + kc4 * 512 + 512],
                                start=(ic == 0), stop=(ic == NE - 1)),
                                reads=[buf("Q_fm"), buf("K_fm")],
                                writes=([bk_] if ic == 0 else []))
                            if ic == NE - 1:
                                bk_.wtok = tok
                        p.run("act", lambda e, kc4=kc4, bank=bank, pp=pp: e.copy(
                            out=S_sb[pp][:, kc4 * 512:(kc4 + 1) * 512],
                            in_=bank[:, :]),
                            reads=[bk_], writes=[buf(f"S_sb{pp}")])

            def emit_softmax(tg):
                pp = tg % 3
                qq = tg % 2
                sb_, pb = buf(f"S_sb{pp}"), buf(f"P_sb{qq}")
                SC = P_sb[1 - qq]
                scb = buf(f"P_sb{1 - qq}")
                p.run("dve", lambda e, pp=pp, SC=SC: e.tensor_copy(out=SC[:], in_=S_sb[pp][:]),
                      reads=[sb_], writes=[scb])
                for r in range(4):
                    p.run("dve", lambda e, r=r, SC=SC: e.max(
                        out=t8[:, r * 8:(r + 1) * 8], in_=SC[:]),
                        reads=[scb], writes=[buf("t8")])
                    if r < 3:
                        p.run("dve", lambda e, r=r, SC=SC: e.match_replace(
                            out=SC[:], in_to_replace=t8[:, r * 8:(r + 1) * 8],
                            in_values=SC[:], imm_value=NEG_BIG),
                            reads=[buf("t8"), scb], writes=[scb])
                p.run("dve", lambda e: e.tensor_scalar(
                    out=smalls[:, 0:1], in0=t8[:, 0:1], scalar1=-1.0,
                    scalar2=None, op0=ALU.mult),
                    reads=[buf("t8")], writes=[buf("smalls")])
                p.run("dve", lambda e: e.tensor_copy(
                    out=smalls[:, 1:2], in_=t8[:, 31:32]),
                    reads=[buf("t8")], writes=[buf("smalls")])
                p.run("act", lambda e, pp=pp, qq=qq: e.activation(
                    out=P_sb[qq][:], in_=S_sb[pp][:], func=AF.Exp,
                    bias=smalls[:, 0:1], scale=1.0),
                    reads=[sb_, buf("smalls")], writes=[pb])
                p.run("dve", lambda e, pp=pp, qq=qq: e.scalar_tensor_tensor(
                    out=P_sb[qq][:], in0=S_sb[pp][:], scalar=smalls[:, 1:2],
                    in1=P_sb[qq][:], op0=ALU.is_ge, op1=ALU.mult),
                    reads=[sb_, buf("smalls"), pb], writes=[pb])
                p.run("dve", lambda e, qq=qq: e.reduce_sum(
                    out=smalls[:, 2:3], in_=P_sb[qq][:], axis=AX.X),
                    reads=[pb], writes=[buf("smalls")])
                p.run("dve", lambda e: e.reciprocal(
                    out=smalls[:, 3:4], in_=smalls[:, 2:3]),
                    reads=[buf("smalls")], writes=[buf("smalls")])
                p.run("act", lambda e, qq=qq: e.activation(
                    out=P_sb[qq][:], in_=P_sb[qq][:], func=AF.Copy,
                    bias=0.0, scale=smalls[:, 3:4]),
                    reads=[pb, buf("smalls")], writes=[pb])
                if dbg_out(f"p_{tg}", [128, S], BF16) is not None:
                    p.run(mkdbg(f"p_{tg}"), lambda e, tg=tg, qq=qq: e.dma_start(
                        out=dbg_d[f"p_{tg}"][:, :], in_=P_sb[qq][:]),
                        reads=[pb])

            def emit_ptrans_drain(tg):
                drain_wq(5)

            def emit_ptrans(tg):
                pp = tg % 2
                t = tg % 2
                pb = buf(f"P_sb{pp}")
                for kc in range(16):
                    tb = buf(f"psT{kc % 2}")
                    tok = p.run("pe", lambda e, kc=kc, pp=pp: e.transpose(
                        psT[kc % 2][:, :], P_sb[pp][:, kc * 128:(kc + 1) * 128],
                        eye_bf[:, :]),
                        reads=[pb, buf("eye_bf")], writes=[tb])
                    tb.wtok = tok
                    p.run("act", lambda e, kc=kc, t=t: e.copy(
                        out=PT_g[:, kc * 256 + t * 128: kc * 256 + t * 128 + 128],
                        in_=psT[kc % 2][:, :]),
                        reads=[tb], writes=[buf("PT_g")])

            def emit_attnv(g):
                for ec in range(NE):
                    bank, bk_ = (psD, buf("psD")) if ec % 2 == 0 else (psE, buf("psE"))
                    for kc in range(16):
                        tok = p.run("pe", lambda e, ec=ec, kc=kc, bank=bank: e.matmul(
                            bank[:, 0:256],
                            V_tm[:, kc * E + ec * 128: kc * E + ec * 128 + 128],
                            PT_g[:, kc * 256:(kc + 1) * 256],
                            start=(kc == 0), stop=(kc == 15)),
                            reads=[buf("V_tm"), buf("PT_g")],
                            writes=([bk_] if kc == 0 else []))
                        if kc == 15:
                            bk_.wtok = tok
                    p.run("act", lambda e, ec=ec, bank=bank: e.copy(
                        out=attn_g[:, ec * 256:(ec + 1) * 256], in_=bank[:, 0:256]),
                        reads=[bk_], writes=[buf("attn_g")])

            def emit_oproj_mm(g, t):
                tg = g * 2 + t
                i = tg % 2
                xb = buf(f"x_sb{i}")
                x2b = buf(f"x2_sb{i}")
                p.run(f"sx{i}", lambda e, tg=tg, i=i: e.dma_start(
                    out=x_sb[i][:], in_=x_d[tg * 128:(tg + 1) * 128, :]),
                    writes=[xb])
                for oh in range(2):
                    bank, bk_ = (psD, buf("psD")) if oh == 0 else (psE, buf("psE"))
                    for ec in range(NE):
                        p.run("pe", lambda e, t=t, oh=oh, ec=ec, bank=bank: e.matmul(
                            bank[:, :],
                            attn_g[:, ec * 256 + t * 128: ec * 256 + t * 128 + 128],
                            Wbuf[1][:, ec * E + oh * 512: ec * E + oh * 512 + 512],
                            start=(ec == 0), stop=False),
                            reads=[buf("attn_g"), buf("Wbuf1")],
                            writes=([bk_] if ec == 0 else []))
                    tok = p.run("pe", lambda e, oh=oh, bank=bank: e.matmul(
                        bank[:, :], ones_bf[0:1, :],
                        borow[0:1, oh * 512:(oh + 1) * 512],
                        start=False, stop=True),
                        reads=[buf("ones_bf"), buf("borow")])
                    bk_.wtok = tok
                    p.run("dve", lambda e, i=i, oh=oh, bank=bank: e.tensor_tensor(
                        out=x2_sb[i][:, oh * 512:(oh + 1) * 512],
                        in0=x_sb[i][:, oh * 512:(oh + 1) * 512],
                        in1=bank[:, :], op=ALU.add),
                        reads=[xb, bk_], writes=[x2b])
                p.run("swr2", lambda e, tg=tg, i=i: e.dma_start(
                    out=scr_x2[tg * 128:(tg + 1) * 128, :], in_=x2_sb[i][:]),
                    reads=[x2b], writes=[buf("scr_x2")])
                if dbg_out(f"x2_{tg}", [128, E]) is not None:
                    p.run(mkdbg(f"x2_{tg}"), lambda e, tg=tg, i=i: e.dma_start(
                        out=dbg_d[f"x2_{tg}"][:, :], in_=x2_sb[i][:]),
                        reads=[x2b])

            def emit_tail_ln2(g, t):
                tg = g * 2 + t
                i = tg % 2
                x2b = buf(f"x2_sb{i}")
                bm = emit_ln(x2b, x2_sb[i][:], i)
                p.run("dve", lambda e, i=i: e.tensor_scalar(
                    out=xn2_sb[:], in0=x2_sb[i][:], scalar1=bnm[i][:, 0:1],
                    scalar2=bnm[i][:, 3:4], op0=ALU.subtract, op1=ALU.mult),
                    reads=[x2b, buf(f"bnm{i}")], writes=[buf("xn2_sb")])
                p.run("dve", lambda e: e.reduce_max(
                    out=smalls[:, 4:5], in_=xn2_sb[:], axis=AX.X,
                    apply_absolute_value=True),
                    reads=[buf("xn2_sb")], writes=[buf("smalls")])
                p.run("dve", lambda e: e.tensor_scalar(
                    out=smalls[:, 5:6], in0=smalls[:, 4:5], scalar1=1e-5,
                    scalar2=None, op0=ALU.max),
                    reads=[buf("smalls")], writes=[buf("smalls")])
                p.run("dve", lambda e, tg=tg: e.tensor_scalar(
                    out=qinv_x[:, tg:tg + 1], in0=smalls[:, 5:6],
                    scalar1=1.0 / 127.0, scalar2=None, op0=ALU.mult),
                    reads=[buf("smalls")], writes=[buf("qinv_x")])
                p.run("dve", lambda e: e.reciprocal(
                    out=smalls[:, 6:7], in_=smalls[:, 5:6]),
                    reads=[buf("smalls")], writes=[buf("smalls")])
                p.run("dve", lambda e: e.tensor_scalar(
                    out=smalls[:, 7:8], in0=smalls[:, 6:7], scalar1=-127.0,
                    scalar2=None, op0=ALU.mult),
                    reads=[buf("smalls")], writes=[buf("smalls")])
                emit_round_neg(buf("xn2_sb"), xn2_sb[:], smalls[:, 7:8],
                               buf("smalls"), buf("kx"), kx[:],
                               buf("xn2_sb"), xn2_sb[:], buf("qr"), qr[:])
                p.run("sxn", lambda e, tg=tg: e.dma_start(
                    out=scr_xn[tg * 128:(tg + 1) * 128, :], in_=kx[:]),
                    reads=[buf("kx")], writes=[buf("scr_xn")])
                drain_wq(7)

            emit_scores(0)
            for g in range(8):
                t0, t1 = 2 * g, 2 * g + 1
                emit_scores(t1)
                emit_softmax(t0)
                drain_wq(5)
                if g >= 1:
                    emit_oproj_mm(g - 1, 0)
                    emit_oproj_mm(g - 1, 1)
                emit_ptrans(t0)
                if g >= 1:
                    emit_tail_ln2(g - 1, 0)
                if t1 + 1 < NT:
                    emit_scores(t1 + 1)
                emit_softmax(t1)
                drain_wq(5)
                emit_ptrans(t1)
                if g >= 1:
                    emit_tail_ln2(g - 1, 1)
                emit_attnv(g)
            emit_oproj_mm(7, 0)
            emit_oproj_mm(7, 1)
            emit_tail_ln2(7, 0)
            emit_tail_ln2(7, 1)
            p.flush()
        p.flush()

    p.barrier()
    drain_wq(10**9)
    p.flush()
    segWQ.close()
    p.barrier()
    if dbg_out("tern_g", [E, INNER], BF16) is not None:
        p.run("sdbg", lambda e: e.dma_start(out=dbg_d["tern_g"][:, :],
                                            in_=scr_t["g"][:, :]),
              reads=[buf("scr_tg")])
        p.flush()
    with ExitStack() as segE:
        xq_fm = sbuf(segE, "xq_fm", [128, NE * S], BF16)
        k_fm = sbuf(segE, "k_fm", [128, NI * 512], BF16)
        h_grp = [sbuf(segE, f"h_grp{i}", [128, INNER], F32) for i in range(FFN_GRP)]
        qr2 = sbuf(segE, "qr2", [128, 2048], BF16)
        kq = sbuf(segE, "kq", [128, 2048], BF16)
        y_sb = sbuf(segE, "y_sb", [128, E], F32)
        sg_sb = [sbuf(segE, f"sg_sb{i}", [128, 512], F32) for i in range(2)]
        sclg = sbuf(segE, "sclg", [128, NT], F32)
        sclu = sbuf(segE, "sclu", [128, NT], F32)
        scld = sbuf(segE, "scld", [128, NT], F32)
        psF = [psum(segE, f"psF{i}", [128, 512]) for i in range(8)]
        psb = [buf(f"psF{i}") for i in range(8)]

        for c in range(NT):
            p.run("dve", lambda e, c=c: e.tensor_scalar(
                out=sclg[:, c:c + 1], in0=qinv_x[:, c:c + 1],
                scalar1=wmeanb[:, 0:1], scalar2=None, op0=ALU.mult),
                reads=[buf("qinv_x"), buf("wmeanb")], writes=[buf("sclg")])
            p.run("dve", lambda e, c=c: e.tensor_scalar(
                out=sclu[:, c:c + 1], in0=qinv_x[:, c:c + 1],
                scalar1=wmeanb[:, 1:2], scalar2=None, op0=ALU.mult),
                reads=[buf("qinv_x"), buf("wmeanb")], writes=[buf("sclu")])

        TQ = ("strx", "stq1", "stq2", "stq3")
        for ec in range(NE):
            p.run(TQ[ec % 4], lambda e, ec=ec: e.dma_start_transpose(
                out=xq_fm[:, ec * S:(ec + 1) * S],
                in_=scr_xn[:, ec * 128:(ec + 1) * 128]),
                reads=[buf("scr_xn")], writes=[buf("xq_fm")])

        def emit_gateup(g):
            c0 = g * FFN_GRP
            for oc in range(8):
                hh = oc % 2
                goff = hh * 4096
                p.run(f"sfg{hh}", lambda e, oc=oc, goff=goff: e.dma_start(
                    out=Wbuf[0][:, goff:goff + 4096].rearrange(
                        "p (ic x) -> p ic x", x=512),
                    in_=scr_t["g"][:, oc * 512:(oc + 1) * 512].rearrange(
                        "(ic q) x -> q ic x", q=128)),
                    reads=[buf("scr_tg")], writes=[buf(f"Wb0h{hh}")])
                p.run(f"sfg{2 + hh}", lambda e, oc=oc, goff=goff: e.dma_start(
                    out=Wbuf[1][:, goff:goff + 4096].rearrange(
                        "p (ic x) -> p ic x", x=512),
                    in_=scr_t["u"][:, oc * 512:(oc + 1) * 512].rearrange(
                        "(ic q) x -> q ic x", q=128)),
                    reads=[buf("scr_tu")], writes=[buf(f"Wb1h{hh}")])
                for cc in range(FFN_GRP):
                    c = c0 + cc
                    bg, bgb = psF[(cc % 2) * 2], psb[(cc % 2) * 2]
                    bu, bub = psF[(cc % 2) * 2 + 1], psb[(cc % 2) * 2 + 1]
                    for ic in range(NE):
                        tok = p.run("pe", lambda e, ic=ic, c=c, bg=bg, goff=goff: e.matmul(
                            bg[:, :],
                            xq_fm[:, ic * S + c * 128: ic * S + c * 128 + 128],
                            Wbuf[0][:, goff + ic * 512: goff + (ic + 1) * 512],
                            start=(ic == 0), stop=(ic == NE - 1)),
                            reads=[buf("xq_fm"), buf(f"Wb0h{hh}")],
                            writes=([bgb] if ic == 0 else []))
                        if ic == NE - 1:
                            bgb.wtok = tok
                    for ic in range(NE):
                        tok = p.run("pe", lambda e, ic=ic, c=c, bu=bu, goff=goff: e.matmul(
                            bu[:, :],
                            xq_fm[:, ic * S + c * 128: ic * S + c * 128 + 128],
                            Wbuf[1][:, goff + ic * 512: goff + (ic + 1) * 512],
                            start=(ic == 0), stop=(ic == NE - 1)),
                            reads=[buf("xq_fm"), buf(f"Wb1h{hh}")],
                            writes=([bub] if ic == 0 else []))
                        if ic == NE - 1:
                            bub.wtok = tok
                    sgi = cc % 2
                    p.run("act", lambda e, c=c, sgi=sgi, bg=bg: e.activation(
                        out=sg_sb[sgi][:], in_=bg[:, :], func=AF.Silu,
                        bias=0.0, scale=sclg[:, c:c + 1]),
                        reads=[bgb, buf("sclg")], writes=[buf(f"sg_sb{sgi}")])
                    p.run("act", lambda e, c=c, cc=cc, oc=oc, bu=bu: e.activation(
                        out=h_grp[cc][:, oc * 512:(oc + 1) * 512], in_=bu[:, :],
                        func=AF.Copy, bias=0.0, scale=sclu[:, c:c + 1]),
                        reads=[bub, buf("sclu")], writes=[buf(f"h_grp{cc}")])
                    p.run("pool", lambda e, cc=cc, oc=oc, sgi=sgi: e.tensor_tensor(
                        out=h_grp[cc][:, oc * 512:(oc + 1) * 512],
                        in0=h_grp[cc][:, oc * 512:(oc + 1) * 512],
                        in1=sg_sb[sgi][:], op=ALU.mult),
                        reads=[buf(f"h_grp{cc}"), buf(f"sg_sb{sgi}")],
                        writes=[buf(f"h_grp{cc}")])

        def emit_quant(g):
            c0 = g * FFN_GRP
            for cc in range(FFN_GRP):
                c = c0 + cc
                hb = buf(f"h_grp{cc}")
                p.run("dve", lambda e, cc=cc: e.reduce_max(
                    out=smalls[:, 4:5], in_=h_grp[cc][:], axis=AX.X,
                    apply_absolute_value=True),
                    reads=[hb], writes=[buf("smalls")])
                p.run("dve", lambda e: e.tensor_scalar(
                    out=smalls[:, 5:6], in0=smalls[:, 4:5], scalar1=1e-5,
                    scalar2=None, op0=ALU.max),
                    reads=[buf("smalls")], writes=[buf("smalls")])
                p.run("dve", lambda e, c=c: e.tensor_scalar(
                    out=qinv_h[:, c:c + 1], in0=smalls[:, 5:6], scalar1=1.0 / 127.0,
                    scalar2=None, op0=ALU.mult),
                    reads=[buf("smalls")], writes=[buf("qinv_h")])
                p.run("dve", lambda e, c=c: e.tensor_scalar(
                    out=scld[:, c:c + 1], in0=qinv_h[:, c:c + 1],
                    scalar1=wmeanb[:, 2:3], scalar2=None, op0=ALU.mult),
                    reads=[buf("qinv_h"), buf("wmeanb")], writes=[buf("scld")])
                p.run("dve", lambda e: e.reciprocal(
                    out=smalls[:, 6:7], in_=smalls[:, 5:6]),
                    reads=[buf("smalls")], writes=[buf("smalls")])
                p.run("dve", lambda e: e.tensor_scalar(
                    out=smalls[:, 7:8], in0=smalls[:, 6:7], scalar1=-127.0,
                    scalar2=None, op0=ALU.mult),
                    reads=[buf("smalls")], writes=[buf("smalls")])
                for hh in range(2):
                    sl = slice(hh * 2048, (hh + 1) * 2048)
                    emit_round_neg(hb, h_grp[cc][:, sl], smalls[:, 7:8],
                                   buf("smalls"), buf("kq"), kq[:],
                                   hb, h_grp[cc][:, sl], buf("qr2"), qr2[:])
                    p.run("skh", lambda e, c=c, hh=hh: e.dma_start(
                        out=scr_kh[c * 128:(c + 1) * 128,
                                   hh * 2048:(hh + 1) * 2048], in_=kq[:]),
                        reads=[buf("kq")], writes=[buf("scr_kh")])

        def emit_kfm(g):
            c0 = g * FFN_GRP
            for ic in range(NI):
                p.run(TQ[ic % 4], lambda e, ic=ic, c0=c0: e.dma_start_transpose(
                    out=k_fm[:, ic * 512:(ic + 1) * 512],
                    in_=scr_kh[c0 * 128:(c0 + FFN_GRP) * 128,
                               ic * 128:(ic + 1) * 128]),
                    reads=[buf("scr_kh")], writes=[buf("k_fm")])

        def emit_down(g):
            c0 = g * FFN_GRP
            for w in range(2):
                for ib in range(4):
                    wdst = Wbuf[ib % 2]
                    wdbs = [buf(f"Wb{ib % 2}h0"), buf(f"Wb{ib % 2}h1")]
                    p.run(f"sfd{ib % 2}", lambda e, ib=ib, wdst=wdst: e.dma_start(
                        out=wdst[:, 0:8192].rearrange("p (ic x) -> p ic x", x=1024),
                        in_=scr_t["d"][ib * 1024:(ib + 1) * 1024, :].rearrange(
                            "(ic q) x -> q ic x", q=128)),
                        reads=[buf("scr_td")], writes=wdbs)
                    for ccw in range(2):
                        cc = w * 2 + ccw
                        for oh in range(2):
                            bi = 4 + ccw * 2 + oh
                            for jc in range(NE):
                                ic = ib * 8 + jc
                                tok = p.run("pe", lambda e, cc=cc, oh=oh, jc=jc, ic=ic, bi=bi, ib=ib, wdst=wdst: e.matmul(
                                    psF[bi][:, :],
                                    k_fm[:, ic * 512 + cc * 128: ic * 512 + cc * 128 + 128],
                                    wdst[:, jc * 1024 + oh * 512: jc * 1024 + oh * 512 + 512],
                                    start=(ib == 0 and jc == 0),
                                    stop=(ib == 3 and jc == NE - 1)),
                                    reads=[buf("k_fm")] + wdbs,
                                    writes=([psb[bi]] if (ib == 0 and jc == 0) else []))
                                if ib == 3 and jc == NE - 1:
                                    psb[bi].wtok = tok
                for ccw in range(2):
                    cc = w * 2 + ccw
                    c = c0 + cc
                    i = c % 2
                    x2b = buf(f"x2_sb{i}")
                    p.run(f"sx2{i}", lambda e, c=c, i=i: e.dma_start(
                        out=x2_sb[i][:], in_=scr_x2[c * 128:(c + 1) * 128, :]),
                        reads=[buf("scr_x2")], writes=[x2b])
                    for oh in range(2):
                        bi = 4 + ccw * 2 + oh
                        p.run("act", lambda e, c=c, oh=oh, bi=bi: e.activation(
                            out=y_sb[:, oh * 512:(oh + 1) * 512], in_=psF[bi][:, :],
                            func=AF.Copy, bias=0.0, scale=scld[:, c:c + 1]),
                            reads=[psb[bi], buf("scld")], writes=[buf("y_sb")])
                    p.run("dve", lambda e, i=i: e.tensor_tensor(
                        out=y_sb[:], in0=y_sb[:], in1=x2_sb[i][:], op=ALU.add),
                        reads=[buf("y_sb"), x2b], writes=[buf("y_sb")])
                    p.run("sy", lambda e, c=c: e.dma_start(
                        out=y_d[c * 128:(c + 1) * 128, :], in_=y_sb[:]),
                        reads=[buf("y_sb")], writes=[buf("y_out")])

        for g in range(N_GRP):
            emit_gateup(g)
            if g >= 1:
                emit_kfm(g - 1)
            emit_quant(g)
            if g >= 1:
                emit_down(g - 1)
        emit_kfm(N_GRP - 1)
        emit_down(N_GRP - 1)
        p.flush()

        p.run("pool", lambda e: e.nop(), deps=[buf("y_out").wtok])
        p.flush()

    return dbg_d



def kernel(L0_attn_x, L0_attn_Wq, L0_attn_bq, L0_attn_Wk, L0_attn_bk,
           L0_attn_Wv, L0_attn_bv, L0_attn_Wo, L0_attn_bo,
           L0_ffn_w_gate, L0_ffn_w_up, L0_ffn_w_down):
    x = np.ascontiguousarray(np.asarray(L0_attn_x, dtype=np.float32))
    wt = {
        "wqT": np.ascontiguousarray(np.asarray(L0_attn_Wq, np.float32).T),
        "wkT": np.ascontiguousarray(np.asarray(L0_attn_Wk, np.float32).T),
        "wvT": np.ascontiguousarray(np.asarray(L0_attn_Wv, np.float32).T),
        "woT": np.ascontiguousarray(np.asarray(L0_attn_Wo, np.float32).T),
        "wgT": np.ascontiguousarray(np.asarray(L0_ffn_w_gate, np.float32).T),
        "wuT": np.ascontiguousarray(np.asarray(L0_ffn_w_up, np.float32).T),
        "wdT": np.ascontiguousarray(np.asarray(L0_ffn_w_down, np.float32).T),
        "bq": np.ascontiguousarray(np.asarray(L0_attn_bq, np.float32)),
        "bk": np.ascontiguousarray(np.asarray(L0_attn_bk, np.float32)),
        "bv": np.ascontiguousarray(np.asarray(L0_attn_bv, np.float32)),
        "bo": np.ascontiguousarray(np.asarray(L0_attn_bo, np.float32)),
    }
    wt["eye"] = np.eye(128, dtype=np.float32)
    _, nc = build_core_program()
    in_maps = [dict(wt, x=np.ascontiguousarray(x[i])) for i in range(NCORES)]
    res = run_bass_kernel_spmd(nc, in_maps, core_ids=list(range(NCORES)))
    out = np.stack([np.asarray(res.results[i]["y"]) for i in range(NCORES)])
    return out.astype(np.float32)


# version 34
# speedup vs baseline: 1.7128x; 1.7128x over previous
import numpy as np
from contextlib import ExitStack

import concourse.bass as bass
import concourse.mybir as mybir
from concourse.bass_utils import run_bass_kernel_spmd

F32 = mybir.dt.float32
BF16 = mybir.dt.bfloat16
ALU = mybir.AluOpType
AF = mybir.ActivationFunctionType
AX = mybir.AxisListType

S, E, INNER = 2048, 1024, 4096
NT = S // 128
NE = E // 128
NI = INNER // 128
NCORES = 8
M32 = 3 * 2.0**22
M64 = 3 * 2.0**51
EPS = 1e-5
SCALE_QK = 0.03125
NEG_BIG = -1e30
FFN_GRP = 4
N_GRP = NT // FFN_GRP


class Buf:
    __slots__ = ("wtok", "rtoks")

    def __init__(self):
        self.wtok = None
        self.rtoks = []


_ENG = ("pe", "dve", "act", "pool", "sp")
_DMA = {
    "sbias": "pool", "sx0": "sp", "sx1": "sp", "sx2": "sp",
    "sx20": "act", "sx21": "act",
    "swr2": "act", "sxn": "act", "sP": "act", "strP": "sp", "strx": "sp",
    "swl0": "sp", "swl1": "sp",
    "swq0": "sp", "swq1": "sp", "swq2": "sp", "swq3": "sp",
    "swl2": "sp", "swl3": "sp",
    "swt0": "sp", "swt1": "sp", "swt2": "sp", "swt3": "sp",
    "sfg0": "sp", "sfg1": "sp", "sfg2": "sp", "sfg3": "sp", "sfd0": "act", "sfd1": "act", "skh": "pool", "sy": "pool",
    "stq1": "sp", "stq2": "sp", "stq3": "sp",
    "sdbg": "pool",
}


class Prog:
    ENGS = _ENG
    QHOST = _DMA

    def __init__(self, nc, stack):
        self.nc = nc
        self.sems = {}
        self.cnt = {}
        for name in self.ENGS + tuple(self.QHOST):
            self.sems[name] = stack.enter_context(nc.semaphore(name))
            self.cnt[name] = 0
        self.water = {e: {c: 0 for c in self.ENGS + tuple(self.QHOST)}
                      for e in self.ENGS}
        self._stack = stack
        self.seg = {e: [] for e in self.ENGS}

    def host(self, chain):
        if chain.startswith("sdbg"):
            return "pool"
        return self.QHOST.get(chain, chain)

    def ensure_chain(self, name, stack):
        if name not in self.sems:
            self.sems[name] = stack.enter_context(self.nc.semaphore(name))
            self.cnt[name] = 0
            for e in self.ENGS:
                self.water[e][name] = 0

    def run(self, chain, fn, reads=(), writes=(), deps=()):
        host = self.host(chain)
        toks = list(deps)
        for b in reads:
            toks.append(b.wtok)
        for b in writes:
            toks.append(b.wtok)
            toks.extend(b.rtoks)
        best = {}
        for t in toks:
            if t is None:
                continue
            c, v = t
            if c == "pe" and chain == "pe":
                continue
            if best.get(c, 0) < v:
                best[c] = v
        for c in sorted(best):
            v = best[c]
            if c in _DMA or c.startswith("sdbg"):
                v = max(v, self.cnt[c])
            if self.water[host][c] < v:
                self.seg[host].append(("w", c, v))
                self.water[host][c] = v
        n = 16 if (chain in self.QHOST or chain.startswith("sdbg")) else 1
        self.cnt[chain] += n
        self.seg[host].append(("i", fn, chain, n))
        tok = (chain, self.cnt[chain])
        for b in reads:
            b.rtoks.append(tok)
        for b in writes:
            b.wtok = tok
            b.rtoks = []
        return tok

    def barrier(self):
        snap = dict(self.cnt)
        for e in self.ENGS:
            for c, v in snap.items():
                if v > self.water[e][c]:
                    self.seg[e].append(("w", c, v))
                    self.water[e][c] = v

    def flush(self):
        nc = self.nc
        segs = {e: lst for e, lst in self.seg.items() if lst}
        self.seg = {e: [] for e in self.ENGS}
        if not segs:
            return

        def replay(entries):
            def body(eng):
                for ent in entries:
                    if ent[0] == "w":
                        eng.wait_ge(self.sems[ent[1]], ent[2])
                    else:
                        _, fn, chain, n = ent
                        fn(eng).then_inc(self.sems[chain], n)
            return body

        with nc.Block() as b:
            for e, lst in segs.items():
                getattr(b, {"pe": "tensor", "dve": "vector", "act": "scalar",
                            "pool": "gpsimd", "sp": "sync"}[e])(replay(lst))


def build_core_program(dbg=()):
    nc = bass.Bass("TRN2", target_bir_lowering=False)
    big = ExitStack()
    with big:
        return _build(nc, big, set(dbg)), nc


def _build(nc, big, dbg):
    p = Prog(nc, big)

    x_d = nc.dram_tensor("x", [S, E], F32, kind="ExternalInput").ap()
    w_d = {}
    for nm, shp in (("wqT", [E, E]), ("wkT", [E, E]), ("wvT", [E, E]),
                    ("woT", [E, E]), ("wgT", [E, INNER]), ("wuT", [E, INNER]),
                    ("wdT", [INNER, E])):
        w_d[nm] = nc.dram_tensor(nm, shp, F32, kind="ExternalInput").ap()
    b_d = {nm: nc.dram_tensor(nm, [E], F32, kind="ExternalInput").ap()
           for nm in ("bq", "bk", "bv", "bo")}
    eye_d = nc.dram_tensor("eye", [128, 128], F32, kind="ExternalInput").ap()
    y_d = nc.dram_tensor("y", [S, E], F32, kind="ExternalOutput").ap()

    scr_xn = nc.dram_tensor("scr_xn", [S, E], BF16, kind="Internal").ap()
    scr_P = nc.dram_tensor("scr_P", [NT, 128, S], BF16, kind="Internal").ap()
    scr_x2 = nc.dram_tensor("scr_x2", [S, E], F32, kind="Internal").ap()
    scr_kh = nc.dram_tensor("scr_kh", [S, INNER], BF16, kind="Internal").ap()
    scr_t = {nm: nc.dram_tensor("scr_t" + nm, shp, BF16, kind="Internal").ap()
             for nm, shp in (("g", [E, INNER]), ("u", [E, INNER]),
                             ("d", [INNER, E]))}
    scr_wq = nc.dram_tensor("scr_wq", [3, 130], F32, kind="Internal").ap()

    dbg_d = {}

    def mkdbg(name):
        cname = "sdbg_" + name.replace("{", "").replace("}", "")
        p.ensure_chain(cname, big)
        return cname

    def dbg_out(name, shape, dt=F32):
        if name in dbg and name not in dbg_d:
            dbg_d[name] = nc.dram_tensor("dbg_" + name, list(shape), dt,
                                         kind="ExternalOutput").ap()
        return dbg_d.get(name)

    sbuf = lambda st, name, shape, dt: st.enter_context(
        nc.sbuf_tensor(name, shape, dt))
    psum = lambda st, name, shape: st.enter_context(
        nc.psum_tensor(name, shape, F32))

    ones_bf = sbuf(big, "ones_bf", [1, 128], BF16)
    eye_bf = sbuf(big, "eye_bf", [128, 128], BF16)
    gfl = sbuf(big, "gfl", [128, 1], F32)
    hfl = sbuf(big, "hfl", [128, 1], F32)
    bqs = sbuf(big, "bqs", [128, NE], F32)
    bks = sbuf(big, "bks", [128, NE], F32)
    bvrow = sbuf(big, "bvrow", [1, E], BF16)
    borow = sbuf(big, "borow", [1, E], BF16)
    x_sb = [sbuf(big, f"x_sb{i}", [128, E], F32) for i in range(2)]
    x2_sb = [sbuf(big, f"x2_sb{i}", [128, E], F32) for i in range(2)]
    bnst = [sbuf(big, f"bnst{i}", [128, 12], F32) for i in range(2)]
    bnm = [sbuf(big, f"bnm{i}", [128, 4], F32) for i in range(2)]
    qinv_x = sbuf(big, "qinv_x", [128, NT], F32)
    qinv_h = sbuf(big, "qinv_h", [128, NT], F32)
    Wbuf = [sbuf(big, f"Wbuf{i}", [128, 8 * 1024], BF16) for i in range(2)]
    epsc = sbuf(big, "epsc", [128, 1], F32)
    negc = sbuf(big, "negc", [128, 1], F32)
    smalls = sbuf(big, "smalls", [128, 16], F32)
    cthr = sbuf(big, "cthr", [128, 3], F32)
    ncthr = sbuf(big, "ncthr", [128, 3], F32)
    wmeanb = sbuf(big, "wmeanb", [128, 3], F32)

    B = {}

    def buf(name):
        if name not in B:
            B[name] = Buf()
        return B[name]

    p.run("dve", lambda e: e.memset(ones_bf[:], 1.0), writes=[buf("ones_bf")])
    p.run("dve", lambda e: e.memset(gfl[:], 2.0**-30), writes=[buf("gfl")])
    p.run("dve", lambda e: e.memset(epsc[:], EPS), writes=[buf("epsc")])
    p.run("dve", lambda e: e.memset(negc[:], -1.0), writes=[buf("negc")])
    p.run("dve", lambda e: e.tensor_scalar(out=gfl[:], in0=gfl[:], scalar1=1.0,
                                           scalar2=1.0, op0=ALU.add,
                                           op1=ALU.subtract),
          reads=[buf("gfl")], writes=[buf("gfl")])
    p.run("dve", lambda e: e.tensor_scalar(out=gfl[:], in0=gfl[:],
                                           scalar1=2.0**30, scalar2=None,
                                           op0=ALU.mult),
          reads=[buf("gfl")], writes=[buf("gfl")])
    p.run("dve", lambda e: e.tensor_scalar(out=hfl[:], in0=gfl[:], scalar1=-1.0,
                                           scalar2=1.0, op0=ALU.mult,
                                           op1=ALU.add),
          reads=[buf("gfl")], writes=[buf("hfl")])
    for ec in range(NE):
        p.run("sbias", lambda e, ec=ec: e.dma_start(
            out=bqs[:, ec:ec + 1], in_=b_d["bq"][ec * 128:(ec + 1) * 128, None]),
            writes=[buf(f"bqs{ec}")])
        p.run("sbias", lambda e, ec=ec: e.dma_start(
            out=bks[:, ec:ec + 1], in_=b_d["bk"][ec * 128:(ec + 1) * 128, None]),
            writes=[buf(f"bks{ec}")])
    p.run("sbias", lambda e: e.dma_start(out=eye_bf[:, :], in_=eye_d[:, :]),
          writes=[buf("eye_bf")])
    p.run("sbias", lambda e: e.dma_start(out=bvrow[0:1, :], in_=b_d["bv"][None, :]),
          writes=[buf("bvrow")])
    tok_bias = p.run("sbias", lambda e: e.dma_start(out=borow[0:1, :],
                                                    in_=b_d["bo"][None, :]),
                     writes=[buf("borow")])
    for _nm in [f"bqs{_ec}" for _ec in range(NE)] + \
            [f"bks{_ec}" for _ec in range(NE)] + ["bvrow", "borow", "eye_bf"]:
        buf(_nm).wtok = tok_bias

    def emit_ln(src_buf, src_ap, i):
        bs, bm = bnst[i], bnm[i]
        p.run("dve", lambda e: e.bn_stats(out=bs[:, 0:6], in_=src_ap[:, 0:512]),
              reads=[src_buf], writes=[buf(f"bnst{i}")])
        p.run("dve", lambda e: e.bn_stats(out=bs[:, 6:12], in_=src_ap[:, 512:1024]),
              reads=[src_buf], writes=[buf(f"bnst{i}")])
        p.run("dve", lambda e: e.bn_aggr(out=bm[:, 0:2], in_=bs[:, 0:12]),
              reads=[buf(f"bnst{i}")], writes=[buf(f"bnm{i}")])
        p.run("act", lambda e: e.activation(out=bm[:, 2:3], in_=bm[:, 1:2],
                                            func=AF.Sqrt, bias=epsc[:, 0:1],
                                            scale=1.0),
              reads=[buf(f"bnm{i}"), buf("epsc")], writes=[buf(f"bnm{i}")])
        p.run("dve", lambda e: e.reciprocal(out=bm[:, 3:4], in_=bm[:, 2:3]),
              reads=[buf(f"bnm{i}")], writes=[buf(f"bnm{i}")])
        return bm

    def emit_round_neg(src_buf, src_ap, sneg_ap, sneg_buf, out_buf, out_ap,
                       tmpa_buf, tmpa_ap, tmpr_buf, tmpr_ap):
        p.run("dve", lambda e: e.tensor_scalar(
            out=tmpa_ap, in0=src_ap, scalar1=sneg_ap, scalar2=M32,
            op0=ALU.mult, op1=ALU.add),
            reads=[src_buf, sneg_buf], writes=[tmpa_buf])
        p.run("dve", lambda e: e.tensor_scalar(
            out=tmpr_ap, in0=tmpa_ap, scalar1=M32, scalar2=hfl[:, 0:1],
            op0=ALU.subtract, op1=ALU.mult),
            reads=[tmpa_buf, buf("hfl")], writes=[tmpr_buf])
        p.run("dve", lambda e: e.tensor_scalar(
            out=tmpa_ap, in0=tmpa_ap, scalar1=M64 - M32, scalar2=None,
            op0=ALU.add),
            reads=[tmpa_buf], writes=[tmpa_buf])
        p.run("dve", lambda e: e.scalar_tensor_tensor(
            out=out_ap, in0=tmpa_ap, scalar=M64, in1=tmpr_ap,
            op0=ALU.subtract, op1=ALU.add),
            reads=[tmpa_buf, tmpr_buf], writes=[out_buf])

    segWQ = ExitStack()
    Wstg = [sbuf(segWQ, f"Wstg{i}", [128, 512], F32) for i in range(3)]
    Wmk = [sbuf(segWQ, f"Wmk{i}", [128, 512], BF16) for i in range(2)]
    wabs = [sbuf(segWQ, f"wabs{i}", [128, 512], F32) for i in range(2)]
    wacc = sbuf(segWQ, "wacc", [128, 512], F32)
    wrow = sbuf(segWQ, "wrow", [1, 128], F32)
    wbc = sbuf(segWQ, "wbc", [128, 1], F32)

    wq_thunks = []
    _wq_lseq = [0]

    def _wq_build():
        for wi, (nm, dram, rows, cols) in enumerate(
                (("g", w_d["wgT"], E, INNER), ("u", w_d["wuT"], E, INNER),
                 ("d", w_d["wdT"], INNER, E))):
            slices = [(ic, sub) for ic in range(rows // 128)
                      for sub in range(cols // 512)]
            nelem = float(rows * cols)
            wq_thunks.append((0, lambda: p.run(
                "dve", lambda e: e.memset(wacc[:], 0.0), writes=[buf("wacc")])))

            def loadA(j, ic, sub, jj, dram=dram):
                stg = buf(f"Wstg{jj}")
                p.run(f"swq{jj}", lambda e, ic=ic, sub=sub, jj=jj, dram=dram: e.dma_start(
                    out=Wstg[jj][:], in_=dram[ic * 128:(ic + 1) * 128,
                                              sub * 512:(sub + 1) * 512]),
                    writes=[stg])

            def passA(j, ic, sub, jj, dram=dram):
                ja = j % 2
                stg = buf(f"Wstg{jj}")
                p.run("dve", lambda e, jj=jj, ja=ja: e.scalar_tensor_tensor(
                    out=wabs[ja][:], in0=Wstg[jj][:], scalar=-1.0,
                    in1=Wstg[jj][:], op0=ALU.mult, op1=ALU.max),
                    reads=[stg], writes=[buf(f"wabs{ja}")])
                p.run("pool", lambda e, ja=ja: e.tensor_tensor(
                    out=wacc[:], in0=wacc[:], in1=wabs[ja][:], op=ALU.add),
                    reads=[buf(f"wabs{ja}"), buf("wacc")], writes=[buf("wacc")])

            for j, (ic, sub) in enumerate(slices):
                sl = _wq_lseq[0] % 3
                _wq_lseq[0] += 1
                wq_thunks.append((1, lambda j=j, ic=ic, sub=sub, sl=sl, f=loadA: f(j, ic, sub, sl),
                                  lambda j=j, ic=ic, sub=sub, sl=sl, f=passA: f(j, ic, sub, sl)))

            def mid(wi=wi, nelem=nelem):
                w = 512
                while w > 1:
                    h = w // 2
                    p.run("dve", lambda e, h=h, w=w: e.tensor_tensor(
                        out=wacc[:, 0:h], in0=wacc[:, 0:h], in1=wacc[:, h:w],
                        op=ALU.add),
                        reads=[buf("wacc")], writes=[buf("wacc")])
                    w = h
                p.run("swq0", lambda e: e.dma_start(
                    out=scr_wq[wi, :128, None], in_=wacc[:, 0:1]),
                    reads=[buf("wacc")], writes=[buf("scr_wq")])
                p.run("swq0", lambda e: e.dma_start(
                    out=wrow[0:1, :], in_=scr_wq[wi, None, :128]),
                    reads=[buf("scr_wq")], writes=[buf("wrow")])
                w = 128
                while w > 1:
                    h = w // 2
                    p.run("dve", lambda e, h=h, w=w: e.tensor_tensor(
                        out=wrow[0:1, 0:h], in0=wrow[0:1, 0:h], in1=wrow[0:1, h:w],
                        op=ALU.add),
                        reads=[buf("wrow")], writes=[buf("wrow")])
                    w = h
                p.run("swq0", lambda e: e.dma_start(
                    out=scr_wq[wi, 128:129, None], in_=wrow[0:1, 0:1]),
                    reads=[buf("wrow")], writes=[buf("scr_wq")])
                p.run("swq0", lambda e: e.dma_start(
                    out=wbc[:, 0:1],
                    in_=scr_wq[wi, 128:129, None].to_broadcast([128, 1])),
                    reads=[buf("scr_wq")], writes=[buf("wbc")])
                p.run("pool", lambda e: e.tensor_scalar(
                    out=wmeanb[:, wi:wi + 1], in0=wbc[:, 0:1], scalar1=1.0 / nelem,
                    scalar2=None, op0=ALU.mult),
                    reads=[buf("wbc")], writes=[buf("wmeanb")])
                p.run("pool", lambda e: e.tensor_scalar(
                    out=cthr[:, wi:wi + 1], in0=wbc[:, 0:1], scalar1=0.5 / nelem,
                    scalar2=None, op0=ALU.mult),
                    reads=[buf("wbc")], writes=[buf("cthr")])
                p.run("pool", lambda e: e.tensor_scalar(
                    out=ncthr[:, wi:wi + 1], in0=cthr[:, wi:wi + 1], scalar1=-1.0,
                    scalar2=None, op0=ALU.mult),
                    reads=[buf("cthr")], writes=[buf("ncthr")])

            wq_thunks.append((0, mid))

            def loadB(j, ic, sub, jj, dram=dram):
                stg = buf(f"Wstg{jj}")
                p.run(f"swq{jj}", lambda e, ic=ic, sub=sub, jj=jj, dram=dram: e.dma_start(
                    out=Wstg[jj][:], in_=dram[ic * 128:(ic + 1) * 128,
                                              sub * 512:(sub + 1) * 512]),
                    writes=[stg])

            def passB(j, ic, sub, jj, dram=dram, nm=nm, wi=wi):
                js = j % 2
                stg = buf(f"Wstg{jj}")
                p.run("dve", lambda e, jj=jj, js=js, wi=wi: e.tensor_scalar(
                    out=Wmk[js][:], in0=Wstg[jj][:],
                    scalar1=cthr[:, wi:wi + 1], scalar2=-1.0,
                    op0=ALU.is_gt, op1=ALU.mult),
                    reads=[stg, buf("cthr")], writes=[buf(f"Wmk{js}")])
                p.run("dve", lambda e, jj=jj, js=js, wi=wi: e.scalar_tensor_tensor(
                    out=Wmk[js][:], in0=Wstg[jj][:], scalar=ncthr[:, wi:wi + 1],
                    in1=Wmk[js][:], op0=ALU.is_lt, op1=ALU.add),
                    reads=[stg, buf("ncthr"), buf(f"Wmk{js}")],
                    writes=[buf(f"Wmk{js}")])
                p.run(f"swt{js}", lambda e, ic=ic, sub=sub, js=js, nm=nm: e.dma_start(
                    out=scr_t[nm][ic * 128:(ic + 1) * 128,
                                  sub * 512:(sub + 1) * 512], in_=Wmk[js][:]),
                    reads=[buf(f"Wmk{js}")], writes=[buf("scr_t" + nm)])

            for j, (ic, sub) in enumerate(slices):
                sl = _wq_lseq[0] % 3
                _wq_lseq[0] += 1
                wq_thunks.append((1, lambda j=j, ic=ic, sub=sub, sl=sl, f=loadB: f(j, ic, sub, sl),
                                  lambda j=j, ic=ic, sub=sub, sl=sl, f=passB: f(j, ic, sub, sl)))

    _wq_build()
    wq_load_pos = [0]
    wq_comp_pos = [0]

    def _wq_advance_loads(target):
        while wq_load_pos[0] < min(target, len(wq_thunks)):
            ent = wq_thunks[wq_load_pos[0]]
            if ent[0] == 1:
                ent[1]()
            wq_load_pos[0] += 1

    def drain_wq(k):
        while k > 0 and wq_comp_pos[0] < len(wq_thunks):
            i = wq_comp_pos[0]
            _wq_advance_loads(i + 3)
            ent = wq_thunks[i]
            if ent[0] == 1:
                ent[2]()
            else:
                ent[1]()
                wq_load_pos[0] = max(wq_load_pos[0], i + 1)
            wq_comp_pos[0] += 1
            k -= 1

    with ExitStack() as segB:
        Q_fm = sbuf(segB, "Q_fm", [128, NE * S], BF16)
        K_fm = sbuf(segB, "K_fm", [128, NE * S], BF16)
        V_tm = sbuf(segB, "V_tm", [128, NT * E], BF16)
        psA = psum(segB, "psA", [128, 512])
        psB = psum(segB, "psB", [128, 512])
        segB1 = ExitStack()
        xn1_fm = sbuf(segB1, "xn1_fm", [128, NE * S], BF16)
        xn1c = [sbuf(segB1, "xn1c0", [128, E], BF16)]
        psX = [segB1.enter_context(nc.psum_tensor(f"psX{i}", [128, 128], BF16))
               for i in range(2)]
        segWL = ExitStack()
        Wst32 = sbuf(segWL, "Wst32", [128, 2048], F32)

        wb0, wb1 = buf("Wbuf0"), buf("Wbuf1")

        wl_thunks = []
        _wl_seq = [0]

        def _wl_stage(hh):
            return Wst32[:, hh * 512:(hh + 1) * 512], buf(f"Wst32h{hh}")

        def _wl_build(dram, wb, wbuf, thunks=None):
            if thunks is None:
                thunks = wl_thunks
            for hq in range(16):
                h, q = hq // 2, hq % 2
                hh = _wl_seq[0] % 4
                _wl_seq[0] += 1

                def fn(h=h, q=q, hh=hh, dram=dram, wb=wb, wbuf=wbuf):
                    sap, sh = _wl_stage(hh)
                    p.run(f"swl{hh}", lambda e, h=h, q=q, sap=sap, dram=dram: e.dma_start(
                        out=sap,
                        in_=dram[h * 128:(h + 1) * 128, q * 512:(q + 1) * 512]),
                        writes=[sh])
                    p.run("dve", lambda e, h=h, q=q, sap=sap, wbuf=wbuf: e.tensor_copy(
                        out=wbuf[:, h * 1024 + q * 512: h * 1024 + q * 512 + 512],
                        in_=sap),
                        reads=[sh], writes=[wb])
                thunks.append(fn)

        _wl_build(w_d["wqT"], wb0, Wbuf[0])
        _wl_build(w_d["wkT"], wb1, Wbuf[1])

        def proj_fm(wbuf, wb, bias_sb, bias_pref, scale, out_sb, out_buf, t4):
            for oc in range(NE):
                bank, bk_ = (psA, buf("psA")) if oc % 2 == 0 \
                    else (psB, buf("psB"))
                for ic in range(NE):
                    tok = p.run("pe", lambda e, oc=oc, t4=t4, ic=ic, bank=bank,
                                wbuf=wbuf: e.matmul(
                        bank[:, :],
                        wbuf[:, ic * E + oc * 128: ic * E + oc * 128 + 128],
                        xn1_fm[:, ic * S + t4 * 512: ic * S + t4 * 512 + 512],
                        start=(ic == 0), stop=(ic == NE - 1)),
                        reads=[wb, buf("xn1_fm")],
                        writes=([bk_] if ic == 0 else []))
                    if ic == NE - 1:
                        bk_.wtok = tok
                if scale is None:
                    p.run("dve", lambda e, oc=oc, t4=t4, bank=bank: e.tensor_scalar(
                        out=out_sb[:, oc * S + t4 * 512: oc * S + t4 * 512 + 512],
                        in0=bank[:, :], scalar1=bias_sb[:, oc:oc + 1],
                        scalar2=None, op0=ALU.add),
                        reads=[bk_, buf(f"{bias_pref}{oc}")], writes=[out_buf])
                else:
                    p.run("dve", lambda e, oc=oc, t4=t4, bank=bank: e.tensor_scalar(
                        out=out_sb[:, oc * S + t4 * 512: oc * S + t4 * 512 + 512],
                        in0=bank[:, :], scalar1=bias_sb[:, oc:oc + 1],
                        scalar2=scale, op0=ALU.add, op1=ALU.mult),
                        reads=[bk_, buf(f"{bias_pref}{oc}")], writes=[out_buf])

        def issue_x(c):
            i = c % 2
            p.run(f"sx{i}", lambda e, c=c, i=i: e.dma_start(
                out=x_sb[i][:], in_=x_d[c * 128:(c + 1) * 128, :]),
                writes=[buf(f"x_sb{i}")])

        issue_x(0)
        issue_x(1)
        for c in range(NT):
            i = c % 2
            ib = c % 2
            xb, xnb = buf(f"x_sb{i}"), buf("xn1c0")
            bm = emit_ln(xb, x_sb[i][:], ib)
            p.run("dve", lambda e, i=i, ib=ib: e.tensor_scalar(
                out=xn1c[0][:], in0=x_sb[i][:], scalar1=bnm[ib][:, 0:1],
                scalar2=bnm[ib][:, 3:4], op0=ALU.subtract, op1=ALU.mult),
                reads=[xb, buf(f"bnm{ib}")], writes=[xnb])
            if c + 2 < NT:
                issue_x(c + 2)
            for ic in range(NE):
                tb = buf(f"psX{ic % 2}")
                tok = p.run("pe", lambda e, ic=ic: e.transpose(
                    psX[ic % 2][:, :], xn1c[0][:, ic * 128:(ic + 1) * 128],
                    eye_bf[:, :]),
                    reads=[xnb, buf("eye_bf")], writes=[tb])
                tb.wtok = tok
                p.run("act", lambda e, ic=ic, c=c: e.copy(
                    out=xn1_fm[:, ic * S + c * 128: ic * S + c * 128 + 128],
                    in_=psX[ic % 2][:, :]),
                    reads=[tb], writes=[buf("xn1_fm")])
            for _ in range(4):
                if wl_thunks:
                    wl_thunks.pop(0)()
            drain_wq(6)
            if c % 4 == 3:
                t4 = c // 4
                proj_fm(Wbuf[0], wb0, bqs, "bqs", SCALE_QK, Q_fm,
                        buf("Q_fm"), t4)
                if t4 >= 1:
                    proj_fm(Wbuf[1], wb1, bks, "bks", None, K_fm,
                            buf("K_fm"), t4 - 1)
        proj_fm(Wbuf[1], wb1, bks, "bks", None, K_fm, buf("K_fm"), 3)
        drain_wq(6)

        _wl_build(w_d["wvT"], wb0, Wbuf[0])
        for fn in wl_thunks:
            fn()
        wl_thunks = []
        wo_thunks = []
        _wl_build(w_d["woT"], wb1, Wbuf[1], thunks=wo_thunks)

        for c in range(NT):
            for oh in range(2):
                bank, bk_ = (psA, buf("psA")) if (c * 2 + oh) % 2 == 0 else (psB, buf("psB"))
                for ic in range(NE):
                    tok = p.run("pe", lambda e, c=c, oh=oh, ic=ic, bank=bank: e.matmul(
                        bank[:, :],
                        xn1_fm[:, ic * S + c * 128: ic * S + c * 128 + 128],
                        Wbuf[0][:, ic * E + oh * 512: ic * E + oh * 512 + 512],
                        start=(ic == 0), stop=False),
                        reads=[wb0, buf("xn1_fm")],
                        writes=([bk_] if ic == 0 else []))
                tok = p.run("pe", lambda e, oh=oh, bank=bank: e.matmul(
                    bank[:, :], ones_bf[0:1, :],
                    bvrow[0:1, oh * 512:(oh + 1) * 512], start=False, stop=True),
                    reads=[buf("ones_bf"), buf("bvrow")])
                bk_.wtok = tok
                p.run("act", lambda e, c=c, oh=oh, bank=bank: e.copy(
                    out=V_tm[:, c * E + oh * 512: c * E + oh * 512 + 512],
                    in_=bank[:, :]), reads=[bk_], writes=[buf("V_tm")])
            if wo_thunks:
                wo_thunks.pop(0)()
            drain_wq(4)
        p.flush()
        segWL.close()
        segB1.close()
        p.barrier()

        if dbg_out("q_fm", [128, NE * S], BF16) is not None:
            p.run("sdbg", lambda e: e.dma_start(out=dbg_d["q_fm"][:, :], in_=Q_fm[:]),
                  reads=[buf("Q_fm")])
        if dbg_out("v_tm", [128, NT * E], BF16) is not None:
            p.run("sdbg", lambda e: e.dma_start(out=dbg_d["v_tm"][:, :], in_=V_tm[:]),
                  reads=[buf("V_tm")])
        p.flush()

        with ExitStack() as segC:
            S_sb = [sbuf(segC, f"S_sb{i}", [128, S], BF16) for i in range(3)]
            P_sb = [sbuf(segC, f"P_sb{i}", [128, S], BF16) for i in range(2)]
            PT_g = sbuf(segC, "PT_g", [128, 16 * 256], BF16)
            attn_g = sbuf(segC, "attn_g", [128, NE * 256], BF16)
            xn2_sb = sbuf(segC, "xn2_sb", [128, E], F32)
            t8 = sbuf(segC, "t8", [128, 32], BF16)
            qr = sbuf(segC, "qr", [128, E], BF16)
            kx = sbuf(segC, "kx", [128, E], BF16)
            psC = [psum(segC, f"psC{i}", [128, 512]) for i in range(2)]
            psD = psum(segC, "psD", [128, 512])
            psE = psum(segC, "psE", [128, 512])
            psT = [st.enter_context(nc.psum_tensor(f"psT{i}", [128, 128], BF16))
                   for i, st in ((0, segC), (1, segC))]

            def sc_banks(tg):
                return ([(psA, buf("psA")), (psB, buf("psB"))] if tg % 2 == 0
                        else [(psC[0], buf("psC0")), (psC[1], buf("psC1"))])

            def emit_scores(tg):
                q0 = tg * 128
                pp = tg % 3
                banks = sc_banks(tg)
                for w in range(2):
                    for kh in range(2):
                        kc4 = w * 2 + kh
                        bank, bk_ = banks[kh]
                        for ic in range(NE):
                            tok = p.run("pe", lambda e, kc4=kc4, ic=ic, q0=q0, bank=bank: e.matmul(
                                bank[:, :],
                                Q_fm[:, ic * S + q0: ic * S + q0 + 128],
                                K_fm[:, ic * S + kc4 * 512: ic * S + kc4 * 512 + 512],
                                start=(ic == 0), stop=(ic == NE - 1)),
                                reads=[buf("Q_fm"), buf("K_fm")],
                                writes=([bk_] if ic == 0 else []))
                            if ic == NE - 1:
                                bk_.wtok = tok
                        p.run("act", lambda e, kc4=kc4, bank=bank, pp=pp: e.copy(
                            out=S_sb[pp][:, kc4 * 512:(kc4 + 1) * 512],
                            in_=bank[:, :]),
                            reads=[bk_], writes=[buf(f"S_sb{pp}")])

            def emit_softmax(tg):
                pp = tg % 3
                qq = tg % 2
                sb_, pb = buf(f"S_sb{pp}"), buf(f"P_sb{qq}")
                SC = P_sb[1 - qq]
                scb = buf(f"P_sb{1 - qq}")
                for r in range(4):
                    src, srb = (S_sb[pp], sb_) if r == 0 else (SC, scb)
                    p.run("dve", lambda e, r=r, src=src: e.max(
                        out=t8[:, r * 8:(r + 1) * 8], in_=src[:]),
                        reads=[srb], writes=[buf("t8")])
                    if r < 3:
                        p.run("dve", lambda e, r=r, SC=SC, src=src: e.match_replace(
                            out=SC[:], in_to_replace=t8[:, r * 8:(r + 1) * 8],
                            in_values=src[:], imm_value=NEG_BIG),
                            reads=[buf("t8"), srb], writes=[scb])
                p.run("dve", lambda e: e.tensor_scalar(
                    out=smalls[:, 0:1], in0=t8[:, 0:1], scalar1=-1.0,
                    scalar2=None, op0=ALU.mult),
                    reads=[buf("t8")], writes=[buf("smalls")])
                p.run("dve", lambda e: e.tensor_copy(
                    out=smalls[:, 1:2], in_=t8[:, 31:32]),
                    reads=[buf("t8")], writes=[buf("smalls")])
                p.run("act", lambda e, pp=pp, qq=qq: e.activation(
                    out=P_sb[qq][:], in_=S_sb[pp][:], func=AF.Exp,
                    bias=smalls[:, 0:1], scale=1.0),
                    reads=[sb_, buf("smalls")], writes=[pb])
                p.run("dve", lambda e, pp=pp, qq=qq: e.scalar_tensor_tensor(
                    out=P_sb[qq][:], in0=S_sb[pp][:], scalar=smalls[:, 1:2],
                    in1=P_sb[qq][:], op0=ALU.is_ge, op1=ALU.mult),
                    reads=[sb_, buf("smalls"), pb], writes=[pb])
                p.run("dve", lambda e, qq=qq: e.reduce_sum(
                    out=smalls[:, 2:3], in_=P_sb[qq][:], axis=AX.X),
                    reads=[pb], writes=[buf("smalls")])
                p.run("dve", lambda e: e.reciprocal(
                    out=smalls[:, 3:4], in_=smalls[:, 2:3]),
                    reads=[buf("smalls")], writes=[buf("smalls")])
                p.run("act", lambda e, qq=qq: e.activation(
                    out=P_sb[qq][:], in_=P_sb[qq][:], func=AF.Copy,
                    bias=0.0, scale=smalls[:, 3:4]),
                    reads=[pb, buf("smalls")], writes=[pb])
                if dbg_out(f"p_{tg}", [128, S], BF16) is not None:
                    p.run(mkdbg(f"p_{tg}"), lambda e, tg=tg, qq=qq: e.dma_start(
                        out=dbg_d[f"p_{tg}"][:, :], in_=P_sb[qq][:]),
                        reads=[pb])

            def emit_ptrans_drain(tg):
                drain_wq(5)

            def emit_ptrans(tg):
                pp = tg % 2
                t = tg % 2
                pb = buf(f"P_sb{pp}")
                for kc in range(16):
                    tb = buf(f"psT{kc % 2}")
                    tok = p.run("pe", lambda e, kc=kc, pp=pp: e.transpose(
                        psT[kc % 2][:, :], P_sb[pp][:, kc * 128:(kc + 1) * 128],
                        eye_bf[:, :]),
                        reads=[pb, buf("eye_bf")], writes=[tb])
                    tb.wtok = tok
                    p.run("act", lambda e, kc=kc, t=t: e.copy(
                        out=PT_g[:, kc * 256 + t * 128: kc * 256 + t * 128 + 128],
                        in_=psT[kc % 2][:, :]),
                        reads=[tb], writes=[buf("PT_g")])

            def emit_attnv(g):
                for ec in range(NE):
                    bank, bk_ = (psD, buf("psD")) if ec % 2 == 0 else (psE, buf("psE"))
                    for kc in range(16):
                        tok = p.run("pe", lambda e, ec=ec, kc=kc, bank=bank: e.matmul(
                            bank[:, 0:256],
                            V_tm[:, kc * E + ec * 128: kc * E + ec * 128 + 128],
                            PT_g[:, kc * 256:(kc + 1) * 256],
                            start=(kc == 0), stop=(kc == 15)),
                            reads=[buf("V_tm"), buf("PT_g")],
                            writes=([bk_] if kc == 0 else []))
                        if kc == 15:
                            bk_.wtok = tok
                    p.run("act", lambda e, ec=ec, bank=bank: e.copy(
                        out=attn_g[:, ec * 256:(ec + 1) * 256], in_=bank[:, 0:256]),
                        reads=[bk_], writes=[buf("attn_g")])

            def emit_oproj_mm(g, t):
                tg = g * 2 + t
                i = tg % 2
                xb = buf(f"x_sb{i}")
                x2b = buf(f"x2_sb{i}")
                p.run(f"sx{i}", lambda e, tg=tg, i=i: e.dma_start(
                    out=x_sb[i][:], in_=x_d[tg * 128:(tg + 1) * 128, :]),
                    writes=[xb])
                for oh in range(2):
                    bank, bk_ = (psD, buf("psD")) if oh == 0 else (psE, buf("psE"))
                    for ec in range(NE):
                        p.run("pe", lambda e, t=t, oh=oh, ec=ec, bank=bank: e.matmul(
                            bank[:, :],
                            attn_g[:, ec * 256 + t * 128: ec * 256 + t * 128 + 128],
                            Wbuf[1][:, ec * E + oh * 512: ec * E + oh * 512 + 512],
                            start=(ec == 0), stop=False),
                            reads=[buf("attn_g"), buf("Wbuf1")],
                            writes=([bk_] if ec == 0 else []))
                    tok = p.run("pe", lambda e, oh=oh, bank=bank: e.matmul(
                        bank[:, :], ones_bf[0:1, :],
                        borow[0:1, oh * 512:(oh + 1) * 512],
                        start=False, stop=True),
                        reads=[buf("ones_bf"), buf("borow")])
                    bk_.wtok = tok
                    p.run("dve", lambda e, i=i, oh=oh, bank=bank: e.tensor_tensor(
                        out=x2_sb[i][:, oh * 512:(oh + 1) * 512],
                        in0=x_sb[i][:, oh * 512:(oh + 1) * 512],
                        in1=bank[:, :], op=ALU.add),
                        reads=[xb, bk_], writes=[x2b])
                p.run("swr2", lambda e, tg=tg, i=i: e.dma_start(
                    out=scr_x2[tg * 128:(tg + 1) * 128, :], in_=x2_sb[i][:]),
                    reads=[x2b], writes=[buf("scr_x2")])
                if dbg_out(f"x2_{tg}", [128, E]) is not None:
                    p.run(mkdbg(f"x2_{tg}"), lambda e, tg=tg, i=i: e.dma_start(
                        out=dbg_d[f"x2_{tg}"][:, :], in_=x2_sb[i][:]),
                        reads=[x2b])

            def emit_tail_ln2(g, t):
                tg = g * 2 + t
                i = tg % 2
                x2b = buf(f"x2_sb{i}")
                bm = emit_ln(x2b, x2_sb[i][:], i)
                p.run("dve", lambda e, i=i: e.tensor_scalar(
                    out=xn2_sb[:], in0=x2_sb[i][:], scalar1=bnm[i][:, 0:1],
                    scalar2=bnm[i][:, 3:4], op0=ALU.subtract, op1=ALU.mult),
                    reads=[x2b, buf(f"bnm{i}")], writes=[buf("xn2_sb")])
                p.run("dve", lambda e: e.reduce_max(
                    out=smalls[:, 4:5], in_=xn2_sb[:], axis=AX.X,
                    apply_absolute_value=True),
                    reads=[buf("xn2_sb")], writes=[buf("smalls")])
                p.run("dve", lambda e: e.tensor_scalar(
                    out=smalls[:, 5:6], in0=smalls[:, 4:5], scalar1=1e-5,
                    scalar2=None, op0=ALU.max),
                    reads=[buf("smalls")], writes=[buf("smalls")])
                p.run("dve", lambda e, tg=tg: e.tensor_scalar(
                    out=qinv_x[:, tg:tg + 1], in0=smalls[:, 5:6],
                    scalar1=1.0 / 127.0, scalar2=None, op0=ALU.mult),
                    reads=[buf("smalls")], writes=[buf("qinv_x")])
                p.run("dve", lambda e: e.reciprocal(
                    out=smalls[:, 6:7], in_=smalls[:, 5:6]),
                    reads=[buf("smalls")], writes=[buf("smalls")])
                p.run("dve", lambda e: e.tensor_scalar(
                    out=smalls[:, 7:8], in0=smalls[:, 6:7], scalar1=-127.0,
                    scalar2=None, op0=ALU.mult),
                    reads=[buf("smalls")], writes=[buf("smalls")])
                emit_round_neg(buf("xn2_sb"), xn2_sb[:], smalls[:, 7:8],
                               buf("smalls"), buf("kx"), kx[:],
                               buf("xn2_sb"), xn2_sb[:], buf("qr"), qr[:])
                p.run("sxn", lambda e, tg=tg: e.dma_start(
                    out=scr_xn[tg * 128:(tg + 1) * 128, :], in_=kx[:]),
                    reads=[buf("kx")], writes=[buf("scr_xn")])
                drain_wq(7)

            emit_scores(0)
            for g in range(8):
                t0, t1 = 2 * g, 2 * g + 1
                emit_scores(t1)
                emit_softmax(t0)
                drain_wq(5)
                if g >= 1:
                    emit_oproj_mm(g - 1, 0)
                    emit_oproj_mm(g - 1, 1)
                emit_ptrans(t0)
                if g >= 1:
                    emit_tail_ln2(g - 1, 0)
                if t1 + 1 < NT:
                    emit_scores(t1 + 1)
                emit_softmax(t1)
                drain_wq(5)
                emit_ptrans(t1)
                if g >= 1:
                    emit_tail_ln2(g - 1, 1)
                emit_attnv(g)
            emit_oproj_mm(7, 0)
            emit_oproj_mm(7, 1)
            emit_tail_ln2(7, 0)
            emit_tail_ln2(7, 1)
            p.flush()
        p.flush()

    p.barrier()
    drain_wq(10**9)
    p.flush()
    segWQ.close()
    p.barrier()
    if dbg_out("tern_g", [E, INNER], BF16) is not None:
        p.run("sdbg", lambda e: e.dma_start(out=dbg_d["tern_g"][:, :],
                                            in_=scr_t["g"][:, :]),
              reads=[buf("scr_tg")])
        p.flush()
    with ExitStack() as segE:
        xq_fm = sbuf(segE, "xq_fm", [128, NE * S], BF16)
        k_fm = sbuf(segE, "k_fm", [128, NI * 512], BF16)
        h_grp = [sbuf(segE, f"h_grp{i}", [128, INNER], F32) for i in range(FFN_GRP)]
        qr2 = sbuf(segE, "qr2", [128, 2048], BF16)
        kq = sbuf(segE, "kq", [128, 2048], BF16)
        y_sb = sbuf(segE, "y_sb", [128, E], F32)
        sg_sb = [sbuf(segE, f"sg_sb{i}", [128, 512], F32) for i in range(2)]
        sclg = sbuf(segE, "sclg", [128, NT], F32)
        sclu = sbuf(segE, "sclu", [128, NT], F32)
        scld = sbuf(segE, "scld", [128, NT], F32)
        psF = [psum(segE, f"psF{i}", [128, 512]) for i in range(8)]
        psb = [buf(f"psF{i}") for i in range(8)]

        for c in range(NT):
            p.run("dve", lambda e, c=c: e.tensor_scalar(
                out=sclg[:, c:c + 1], in0=qinv_x[:, c:c + 1],
                scalar1=wmeanb[:, 0:1], scalar2=None, op0=ALU.mult),
                reads=[buf("qinv_x"), buf("wmeanb")], writes=[buf("sclg")])
            p.run("dve", lambda e, c=c: e.tensor_scalar(
                out=sclu[:, c:c + 1], in0=qinv_x[:, c:c + 1],
                scalar1=wmeanb[:, 1:2], scalar2=None, op0=ALU.mult),
                reads=[buf("qinv_x"), buf("wmeanb")], writes=[buf("sclu")])

        TQ = ("strx", "stq1", "stq2", "stq3")
        for ec in range(NE):
            p.run(TQ[ec % 4], lambda e, ec=ec: e.dma_start_transpose(
                out=xq_fm[:, ec * S:(ec + 1) * S],
                in_=scr_xn[:, ec * 128:(ec + 1) * 128]),
                reads=[buf("scr_xn")], writes=[buf("xq_fm")])

        def emit_gateup(g):
            c0 = g * FFN_GRP
            for oc in range(8):
                hh = oc % 2
                goff = hh * 4096
                p.run(f"sfg{hh}", lambda e, oc=oc, goff=goff: e.dma_start(
                    out=Wbuf[0][:, goff:goff + 4096].rearrange(
                        "p (ic x) -> p ic x", x=512),
                    in_=scr_t["g"][:, oc * 512:(oc + 1) * 512].rearrange(
                        "(ic q) x -> q ic x", q=128)),
                    reads=[buf("scr_tg")], writes=[buf(f"Wb0h{hh}")])
                p.run(f"sfg{2 + hh}", lambda e, oc=oc, goff=goff: e.dma_start(
                    out=Wbuf[1][:, goff:goff + 4096].rearrange(
                        "p (ic x) -> p ic x", x=512),
                    in_=scr_t["u"][:, oc * 512:(oc + 1) * 512].rearrange(
                        "(ic q) x -> q ic x", q=128)),
                    reads=[buf("scr_tu")], writes=[buf(f"Wb1h{hh}")])
                for cc in range(FFN_GRP):
                    c = c0 + cc
                    bg, bgb = psF[(cc % 2) * 2], psb[(cc % 2) * 2]
                    bu, bub = psF[(cc % 2) * 2 + 1], psb[(cc % 2) * 2 + 1]
                    for ic in range(NE):
                        tok = p.run("pe", lambda e, ic=ic, c=c, bg=bg, goff=goff: e.matmul(
                            bg[:, :],
                            xq_fm[:, ic * S + c * 128: ic * S + c * 128 + 128],
                            Wbuf[0][:, goff + ic * 512: goff + (ic + 1) * 512],
                            start=(ic == 0), stop=(ic == NE - 1)),
                            reads=[buf("xq_fm"), buf(f"Wb0h{hh}")],
                            writes=([bgb] if ic == 0 else []))
                        if ic == NE - 1:
                            bgb.wtok = tok
                    for ic in range(NE):
                        tok = p.run("pe", lambda e, ic=ic, c=c, bu=bu, goff=goff: e.matmul(
                            bu[:, :],
                            xq_fm[:, ic * S + c * 128: ic * S + c * 128 + 128],
                            Wbuf[1][:, goff + ic * 512: goff + (ic + 1) * 512],
                            start=(ic == 0), stop=(ic == NE - 1)),
                            reads=[buf("xq_fm"), buf(f"Wb1h{hh}")],
                            writes=([bub] if ic == 0 else []))
                        if ic == NE - 1:
                            bub.wtok = tok
                    sgi = cc % 2
                    p.run("act", lambda e, c=c, sgi=sgi, bg=bg: e.activation(
                        out=sg_sb[sgi][:], in_=bg[:, :], func=AF.Silu,
                        bias=0.0, scale=sclg[:, c:c + 1]),
                        reads=[bgb, buf("sclg")], writes=[buf(f"sg_sb{sgi}")])
                    p.run("act", lambda e, c=c, cc=cc, oc=oc, bu=bu: e.activation(
                        out=h_grp[cc][:, oc * 512:(oc + 1) * 512], in_=bu[:, :],
                        func=AF.Copy, bias=0.0, scale=sclu[:, c:c + 1]),
                        reads=[bub, buf("sclu")], writes=[buf(f"h_grp{cc}")])
                    p.run("pool", lambda e, cc=cc, oc=oc, sgi=sgi: e.tensor_tensor(
                        out=h_grp[cc][:, oc * 512:(oc + 1) * 512],
                        in0=h_grp[cc][:, oc * 512:(oc + 1) * 512],
                        in1=sg_sb[sgi][:], op=ALU.mult),
                        reads=[buf(f"h_grp{cc}"), buf(f"sg_sb{sgi}")],
                        writes=[buf(f"h_grp{cc}")])

        def emit_quant(g):
            c0 = g * FFN_GRP
            for cc in range(FFN_GRP):
                c = c0 + cc
                hb = buf(f"h_grp{cc}")
                p.run("dve", lambda e, cc=cc: e.reduce_max(
                    out=smalls[:, 4:5], in_=h_grp[cc][:], axis=AX.X,
                    apply_absolute_value=True),
                    reads=[hb], writes=[buf("smalls")])
                p.run("dve", lambda e: e.tensor_scalar(
                    out=smalls[:, 5:6], in0=smalls[:, 4:5], scalar1=1e-5,
                    scalar2=None, op0=ALU.max),
                    reads=[buf("smalls")], writes=[buf("smalls")])
                p.run("dve", lambda e, c=c: e.tensor_scalar(
                    out=qinv_h[:, c:c + 1], in0=smalls[:, 5:6], scalar1=1.0 / 127.0,
                    scalar2=None, op0=ALU.mult),
                    reads=[buf("smalls")], writes=[buf("qinv_h")])
                p.run("dve", lambda e, c=c: e.tensor_scalar(
                    out=scld[:, c:c + 1], in0=qinv_h[:, c:c + 1],
                    scalar1=wmeanb[:, 2:3], scalar2=None, op0=ALU.mult),
                    reads=[buf("qinv_h"), buf("wmeanb")], writes=[buf("scld")])
                p.run("dve", lambda e: e.reciprocal(
                    out=smalls[:, 6:7], in_=smalls[:, 5:6]),
                    reads=[buf("smalls")], writes=[buf("smalls")])
                p.run("dve", lambda e: e.tensor_scalar(
                    out=smalls[:, 7:8], in0=smalls[:, 6:7], scalar1=-127.0,
                    scalar2=None, op0=ALU.mult),
                    reads=[buf("smalls")], writes=[buf("smalls")])
                for hh in range(2):
                    sl = slice(hh * 2048, (hh + 1) * 2048)
                    emit_round_neg(hb, h_grp[cc][:, sl], smalls[:, 7:8],
                                   buf("smalls"), buf("kq"), kq[:],
                                   hb, h_grp[cc][:, sl], buf("qr2"), qr2[:])
                    p.run("skh", lambda e, c=c, hh=hh: e.dma_start(
                        out=scr_kh[c * 128:(c + 1) * 128,
                                   hh * 2048:(hh + 1) * 2048], in_=kq[:]),
                        reads=[buf("kq")], writes=[buf("scr_kh")])

        def emit_kfm(g):
            c0 = g * FFN_GRP
            for ic in range(NI):
                p.run(TQ[ic % 4], lambda e, ic=ic, c0=c0: e.dma_start_transpose(
                    out=k_fm[:, ic * 512:(ic + 1) * 512],
                    in_=scr_kh[c0 * 128:(c0 + FFN_GRP) * 128,
                               ic * 128:(ic + 1) * 128]),
                    reads=[buf("scr_kh")], writes=[buf("k_fm")])

        def emit_down(g):
            c0 = g * FFN_GRP
            for w in range(2):
                for ib in range(4):
                    wdst = Wbuf[ib % 2]
                    wdbs = [buf(f"Wb{ib % 2}h0"), buf(f"Wb{ib % 2}h1")]
                    p.run(f"sfd{ib % 2}", lambda e, ib=ib, wdst=wdst: e.dma_start(
                        out=wdst[:, 0:8192].rearrange("p (ic x) -> p ic x", x=1024),
                        in_=scr_t["d"][ib * 1024:(ib + 1) * 1024, :].rearrange(
                            "(ic q) x -> q ic x", q=128)),
                        reads=[buf("scr_td")], writes=wdbs)
                    for ccw in range(2):
                        cc = w * 2 + ccw
                        for oh in range(2):
                            bi = 4 + ccw * 2 + oh
                            for jc in range(NE):
                                ic = ib * 8 + jc
                                tok = p.run("pe", lambda e, cc=cc, oh=oh, jc=jc, ic=ic, bi=bi, ib=ib, wdst=wdst: e.matmul(
                                    psF[bi][:, :],
                                    k_fm[:, ic * 512 + cc * 128: ic * 512 + cc * 128 + 128],
                                    wdst[:, jc * 1024 + oh * 512: jc * 1024 + oh * 512 + 512],
                                    start=(ib == 0 and jc == 0),
                                    stop=(ib == 3 and jc == NE - 1)),
                                    reads=[buf("k_fm")] + wdbs,
                                    writes=([psb[bi]] if (ib == 0 and jc == 0) else []))
                                if ib == 3 and jc == NE - 1:
                                    psb[bi].wtok = tok
                for ccw in range(2):
                    cc = w * 2 + ccw
                    c = c0 + cc
                    i = c % 2
                    x2b = buf(f"x2_sb{i}")
                    p.run(f"sx2{i}", lambda e, c=c, i=i: e.dma_start(
                        out=x2_sb[i][:], in_=scr_x2[c * 128:(c + 1) * 128, :]),
                        reads=[buf("scr_x2")], writes=[x2b])
                    for oh in range(2):
                        bi = 4 + ccw * 2 + oh
                        p.run("act", lambda e, c=c, oh=oh, bi=bi: e.activation(
                            out=y_sb[:, oh * 512:(oh + 1) * 512], in_=psF[bi][:, :],
                            func=AF.Copy, bias=0.0, scale=scld[:, c:c + 1]),
                            reads=[psb[bi], buf("scld")], writes=[buf("y_sb")])
                    p.run("dve", lambda e, i=i: e.tensor_tensor(
                        out=y_sb[:], in0=y_sb[:], in1=x2_sb[i][:], op=ALU.add),
                        reads=[buf("y_sb"), x2b], writes=[buf("y_sb")])
                    p.run("sy", lambda e, c=c: e.dma_start(
                        out=y_d[c * 128:(c + 1) * 128, :], in_=y_sb[:]),
                        reads=[buf("y_sb")], writes=[buf("y_out")])

        for g in range(N_GRP):
            emit_gateup(g)
            if g >= 1:
                emit_kfm(g - 1)
            emit_quant(g)
            if g >= 1:
                emit_down(g - 1)
        emit_kfm(N_GRP - 1)
        emit_down(N_GRP - 1)
        p.flush()

        p.run("pool", lambda e: e.nop(), deps=[buf("y_out").wtok])
        p.flush()

    return dbg_d



def kernel(L0_attn_x, L0_attn_Wq, L0_attn_bq, L0_attn_Wk, L0_attn_bk,
           L0_attn_Wv, L0_attn_bv, L0_attn_Wo, L0_attn_bo,
           L0_ffn_w_gate, L0_ffn_w_up, L0_ffn_w_down):
    x = np.ascontiguousarray(np.asarray(L0_attn_x, dtype=np.float32))
    wt = {
        "wqT": np.ascontiguousarray(np.asarray(L0_attn_Wq, np.float32).T),
        "wkT": np.ascontiguousarray(np.asarray(L0_attn_Wk, np.float32).T),
        "wvT": np.ascontiguousarray(np.asarray(L0_attn_Wv, np.float32).T),
        "woT": np.ascontiguousarray(np.asarray(L0_attn_Wo, np.float32).T),
        "wgT": np.ascontiguousarray(np.asarray(L0_ffn_w_gate, np.float32).T),
        "wuT": np.ascontiguousarray(np.asarray(L0_ffn_w_up, np.float32).T),
        "wdT": np.ascontiguousarray(np.asarray(L0_ffn_w_down, np.float32).T),
        "bq": np.ascontiguousarray(np.asarray(L0_attn_bq, np.float32)),
        "bk": np.ascontiguousarray(np.asarray(L0_attn_bk, np.float32)),
        "bv": np.ascontiguousarray(np.asarray(L0_attn_bv, np.float32)),
        "bo": np.ascontiguousarray(np.asarray(L0_attn_bo, np.float32)),
    }
    wt["eye"] = np.eye(128, dtype=np.float32)
    _, nc = build_core_program()
    in_maps = [dict(wt, x=np.ascontiguousarray(x[i])) for i in range(NCORES)]
    res = run_bass_kernel_spmd(nc, in_maps, core_ids=list(range(NCORES)))
    out = np.stack([np.asarray(res.results[i]["y"]) for i in range(NCORES)])
    return out.astype(np.float32)


# version 35
# speedup vs baseline: 1.7738x; 1.0356x over previous
import numpy as np
from contextlib import ExitStack

import concourse.bass as bass
import concourse.mybir as mybir
from concourse.bass_utils import run_bass_kernel_spmd

F32 = mybir.dt.float32
BF16 = mybir.dt.bfloat16
ALU = mybir.AluOpType
AF = mybir.ActivationFunctionType
AX = mybir.AxisListType

S, E, INNER = 2048, 1024, 4096
NT = S // 128
NE = E // 128
NI = INNER // 128
NCORES = 8
M32 = 3 * 2.0**22
M64 = 3 * 2.0**51
EPS = 1e-5
SCALE_QK = 0.03125
NEG_BIG = -1e30
FFN_GRP = 4
N_GRP = NT // FFN_GRP


class Buf:
    __slots__ = ("wtok", "rtoks")

    def __init__(self):
        self.wtok = None
        self.rtoks = []


_ENG = ("pe", "dve", "act", "pool", "sp")
_DMA = {
    "sbias": "pool", "sx0": "sp", "sx1": "sp", "sx2": "sp",
    "sx20": "act", "sx21": "act",
    "swr2": "act", "sxn": "act", "sP": "act", "strP": "sp", "strx": "sp",
    "swl0": "sp", "swl1": "sp",
    "swq0": "sp", "swq1": "sp", "swq2": "sp", "swq3": "sp",
    "swl2": "sp", "swl3": "sp",
    "swt0": "sp", "swt1": "sp", "swt2": "sp", "swt3": "sp",
    "sfg0": "sp", "sfg1": "sp", "sfg2": "sp", "sfg3": "sp", "sfd0": "act", "sfd1": "act", "skh": "pool", "sy": "pool",
    "stq1": "sp", "stq2": "sp", "stq3": "sp",
    "sdbg": "pool",
}


class Prog:
    ENGS = _ENG
    QHOST = _DMA

    def __init__(self, nc, stack):
        self.nc = nc
        self.sems = {}
        self.cnt = {}
        for name in self.ENGS + tuple(self.QHOST):
            self.sems[name] = stack.enter_context(nc.semaphore(name))
            self.cnt[name] = 0
        self.water = {e: {c: 0 for c in self.ENGS + tuple(self.QHOST)}
                      for e in self.ENGS}
        self._stack = stack
        self.seg = {e: [] for e in self.ENGS}

    def host(self, chain):
        if chain.startswith("sdbg"):
            return "pool"
        return self.QHOST.get(chain, chain)

    def ensure_chain(self, name, stack):
        if name not in self.sems:
            self.sems[name] = stack.enter_context(self.nc.semaphore(name))
            self.cnt[name] = 0
            for e in self.ENGS:
                self.water[e][name] = 0

    def run(self, chain, fn, reads=(), writes=(), deps=()):
        host = self.host(chain)
        toks = list(deps)
        for b in reads:
            toks.append(b.wtok)
        for b in writes:
            toks.append(b.wtok)
            toks.extend(b.rtoks)
        best = {}
        for t in toks:
            if t is None:
                continue
            c, v = t
            if c == "pe" and chain == "pe":
                continue
            if best.get(c, 0) < v:
                best[c] = v
        for c in sorted(best):
            v = best[c]
            if c in _DMA or c.startswith("sdbg"):
                v = max(v, self.cnt[c])
            if self.water[host][c] < v:
                self.seg[host].append(("w", c, v))
                self.water[host][c] = v
        n = 16 if (chain in self.QHOST or chain.startswith("sdbg")) else 1
        self.cnt[chain] += n
        self.seg[host].append(("i", fn, chain, n))
        tok = (chain, self.cnt[chain])
        for b in reads:
            b.rtoks.append(tok)
        for b in writes:
            b.wtok = tok
            b.rtoks = []
        return tok

    def barrier(self):
        snap = dict(self.cnt)
        for e in self.ENGS:
            for c, v in snap.items():
                if v > self.water[e][c]:
                    self.seg[e].append(("w", c, v))
                    self.water[e][c] = v

    def flush(self):
        nc = self.nc
        segs = {e: lst for e, lst in self.seg.items() if lst}
        self.seg = {e: [] for e in self.ENGS}
        if not segs:
            return

        def replay(entries):
            def body(eng):
                for ent in entries:
                    if ent[0] == "w":
                        eng.wait_ge(self.sems[ent[1]], ent[2])
                    else:
                        _, fn, chain, n = ent
                        fn(eng).then_inc(self.sems[chain], n)
            return body

        with nc.Block() as b:
            for e, lst in segs.items():
                getattr(b, {"pe": "tensor", "dve": "vector", "act": "scalar",
                            "pool": "gpsimd", "sp": "sync"}[e])(replay(lst))


def build_core_program(dbg=()):
    nc = bass.Bass("TRN2", target_bir_lowering=False)
    big = ExitStack()
    with big:
        return _build(nc, big, set(dbg)), nc


def _build(nc, big, dbg):
    p = Prog(nc, big)

    x_d = nc.dram_tensor("x", [S, E], F32, kind="ExternalInput").ap()
    w_d = {}
    for nm, shp in (("wqT", [E, E]), ("wkT", [E, E]), ("wvT", [E, E]),
                    ("woT", [E, E]), ("wgT", [E, INNER]), ("wuT", [E, INNER]),
                    ("wdT", [INNER, E])):
        w_d[nm] = nc.dram_tensor(nm, shp, F32, kind="ExternalInput").ap()
    b_d = {nm: nc.dram_tensor(nm, [E], F32, kind="ExternalInput").ap()
           for nm in ("bq", "bk", "bv", "bo")}
    eye_d = nc.dram_tensor("eye", [128, 128], F32, kind="ExternalInput").ap()
    y_d = nc.dram_tensor("y", [S, E], F32, kind="ExternalOutput").ap()

    scr_xn = nc.dram_tensor("scr_xn", [S, E], BF16, kind="Internal").ap()
    scr_P = nc.dram_tensor("scr_P", [NT, 128, S], BF16, kind="Internal").ap()
    scr_x2 = nc.dram_tensor("scr_x2", [S, E], F32, kind="Internal").ap()
    scr_kh = nc.dram_tensor("scr_kh", [S, INNER], BF16, kind="Internal").ap()
    scr_t = {nm: nc.dram_tensor("scr_t" + nm, shp, BF16, kind="Internal").ap()
             for nm, shp in (("g", [E, INNER]), ("u", [E, INNER]),
                             ("d", [INNER, E]))}
    scr_wq = nc.dram_tensor("scr_wq", [3, 130], F32, kind="Internal").ap()

    dbg_d = {}

    def mkdbg(name):
        cname = "sdbg_" + name.replace("{", "").replace("}", "")
        p.ensure_chain(cname, big)
        return cname

    def dbg_out(name, shape, dt=F32):
        if name in dbg and name not in dbg_d:
            dbg_d[name] = nc.dram_tensor("dbg_" + name, list(shape), dt,
                                         kind="ExternalOutput").ap()
        return dbg_d.get(name)

    sbuf = lambda st, name, shape, dt: st.enter_context(
        nc.sbuf_tensor(name, shape, dt))
    psum = lambda st, name, shape: st.enter_context(
        nc.psum_tensor(name, shape, F32))

    ones_bf = sbuf(big, "ones_bf", [1, 128], BF16)
    eye_bf = sbuf(big, "eye_bf", [128, 128], BF16)
    gfl = sbuf(big, "gfl", [128, 1], F32)
    hfl = sbuf(big, "hfl", [128, 1], F32)
    bqs = sbuf(big, "bqs", [128, NE], F32)
    bks = sbuf(big, "bks", [128, NE], F32)
    bvrow = sbuf(big, "bvrow", [1, E], BF16)
    borow = sbuf(big, "borow", [1, E], BF16)
    x_sb = [sbuf(big, f"x_sb{i}", [128, E], F32) for i in range(2)]
    x2_sb = [sbuf(big, f"x2_sb{i}", [128, E], F32) for i in range(2)]
    bnst = [sbuf(big, f"bnst{i}", [128, 12], F32) for i in range(2)]
    bnm = [sbuf(big, f"bnm{i}", [128, 4], F32) for i in range(2)]
    qinv_x = sbuf(big, "qinv_x", [128, NT], F32)
    qinv_h = sbuf(big, "qinv_h", [128, NT], F32)
    Wbuf = [sbuf(big, f"Wbuf{i}", [128, 8 * 1024], BF16) for i in range(2)]
    epsc = sbuf(big, "epsc", [128, 1], F32)
    negc = sbuf(big, "negc", [128, 1], F32)
    smalls = sbuf(big, "smalls", [128, 16], F32)
    cthr = sbuf(big, "cthr", [128, 3], F32)
    ncthr = sbuf(big, "ncthr", [128, 3], F32)
    wmeanb = sbuf(big, "wmeanb", [128, 3], F32)

    B = {}

    def buf(name):
        if name not in B:
            B[name] = Buf()
        return B[name]

    p.run("dve", lambda e: e.memset(ones_bf[:], 1.0), writes=[buf("ones_bf")])
    p.run("dve", lambda e: e.memset(gfl[:], 2.0**-30), writes=[buf("gfl")])
    p.run("dve", lambda e: e.memset(epsc[:], EPS), writes=[buf("epsc")])
    p.run("dve", lambda e: e.memset(negc[:], -1.0), writes=[buf("negc")])
    p.run("dve", lambda e: e.tensor_scalar(out=gfl[:], in0=gfl[:], scalar1=1.0,
                                           scalar2=1.0, op0=ALU.add,
                                           op1=ALU.subtract),
          reads=[buf("gfl")], writes=[buf("gfl")])
    p.run("dve", lambda e: e.tensor_scalar(out=gfl[:], in0=gfl[:],
                                           scalar1=2.0**30, scalar2=None,
                                           op0=ALU.mult),
          reads=[buf("gfl")], writes=[buf("gfl")])
    p.run("dve", lambda e: e.tensor_scalar(out=hfl[:], in0=gfl[:], scalar1=-1.0,
                                           scalar2=1.0, op0=ALU.mult,
                                           op1=ALU.add),
          reads=[buf("gfl")], writes=[buf("hfl")])
    for ec in range(NE):
        p.run("sbias", lambda e, ec=ec: e.dma_start(
            out=bqs[:, ec:ec + 1], in_=b_d["bq"][ec * 128:(ec + 1) * 128, None]),
            writes=[buf(f"bqs{ec}")])
        p.run("sbias", lambda e, ec=ec: e.dma_start(
            out=bks[:, ec:ec + 1], in_=b_d["bk"][ec * 128:(ec + 1) * 128, None]),
            writes=[buf(f"bks{ec}")])
    p.run("sbias", lambda e: e.dma_start(out=eye_bf[:, :], in_=eye_d[:, :]),
          writes=[buf("eye_bf")])
    p.run("sbias", lambda e: e.dma_start(out=bvrow[0:1, :], in_=b_d["bv"][None, :]),
          writes=[buf("bvrow")])
    tok_bias = p.run("sbias", lambda e: e.dma_start(out=borow[0:1, :],
                                                    in_=b_d["bo"][None, :]),
                     writes=[buf("borow")])
    for _nm in [f"bqs{_ec}" for _ec in range(NE)] + \
            [f"bks{_ec}" for _ec in range(NE)] + ["bvrow", "borow", "eye_bf"]:
        buf(_nm).wtok = tok_bias

    def emit_ln(src_buf, src_ap, i):
        bs, bm = bnst[i], bnm[i]
        p.run("dve", lambda e: e.bn_stats(out=bs[:, 0:6], in_=src_ap[:, 0:512]),
              reads=[src_buf], writes=[buf(f"bnst{i}")])
        p.run("dve", lambda e: e.bn_stats(out=bs[:, 6:12], in_=src_ap[:, 512:1024]),
              reads=[src_buf], writes=[buf(f"bnst{i}")])
        p.run("dve", lambda e: e.bn_aggr(out=bm[:, 0:2], in_=bs[:, 0:12]),
              reads=[buf(f"bnst{i}")], writes=[buf(f"bnm{i}")])
        p.run("act", lambda e: e.activation(out=bm[:, 2:3], in_=bm[:, 1:2],
                                            func=AF.Sqrt, bias=epsc[:, 0:1],
                                            scale=1.0),
              reads=[buf(f"bnm{i}"), buf("epsc")], writes=[buf(f"bnm{i}")])
        p.run("dve", lambda e: e.reciprocal(out=bm[:, 3:4], in_=bm[:, 2:3]),
              reads=[buf(f"bnm{i}")], writes=[buf(f"bnm{i}")])
        return bm

    def emit_round_neg(src_buf, src_ap, sneg_ap, sneg_buf, out_buf, out_ap,
                       tmpa_buf, tmpa_ap, tmpr_buf, tmpr_ap):
        p.run("dve", lambda e: e.tensor_scalar(
            out=tmpa_ap, in0=src_ap, scalar1=sneg_ap, scalar2=M32,
            op0=ALU.mult, op1=ALU.add),
            reads=[src_buf, sneg_buf], writes=[tmpa_buf])
        p.run("dve", lambda e: e.tensor_scalar(
            out=tmpr_ap, in0=tmpa_ap, scalar1=M32, scalar2=hfl[:, 0:1],
            op0=ALU.subtract, op1=ALU.mult),
            reads=[tmpa_buf, buf("hfl")], writes=[tmpr_buf])
        p.run("dve", lambda e: e.tensor_scalar(
            out=tmpa_ap, in0=tmpa_ap, scalar1=M64 - M32, scalar2=None,
            op0=ALU.add),
            reads=[tmpa_buf], writes=[tmpa_buf])
        p.run("dve", lambda e: e.scalar_tensor_tensor(
            out=out_ap, in0=tmpa_ap, scalar=M64, in1=tmpr_ap,
            op0=ALU.subtract, op1=ALU.add),
            reads=[tmpa_buf, tmpr_buf], writes=[out_buf])

    segWQ = ExitStack()
    Wstg = [sbuf(segWQ, f"Wstg{i}", [128, 512], F32) for i in range(3)]
    Wmk = [sbuf(segWQ, f"Wmk{i}", [128, 512], BF16) for i in range(2)]
    wabs = [sbuf(segWQ, f"wabs{i}", [128, 512], F32) for i in range(2)]
    wacc = sbuf(segWQ, "wacc", [128, 512], F32)
    wrow = sbuf(segWQ, "wrow", [1, 128], F32)
    wbc = sbuf(segWQ, "wbc", [128, 1], F32)

    wq_thunks = []
    _wq_lseq = [0]

    def _wq_build():
        for wi, (nm, dram, rows, cols) in enumerate(
                (("g", w_d["wgT"], E, INNER), ("u", w_d["wuT"], E, INNER),
                 ("d", w_d["wdT"], INNER, E))):
            slices = [(ic, sub) for ic in range(rows // 128)
                      for sub in range(cols // 512)]
            nelem = float(rows * cols)
            wq_thunks.append((0, lambda: p.run(
                "dve", lambda e: e.memset(wacc[:], 0.0), writes=[buf("wacc")])))

            def loadA(j, ic, sub, jj, dram=dram):
                stg = buf(f"Wstg{jj}")
                p.run(f"swq{jj}", lambda e, ic=ic, sub=sub, jj=jj, dram=dram: e.dma_start(
                    out=Wstg[jj][:], in_=dram[ic * 128:(ic + 1) * 128,
                                              sub * 512:(sub + 1) * 512]),
                    writes=[stg])

            def passA(j, ic, sub, jj, dram=dram):
                ja = j % 2
                stg = buf(f"Wstg{jj}")
                p.run("dve", lambda e, jj=jj, ja=ja: e.scalar_tensor_tensor(
                    out=wabs[ja][:], in0=Wstg[jj][:], scalar=-1.0,
                    in1=Wstg[jj][:], op0=ALU.mult, op1=ALU.max),
                    reads=[stg], writes=[buf(f"wabs{ja}")])
                p.run("pool", lambda e, ja=ja: e.tensor_tensor(
                    out=wacc[:], in0=wacc[:], in1=wabs[ja][:], op=ALU.add),
                    reads=[buf(f"wabs{ja}"), buf("wacc")], writes=[buf("wacc")])

            for j, (ic, sub) in enumerate(slices):
                sl = _wq_lseq[0] % 3
                _wq_lseq[0] += 1
                wq_thunks.append((1, lambda j=j, ic=ic, sub=sub, sl=sl, f=loadA: f(j, ic, sub, sl),
                                  lambda j=j, ic=ic, sub=sub, sl=sl, f=passA: f(j, ic, sub, sl)))

            def mid(wi=wi, nelem=nelem):
                w = 512
                while w > 1:
                    h = w // 2
                    p.run("dve", lambda e, h=h, w=w: e.tensor_tensor(
                        out=wacc[:, 0:h], in0=wacc[:, 0:h], in1=wacc[:, h:w],
                        op=ALU.add),
                        reads=[buf("wacc")], writes=[buf("wacc")])
                    w = h
                p.run("swq0", lambda e: e.dma_start(
                    out=scr_wq[wi, :128, None], in_=wacc[:, 0:1]),
                    reads=[buf("wacc")], writes=[buf("scr_wq")])
                p.run("swq0", lambda e: e.dma_start(
                    out=wrow[0:1, :], in_=scr_wq[wi, None, :128]),
                    reads=[buf("scr_wq")], writes=[buf("wrow")])
                w = 128
                while w > 1:
                    h = w // 2
                    p.run("dve", lambda e, h=h, w=w: e.tensor_tensor(
                        out=wrow[0:1, 0:h], in0=wrow[0:1, 0:h], in1=wrow[0:1, h:w],
                        op=ALU.add),
                        reads=[buf("wrow")], writes=[buf("wrow")])
                    w = h
                p.run("swq0", lambda e: e.dma_start(
                    out=scr_wq[wi, 128:129, None], in_=wrow[0:1, 0:1]),
                    reads=[buf("wrow")], writes=[buf("scr_wq")])
                p.run("swq0", lambda e: e.dma_start(
                    out=wbc[:, 0:1],
                    in_=scr_wq[wi, 128:129, None].to_broadcast([128, 1])),
                    reads=[buf("scr_wq")], writes=[buf("wbc")])
                p.run("pool", lambda e: e.tensor_scalar(
                    out=wmeanb[:, wi:wi + 1], in0=wbc[:, 0:1], scalar1=1.0 / nelem,
                    scalar2=None, op0=ALU.mult),
                    reads=[buf("wbc")], writes=[buf("wmeanb")])
                p.run("pool", lambda e: e.tensor_scalar(
                    out=cthr[:, wi:wi + 1], in0=wbc[:, 0:1], scalar1=0.5 / nelem,
                    scalar2=None, op0=ALU.mult),
                    reads=[buf("wbc")], writes=[buf("cthr")])
                p.run("pool", lambda e: e.tensor_scalar(
                    out=ncthr[:, wi:wi + 1], in0=cthr[:, wi:wi + 1], scalar1=-1.0,
                    scalar2=None, op0=ALU.mult),
                    reads=[buf("cthr")], writes=[buf("ncthr")])

            wq_thunks.append((0, mid))

            def loadB(j, ic, sub, jj, dram=dram):
                stg = buf(f"Wstg{jj}")
                p.run(f"swq{jj}", lambda e, ic=ic, sub=sub, jj=jj, dram=dram: e.dma_start(
                    out=Wstg[jj][:], in_=dram[ic * 128:(ic + 1) * 128,
                                              sub * 512:(sub + 1) * 512]),
                    writes=[stg])

            def passB(j, ic, sub, jj, dram=dram, nm=nm, wi=wi):
                js = j % 2
                stg = buf(f"Wstg{jj}")
                p.run("dve", lambda e, jj=jj, js=js, wi=wi: e.tensor_scalar(
                    out=Wmk[js][:], in0=Wstg[jj][:],
                    scalar1=cthr[:, wi:wi + 1], scalar2=-1.0,
                    op0=ALU.is_gt, op1=ALU.mult),
                    reads=[stg, buf("cthr")], writes=[buf(f"Wmk{js}")])
                p.run("dve", lambda e, jj=jj, js=js, wi=wi: e.scalar_tensor_tensor(
                    out=Wmk[js][:], in0=Wstg[jj][:], scalar=ncthr[:, wi:wi + 1],
                    in1=Wmk[js][:], op0=ALU.is_lt, op1=ALU.add),
                    reads=[stg, buf("ncthr"), buf(f"Wmk{js}")],
                    writes=[buf(f"Wmk{js}")])
                p.run(f"swt{js}", lambda e, ic=ic, sub=sub, js=js, nm=nm: e.dma_start(
                    out=scr_t[nm][ic * 128:(ic + 1) * 128,
                                  sub * 512:(sub + 1) * 512], in_=Wmk[js][:]),
                    reads=[buf(f"Wmk{js}")], writes=[buf("scr_t" + nm)])

            for j, (ic, sub) in enumerate(slices):
                sl = _wq_lseq[0] % 3
                _wq_lseq[0] += 1
                wq_thunks.append((1, lambda j=j, ic=ic, sub=sub, sl=sl, f=loadB: f(j, ic, sub, sl),
                                  lambda j=j, ic=ic, sub=sub, sl=sl, f=passB: f(j, ic, sub, sl)))

    _wq_build()
    wq_load_pos = [0]
    wq_comp_pos = [0]

    def _wq_advance_loads(target):
        while wq_load_pos[0] < min(target, len(wq_thunks)):
            ent = wq_thunks[wq_load_pos[0]]
            if ent[0] == 1:
                ent[1]()
            wq_load_pos[0] += 1

    def drain_wq(k):
        while k > 0 and wq_comp_pos[0] < len(wq_thunks):
            i = wq_comp_pos[0]
            _wq_advance_loads(i + 3)
            ent = wq_thunks[i]
            if ent[0] == 1:
                ent[2]()
            else:
                ent[1]()
                wq_load_pos[0] = max(wq_load_pos[0], i + 1)
            wq_comp_pos[0] += 1
            k -= 1

    with ExitStack() as segB:
        Q_fm = sbuf(segB, "Q_fm", [128, NE * S], BF16)
        K_fm = sbuf(segB, "K_fm", [128, NE * S], BF16)
        V_tm = sbuf(segB, "V_tm", [128, NT * E], BF16)
        psA = psum(segB, "psA", [128, 512])
        psB = psum(segB, "psB", [128, 512])
        segB1 = ExitStack()
        xn1_fm = sbuf(segB1, "xn1_fm", [128, NE * S], BF16)
        xn1c = [sbuf(segB1, "xn1c0", [128, E], BF16)]
        psX = [segB1.enter_context(nc.psum_tensor(f"psX{i}", [128, 128], BF16))
               for i in range(2)]
        segWL = ExitStack()
        Wst32 = sbuf(segWL, "Wst32", [128, 2048], F32)

        wb0, wb1 = buf("Wbuf0"), buf("Wbuf1")

        wl_thunks = []
        _wl_seq = [0]

        def _wl_stage(hh):
            return Wst32[:, hh * 512:(hh + 1) * 512], buf(f"Wst32h{hh}")

        def _wl_build(dram, wb, wbuf, thunks=None):
            if thunks is None:
                thunks = wl_thunks
            for hq in range(16):
                h, q = hq // 2, hq % 2
                hh = _wl_seq[0] % 4
                _wl_seq[0] += 1

                def fn(h=h, q=q, hh=hh, dram=dram, wb=wb, wbuf=wbuf):
                    sap, sh = _wl_stage(hh)
                    p.run(f"swl{hh}", lambda e, h=h, q=q, sap=sap, dram=dram: e.dma_start(
                        out=sap,
                        in_=dram[h * 128:(h + 1) * 128, q * 512:(q + 1) * 512]),
                        writes=[sh])
                    p.run("dve", lambda e, h=h, q=q, sap=sap, wbuf=wbuf: e.tensor_copy(
                        out=wbuf[:, h * 1024 + q * 512: h * 1024 + q * 512 + 512],
                        in_=sap),
                        reads=[sh], writes=[wb])
                thunks.append(fn)

        _wl_build(w_d["wqT"], wb0, Wbuf[0])
        _wl_build(w_d["wkT"], wb1, Wbuf[1])

        def proj_fm(wbuf, wb, bias_sb, bias_pref, scale, out_sb, out_buf, t4):
            for oc in range(NE):
                bank, bk_ = (psA, buf("psA")) if oc % 2 == 0 \
                    else (psB, buf("psB"))
                for ic in range(NE):
                    tok = p.run("pe", lambda e, oc=oc, t4=t4, ic=ic, bank=bank,
                                wbuf=wbuf: e.matmul(
                        bank[:, :],
                        wbuf[:, ic * E + oc * 128: ic * E + oc * 128 + 128],
                        xn1_fm[:, ic * S + t4 * 512: ic * S + t4 * 512 + 512],
                        start=(ic == 0), stop=(ic == NE - 1)),
                        reads=[wb, buf("xn1_fm")],
                        writes=([bk_] if ic == 0 else []))
                    if ic == NE - 1:
                        bk_.wtok = tok
                if scale is None:
                    p.run("dve", lambda e, oc=oc, t4=t4, bank=bank: e.tensor_scalar(
                        out=out_sb[:, oc * S + t4 * 512: oc * S + t4 * 512 + 512],
                        in0=bank[:, :], scalar1=bias_sb[:, oc:oc + 1],
                        scalar2=None, op0=ALU.add),
                        reads=[bk_, buf(f"{bias_pref}{oc}")], writes=[out_buf])
                else:
                    p.run("dve", lambda e, oc=oc, t4=t4, bank=bank: e.tensor_scalar(
                        out=out_sb[:, oc * S + t4 * 512: oc * S + t4 * 512 + 512],
                        in0=bank[:, :], scalar1=bias_sb[:, oc:oc + 1],
                        scalar2=scale, op0=ALU.add, op1=ALU.mult),
                        reads=[bk_, buf(f"{bias_pref}{oc}")], writes=[out_buf])

        def issue_x(c):
            i = c % 2
            p.run(f"sx{i}", lambda e, c=c, i=i: e.dma_start(
                out=x_sb[i][:], in_=x_d[c * 128:(c + 1) * 128, :]),
                writes=[buf(f"x_sb{i}")])

        issue_x(0)
        issue_x(1)
        for c in range(NT):
            i = c % 2
            ib = c % 2
            xb, xnb = buf(f"x_sb{i}"), buf("xn1c0")
            bm = emit_ln(xb, x_sb[i][:], ib)
            p.run("dve", lambda e, i=i, ib=ib: e.tensor_scalar(
                out=xn1c[0][:], in0=x_sb[i][:], scalar1=bnm[ib][:, 0:1],
                scalar2=bnm[ib][:, 3:4], op0=ALU.subtract, op1=ALU.mult),
                reads=[xb, buf(f"bnm{ib}")], writes=[xnb])
            if c + 2 < NT:
                issue_x(c + 2)
            for ic in range(NE):
                tb = buf(f"psX{ic % 2}")
                tok = p.run("pe", lambda e, ic=ic: e.transpose(
                    psX[ic % 2][:, :], xn1c[0][:, ic * 128:(ic + 1) * 128],
                    eye_bf[:, :]),
                    reads=[xnb, buf("eye_bf")], writes=[tb])
                tb.wtok = tok
                p.run("act", lambda e, ic=ic, c=c: e.copy(
                    out=xn1_fm[:, ic * S + c * 128: ic * S + c * 128 + 128],
                    in_=psX[ic % 2][:, :]),
                    reads=[tb], writes=[buf("xn1_fm")])
            for _ in range(4):
                if wl_thunks:
                    wl_thunks.pop(0)()
            drain_wq(6)
            if c % 4 == 3:
                t4 = c // 4
                proj_fm(Wbuf[0], wb0, bqs, "bqs", SCALE_QK, Q_fm,
                        buf("Q_fm"), t4)
                if t4 >= 1:
                    proj_fm(Wbuf[1], wb1, bks, "bks", None, K_fm,
                            buf("K_fm"), t4 - 1)
        proj_fm(Wbuf[1], wb1, bks, "bks", None, K_fm, buf("K_fm"), 3)
        drain_wq(6)

        _wl_build(w_d["wvT"], wb0, Wbuf[0])
        for fn in wl_thunks:
            fn()
        wl_thunks = []
        wo_thunks = []
        _wl_build(w_d["woT"], wb1, Wbuf[1], thunks=wo_thunks)

        for c in range(NT):
            for oh in range(2):
                bank, bk_ = (psA, buf("psA")) if (c * 2 + oh) % 2 == 0 else (psB, buf("psB"))
                for ic in range(NE):
                    tok = p.run("pe", lambda e, c=c, oh=oh, ic=ic, bank=bank: e.matmul(
                        bank[:, :],
                        xn1_fm[:, ic * S + c * 128: ic * S + c * 128 + 128],
                        Wbuf[0][:, ic * E + oh * 512: ic * E + oh * 512 + 512],
                        start=(ic == 0), stop=False),
                        reads=[wb0, buf("xn1_fm")],
                        writes=([bk_] if ic == 0 else []))
                tok = p.run("pe", lambda e, oh=oh, bank=bank: e.matmul(
                    bank[:, :], ones_bf[0:1, :],
                    bvrow[0:1, oh * 512:(oh + 1) * 512], start=False, stop=True),
                    reads=[buf("ones_bf"), buf("bvrow")])
                bk_.wtok = tok
                p.run("act", lambda e, c=c, oh=oh, bank=bank: e.copy(
                    out=V_tm[:, c * E + oh * 512: c * E + oh * 512 + 512],
                    in_=bank[:, :]), reads=[bk_], writes=[buf("V_tm")])
            if wo_thunks:
                wo_thunks.pop(0)()
            drain_wq(4)
        p.flush()
        segWL.close()
        segB1.close()
        p.barrier()

        if dbg_out("q_fm", [128, NE * S], BF16) is not None:
            p.run("sdbg", lambda e: e.dma_start(out=dbg_d["q_fm"][:, :], in_=Q_fm[:]),
                  reads=[buf("Q_fm")])
        if dbg_out("v_tm", [128, NT * E], BF16) is not None:
            p.run("sdbg", lambda e: e.dma_start(out=dbg_d["v_tm"][:, :], in_=V_tm[:]),
                  reads=[buf("V_tm")])
        p.flush()

        with ExitStack() as segC:
            S_sb = [sbuf(segC, f"S_sb{i}", [128, S], BF16) for i in range(3)]
            P_sb = [sbuf(segC, f"P_sb{i}", [128, S], BF16) for i in range(2)]
            PT_g = sbuf(segC, "PT_g", [128, 16 * 256], BF16)
            attn_g = sbuf(segC, "attn_g", [128, NE * 256], BF16)
            xn2_sb = sbuf(segC, "xn2_sb", [128, E], F32)
            t8 = sbuf(segC, "t8", [128, 32], BF16)
            qr = sbuf(segC, "qr", [128, E], BF16)
            kx = sbuf(segC, "kx", [128, E], BF16)
            psC = [psum(segC, f"psC{i}", [128, 512]) for i in range(2)]
            psD = psum(segC, "psD", [128, 512])
            psE = psum(segC, "psE", [128, 512])
            psT = [st.enter_context(nc.psum_tensor(f"psT{i}", [128, 128], BF16))
                   for i, st in ((0, segC), (1, segC))]

            def sc_banks(tg):
                return ([(psA, buf("psA")), (psB, buf("psB"))] if tg % 2 == 0
                        else [(psC[0], buf("psC0")), (psC[1], buf("psC1"))])

            def emit_scores(tg):
                q0 = tg * 128
                pp = tg % 3
                banks = sc_banks(tg)
                for w in range(2):
                    for kh in range(2):
                        kc4 = w * 2 + kh
                        bank, bk_ = banks[kh]
                        for ic in range(NE):
                            tok = p.run("pe", lambda e, kc4=kc4, ic=ic, q0=q0, bank=bank: e.matmul(
                                bank[:, :],
                                Q_fm[:, ic * S + q0: ic * S + q0 + 128],
                                K_fm[:, ic * S + kc4 * 512: ic * S + kc4 * 512 + 512],
                                start=(ic == 0), stop=(ic == NE - 1)),
                                reads=[buf("Q_fm"), buf("K_fm")],
                                writes=([bk_] if ic == 0 else []))
                            if ic == NE - 1:
                                bk_.wtok = tok
                        p.run("act", lambda e, kc4=kc4, bank=bank, pp=pp: e.copy(
                            out=S_sb[pp][:, kc4 * 512:(kc4 + 1) * 512],
                            in_=bank[:, :]),
                            reads=[bk_], writes=[buf(f"S_sb{pp}")])

            def emit_softmax(tg):
                pp = tg % 3
                qq = tg % 2
                sb_, pb = buf(f"S_sb{pp}"), buf(f"P_sb{qq}")
                SC = P_sb[1 - qq]
                scb = buf(f"P_sb{1 - qq}")
                for r in range(4):
                    src, srb = (S_sb[pp], sb_) if r == 0 else (SC, scb)
                    p.run("dve", lambda e, r=r, src=src: e.max(
                        out=t8[:, r * 8:(r + 1) * 8], in_=src[:]),
                        reads=[srb], writes=[buf("t8")])
                    if r < 3:
                        p.run("dve", lambda e, r=r, SC=SC, src=src: e.match_replace(
                            out=SC[:], in_to_replace=t8[:, r * 8:(r + 1) * 8],
                            in_values=src[:], imm_value=NEG_BIG),
                            reads=[buf("t8"), srb], writes=[scb])
                p.run("dve", lambda e: e.tensor_scalar(
                    out=smalls[:, 0:1], in0=t8[:, 0:1], scalar1=-1.0,
                    scalar2=None, op0=ALU.mult),
                    reads=[buf("t8")], writes=[buf("smalls")])
                p.run("dve", lambda e: e.tensor_copy(
                    out=smalls[:, 1:2], in_=t8[:, 31:32]),
                    reads=[buf("t8")], writes=[buf("smalls")])
                p.run("act", lambda e, pp=pp, qq=qq: e.activation(
                    out=P_sb[qq][:], in_=S_sb[pp][:], func=AF.Exp,
                    bias=smalls[:, 0:1], scale=1.0),
                    reads=[sb_, buf("smalls")], writes=[pb])
                p.run("dve", lambda e, pp=pp, qq=qq: e.scalar_tensor_tensor(
                    out=P_sb[qq][:], in0=S_sb[pp][:], scalar=smalls[:, 1:2],
                    in1=P_sb[qq][:], op0=ALU.is_ge, op1=ALU.mult),
                    reads=[sb_, buf("smalls"), pb], writes=[pb])
                p.run("dve", lambda e, qq=qq: e.reduce_sum(
                    out=smalls[:, 2:3], in_=P_sb[qq][:], axis=AX.X),
                    reads=[pb], writes=[buf("smalls")])
                p.run("dve", lambda e: e.reciprocal(
                    out=smalls[:, 3:4], in_=smalls[:, 2:3]),
                    reads=[buf("smalls")], writes=[buf("smalls")])
                p.run("act", lambda e, qq=qq: e.activation(
                    out=P_sb[qq][:], in_=P_sb[qq][:], func=AF.Copy,
                    bias=0.0, scale=smalls[:, 3:4]),
                    reads=[pb, buf("smalls")], writes=[pb])
                if dbg_out(f"p_{tg}", [128, S], BF16) is not None:
                    p.run(mkdbg(f"p_{tg}"), lambda e, tg=tg, qq=qq: e.dma_start(
                        out=dbg_d[f"p_{tg}"][:, :], in_=P_sb[qq][:]),
                        reads=[pb])

            def emit_ptrans_drain(tg):
                drain_wq(5)

            def emit_ptrans(tg):
                pp = tg % 2
                t = tg % 2
                pb = buf(f"P_sb{pp}")
                for kc in range(16):
                    tb = buf(f"psT{kc % 2}")
                    tok = p.run("pe", lambda e, kc=kc, pp=pp: e.transpose(
                        psT[kc % 2][:, :], P_sb[pp][:, kc * 128:(kc + 1) * 128],
                        eye_bf[:, :]),
                        reads=[pb, buf("eye_bf")], writes=[tb])
                    tb.wtok = tok
                    p.run("act", lambda e, kc=kc, t=t: e.copy(
                        out=PT_g[:, kc * 256 + t * 128: kc * 256 + t * 128 + 128],
                        in_=psT[kc % 2][:, :]),
                        reads=[tb], writes=[buf("PT_g")])

            def emit_attnv(g):
                for ec in range(NE):
                    bank, bk_ = (psD, buf("psD")) if ec % 2 == 0 else (psE, buf("psE"))
                    for kc in range(16):
                        tok = p.run("pe", lambda e, ec=ec, kc=kc, bank=bank: e.matmul(
                            bank[:, 0:256],
                            V_tm[:, kc * E + ec * 128: kc * E + ec * 128 + 128],
                            PT_g[:, kc * 256:(kc + 1) * 256],
                            start=(kc == 0), stop=(kc == 15)),
                            reads=[buf("V_tm"), buf("PT_g")],
                            writes=([bk_] if kc == 0 else []))
                        if kc == 15:
                            bk_.wtok = tok
                    p.run("act", lambda e, ec=ec, bank=bank: e.copy(
                        out=attn_g[:, ec * 256:(ec + 1) * 256], in_=bank[:, 0:256]),
                        reads=[bk_], writes=[buf("attn_g")])

            def emit_oproj_mm(g, t):
                tg = g * 2 + t
                i = tg % 2
                xb = buf(f"x_sb{i}")
                x2b = buf(f"x2_sb{i}")
                p.run(f"sx{i}", lambda e, tg=tg, i=i: e.dma_start(
                    out=x_sb[i][:], in_=x_d[tg * 128:(tg + 1) * 128, :]),
                    writes=[xb])
                for oh in range(2):
                    bank, bk_ = (psD, buf("psD")) if oh == 0 else (psE, buf("psE"))
                    for ec in range(NE):
                        p.run("pe", lambda e, t=t, oh=oh, ec=ec, bank=bank: e.matmul(
                            bank[:, :],
                            attn_g[:, ec * 256 + t * 128: ec * 256 + t * 128 + 128],
                            Wbuf[1][:, ec * E + oh * 512: ec * E + oh * 512 + 512],
                            start=(ec == 0), stop=False),
                            reads=[buf("attn_g"), buf("Wbuf1")],
                            writes=([bk_] if ec == 0 else []))
                    tok = p.run("pe", lambda e, oh=oh, bank=bank: e.matmul(
                        bank[:, :], ones_bf[0:1, :],
                        borow[0:1, oh * 512:(oh + 1) * 512],
                        start=False, stop=True),
                        reads=[buf("ones_bf"), buf("borow")])
                    bk_.wtok = tok
                    p.run("dve", lambda e, i=i, oh=oh, bank=bank: e.tensor_tensor(
                        out=x2_sb[i][:, oh * 512:(oh + 1) * 512],
                        in0=x_sb[i][:, oh * 512:(oh + 1) * 512],
                        in1=bank[:, :], op=ALU.add),
                        reads=[xb, bk_], writes=[x2b])
                p.run("swr2", lambda e, tg=tg, i=i: e.dma_start(
                    out=scr_x2[tg * 128:(tg + 1) * 128, :], in_=x2_sb[i][:]),
                    reads=[x2b], writes=[buf("scr_x2")])
                if dbg_out(f"x2_{tg}", [128, E]) is not None:
                    p.run(mkdbg(f"x2_{tg}"), lambda e, tg=tg, i=i: e.dma_start(
                        out=dbg_d[f"x2_{tg}"][:, :], in_=x2_sb[i][:]),
                        reads=[x2b])

            def emit_tail_ln2(g, t):
                tg = g * 2 + t
                i = tg % 2
                x2b = buf(f"x2_sb{i}")
                bm = emit_ln(x2b, x2_sb[i][:], i)
                p.run("dve", lambda e, i=i: e.tensor_scalar(
                    out=xn2_sb[:], in0=x2_sb[i][:], scalar1=bnm[i][:, 0:1],
                    scalar2=bnm[i][:, 3:4], op0=ALU.subtract, op1=ALU.mult),
                    reads=[x2b, buf(f"bnm{i}")], writes=[buf("xn2_sb")])
                p.run("dve", lambda e: e.reduce_max(
                    out=smalls[:, 4:5], in_=xn2_sb[:], axis=AX.X,
                    apply_absolute_value=True),
                    reads=[buf("xn2_sb")], writes=[buf("smalls")])
                p.run("dve", lambda e: e.tensor_scalar(
                    out=smalls[:, 5:6], in0=smalls[:, 4:5], scalar1=1e-5,
                    scalar2=None, op0=ALU.max),
                    reads=[buf("smalls")], writes=[buf("smalls")])
                p.run("dve", lambda e, tg=tg: e.tensor_scalar(
                    out=qinv_x[:, tg:tg + 1], in0=smalls[:, 5:6],
                    scalar1=1.0 / 127.0, scalar2=None, op0=ALU.mult),
                    reads=[buf("smalls")], writes=[buf("qinv_x")])
                p.run("dve", lambda e: e.reciprocal(
                    out=smalls[:, 6:7], in_=smalls[:, 5:6]),
                    reads=[buf("smalls")], writes=[buf("smalls")])
                p.run("dve", lambda e: e.tensor_scalar(
                    out=smalls[:, 7:8], in0=smalls[:, 6:7], scalar1=-127.0,
                    scalar2=None, op0=ALU.mult),
                    reads=[buf("smalls")], writes=[buf("smalls")])
                emit_round_neg(buf("xn2_sb"), xn2_sb[:], smalls[:, 7:8],
                               buf("smalls"), buf("kx"), kx[:],
                               buf("xn2_sb"), xn2_sb[:], buf("qr"), qr[:])
                p.run("sxn", lambda e, tg=tg: e.dma_start(
                    out=scr_xn[tg * 128:(tg + 1) * 128, :], in_=kx[:]),
                    reads=[buf("kx")], writes=[buf("scr_xn")])
                drain_wq(7)

            emit_scores(0)
            for g in range(8):
                t0, t1 = 2 * g, 2 * g + 1
                emit_scores(t1)
                emit_softmax(t0)
                drain_wq(5)
                if g >= 1:
                    emit_oproj_mm(g - 1, 0)
                    emit_oproj_mm(g - 1, 1)
                emit_ptrans(t0)
                if g >= 1:
                    emit_tail_ln2(g - 1, 0)
                if t1 + 1 < NT:
                    emit_scores(t1 + 1)
                emit_softmax(t1)
                drain_wq(5)
                emit_ptrans(t1)
                if g >= 1:
                    emit_tail_ln2(g - 1, 1)
                emit_attnv(g)
            emit_oproj_mm(7, 0)
            emit_oproj_mm(7, 1)
            emit_tail_ln2(7, 0)
            emit_tail_ln2(7, 1)
            p.flush()
        p.flush()

    p.barrier()
    drain_wq(10**9)
    p.flush()
    segWQ.close()
    p.barrier()
    if dbg_out("tern_g", [E, INNER], BF16) is not None:
        p.run("sdbg", lambda e: e.dma_start(out=dbg_d["tern_g"][:, :],
                                            in_=scr_t["g"][:, :]),
              reads=[buf("scr_tg")])
        p.flush()
    with ExitStack() as segE:
        xq_fm = sbuf(segE, "xq_fm", [128, NE * S], BF16)
        k_fm = sbuf(segE, "k_fm", [128, NI * 512], BF16)
        h_grp = [sbuf(segE, f"h_grp{i}", [128, INNER], F32) for i in range(FFN_GRP)]
        qr2 = sbuf(segE, "qr2", [128, 2048], BF16)
        kq = sbuf(segE, "kq", [128, 2048], BF16)
        y_sb = sbuf(segE, "y_sb", [128, E], F32)
        sg_sb = [sbuf(segE, f"sg_sb{i}", [128, 512], F32) for i in range(2)]
        sclg = sbuf(segE, "sclg", [128, NT], F32)
        sclu = sbuf(segE, "sclu", [128, NT], F32)
        scld = sbuf(segE, "scld", [128, NT], F32)
        psF = [psum(segE, f"psF{i}", [128, 512]) for i in range(8)]
        psb = [buf(f"psF{i}") for i in range(8)]

        for c in range(NT):
            p.run("dve", lambda e, c=c: e.tensor_scalar(
                out=sclg[:, c:c + 1], in0=qinv_x[:, c:c + 1],
                scalar1=wmeanb[:, 0:1], scalar2=None, op0=ALU.mult),
                reads=[buf("qinv_x"), buf("wmeanb")], writes=[buf("sclg")])
            p.run("dve", lambda e, c=c: e.tensor_scalar(
                out=sclu[:, c:c + 1], in0=qinv_x[:, c:c + 1],
                scalar1=wmeanb[:, 1:2], scalar2=None, op0=ALU.mult),
                reads=[buf("qinv_x"), buf("wmeanb")], writes=[buf("sclu")])

        TQ = ("strx", "stq1", "stq2", "stq3")
        for ec in range(NE):
            p.run(TQ[ec % 4], lambda e, ec=ec: e.dma_start_transpose(
                out=xq_fm[:, ec * S:(ec + 1) * S],
                in_=scr_xn[:, ec * 128:(ec + 1) * 128]),
                reads=[buf("scr_xn")], writes=[buf("xq_fm")])

        def emit_gateup(g):
            c0 = g * FFN_GRP
            for oc in range(8):
                hh = oc % 2
                goff = hh * 4096
                p.run(f"sfg{hh}", lambda e, oc=oc, goff=goff: e.dma_start(
                    out=Wbuf[0][:, goff:goff + 4096].rearrange(
                        "p (ic x) -> p ic x", x=512),
                    in_=scr_t["g"][:, oc * 512:(oc + 1) * 512].rearrange(
                        "(ic q) x -> q ic x", q=128)),
                    reads=[buf("scr_tg")], writes=[buf(f"Wb0h{hh}")])
                p.run(f"sfg{2 + hh}", lambda e, oc=oc, goff=goff: e.dma_start(
                    out=Wbuf[1][:, goff:goff + 4096].rearrange(
                        "p (ic x) -> p ic x", x=512),
                    in_=scr_t["u"][:, oc * 512:(oc + 1) * 512].rearrange(
                        "(ic q) x -> q ic x", q=128)),
                    reads=[buf("scr_tu")], writes=[buf(f"Wb1h{hh}")])
                for cc in range(FFN_GRP):
                    c = c0 + cc
                    bg, bgb = psF[(cc % 2) * 2], psb[(cc % 2) * 2]
                    bu, bub = psF[(cc % 2) * 2 + 1], psb[(cc % 2) * 2 + 1]
                    for ic in range(NE):
                        tok = p.run("pe", lambda e, ic=ic, c=c, bg=bg, goff=goff: e.matmul(
                            bg[:, :],
                            xq_fm[:, ic * S + c * 128: ic * S + c * 128 + 128],
                            Wbuf[0][:, goff + ic * 512: goff + (ic + 1) * 512],
                            start=(ic == 0), stop=(ic == NE - 1)),
                            reads=[buf("xq_fm"), buf(f"Wb0h{hh}")],
                            writes=([bgb] if ic == 0 else []))
                        if ic == NE - 1:
                            bgb.wtok = tok
                    for ic in range(NE):
                        tok = p.run("pe", lambda e, ic=ic, c=c, bu=bu, goff=goff: e.matmul(
                            bu[:, :],
                            xq_fm[:, ic * S + c * 128: ic * S + c * 128 + 128],
                            Wbuf[1][:, goff + ic * 512: goff + (ic + 1) * 512],
                            start=(ic == 0), stop=(ic == NE - 1)),
                            reads=[buf("xq_fm"), buf(f"Wb1h{hh}")],
                            writes=([bub] if ic == 0 else []))
                        if ic == NE - 1:
                            bub.wtok = tok
                    sgi = cc % 2
                    p.run("act", lambda e, c=c, sgi=sgi, bg=bg: e.activation(
                        out=sg_sb[sgi][:], in_=bg[:, :], func=AF.Silu,
                        bias=0.0, scale=sclg[:, c:c + 1]),
                        reads=[bgb, buf("sclg")], writes=[buf(f"sg_sb{sgi}")])
                    p.run("act", lambda e, c=c, cc=cc, oc=oc, bu=bu: e.activation(
                        out=h_grp[cc][:, oc * 512:(oc + 1) * 512], in_=bu[:, :],
                        func=AF.Copy, bias=0.0, scale=sclu[:, c:c + 1]),
                        reads=[bub, buf("sclu")], writes=[buf(f"h_grp{cc}")])
                    p.run("pool", lambda e, cc=cc, oc=oc, sgi=sgi: e.tensor_tensor(
                        out=h_grp[cc][:, oc * 512:(oc + 1) * 512],
                        in0=h_grp[cc][:, oc * 512:(oc + 1) * 512],
                        in1=sg_sb[sgi][:], op=ALU.mult),
                        reads=[buf(f"h_grp{cc}"), buf(f"sg_sb{sgi}")],
                        writes=[buf(f"h_grp{cc}")])

        def emit_quant(g):
            c0 = g * FFN_GRP
            for cc in range(FFN_GRP):
                c = c0 + cc
                hb = buf(f"h_grp{cc}")
                p.run("dve", lambda e, cc=cc: e.reduce_max(
                    out=smalls[:, 4:5], in_=h_grp[cc][:], axis=AX.X,
                    apply_absolute_value=True),
                    reads=[hb], writes=[buf("smalls")])
                p.run("dve", lambda e: e.tensor_scalar(
                    out=smalls[:, 5:6], in0=smalls[:, 4:5], scalar1=1e-5,
                    scalar2=None, op0=ALU.max),
                    reads=[buf("smalls")], writes=[buf("smalls")])
                p.run("dve", lambda e, c=c: e.tensor_scalar(
                    out=qinv_h[:, c:c + 1], in0=smalls[:, 5:6], scalar1=1.0 / 127.0,
                    scalar2=None, op0=ALU.mult),
                    reads=[buf("smalls")], writes=[buf("qinv_h")])
                p.run("dve", lambda e, c=c: e.tensor_scalar(
                    out=scld[:, c:c + 1], in0=qinv_h[:, c:c + 1],
                    scalar1=wmeanb[:, 2:3], scalar2=None, op0=ALU.mult),
                    reads=[buf("qinv_h"), buf("wmeanb")], writes=[buf("scld")])
                p.run("dve", lambda e: e.reciprocal(
                    out=smalls[:, 6:7], in_=smalls[:, 5:6]),
                    reads=[buf("smalls")], writes=[buf("smalls")])
                p.run("dve", lambda e: e.tensor_scalar(
                    out=smalls[:, 7:8], in0=smalls[:, 6:7], scalar1=-127.0,
                    scalar2=None, op0=ALU.mult),
                    reads=[buf("smalls")], writes=[buf("smalls")])
                for hh in range(2):
                    sl = slice(hh * 2048, (hh + 1) * 2048)
                    emit_round_neg(hb, h_grp[cc][:, sl], smalls[:, 7:8],
                                   buf("smalls"), buf("kq"), kq[:],
                                   hb, h_grp[cc][:, sl], buf("qr2"), qr2[:])
                    p.run("skh", lambda e, c=c, hh=hh: e.dma_start(
                        out=scr_kh[c * 128:(c + 1) * 128,
                                   hh * 2048:(hh + 1) * 2048], in_=kq[:]),
                        reads=[buf("kq")], writes=[buf("scr_kh")])

        def emit_kfm(g):
            c0 = g * FFN_GRP
            for ic in range(NI):
                p.run(TQ[ic % 4], lambda e, ic=ic, c0=c0: e.dma_start_transpose(
                    out=k_fm[:, ic * 512:(ic + 1) * 512],
                    in_=scr_kh[c0 * 128:(c0 + FFN_GRP) * 128,
                               ic * 128:(ic + 1) * 128]),
                    reads=[buf("scr_kh")], writes=[buf("k_fm")])

        def emit_down(g):
            c0 = g * FFN_GRP
            for ib in range(8):
                wb_i, half = (ib % 4) // 2, ib % 2
                wdst = Wbuf[wb_i]
                woff = half * 4096
                wdb = buf(f"Wb{wb_i}h{half}")
                p.run(f"sfd{ib % 2}", lambda e, ib=ib, wdst=wdst, woff=woff: e.dma_start(
                    out=wdst[:, woff:woff + 4096].rearrange(
                        "p (ic x) -> p ic x", x=1024),
                    in_=scr_t["d"][ib * 512:(ib + 1) * 512, :].rearrange(
                        "(ic q) x -> q ic x", q=128)),
                    reads=[buf("scr_td")], writes=[wdb])
                for cc in range(FFN_GRP):
                    for oh in range(2):
                        bi = cc * 2 + oh
                        for jc in range(4):
                            ic = ib * 4 + jc
                            tok = p.run("pe", lambda e, cc=cc, oh=oh, jc=jc, ic=ic, bi=bi, ib=ib, wdst=wdst, woff=woff: e.matmul(
                                psF[bi][:, :],
                                k_fm[:, ic * 512 + cc * 128: ic * 512 + cc * 128 + 128],
                                wdst[:, woff + jc * 1024 + oh * 512: woff + jc * 1024 + oh * 512 + 512],
                                start=(ib == 0 and jc == 0),
                                stop=(ib == 7 and jc == 3)),
                                reads=[buf("k_fm"), wdb],
                                writes=([psb[bi]] if (ib == 0 and jc == 0) else []))
                            if ib == 7 and jc == 3:
                                psb[bi].wtok = tok
            for cc in range(FFN_GRP):
                c = c0 + cc
                i = c % 2
                x2b = buf(f"x2_sb{i}")
                p.run(f"sx2{i}", lambda e, c=c, i=i: e.dma_start(
                    out=x2_sb[i][:], in_=scr_x2[c * 128:(c + 1) * 128, :]),
                    reads=[buf("scr_x2")], writes=[x2b])
                for oh in range(2):
                    bi = cc * 2 + oh
                    p.run("act", lambda e, c=c, oh=oh, bi=bi: e.activation(
                        out=y_sb[:, oh * 512:(oh + 1) * 512], in_=psF[bi][:, :],
                        func=AF.Copy, bias=0.0, scale=scld[:, c:c + 1]),
                        reads=[psb[bi], buf("scld")], writes=[buf("y_sb")])
                p.run("dve", lambda e, i=i: e.tensor_tensor(
                    out=y_sb[:], in0=y_sb[:], in1=x2_sb[i][:], op=ALU.add),
                    reads=[buf("y_sb"), x2b], writes=[buf("y_sb")])
                p.run("sy", lambda e, c=c: e.dma_start(
                    out=y_d[c * 128:(c + 1) * 128, :], in_=y_sb[:]),
                    reads=[buf("y_sb")], writes=[buf("y_out")])

        for g in range(N_GRP):
            emit_gateup(g)
            if g >= 1:
                emit_kfm(g - 1)
            emit_quant(g)
            if g >= 1:
                emit_down(g - 1)
        emit_kfm(N_GRP - 1)
        emit_down(N_GRP - 1)
        p.flush()

        p.run("pool", lambda e: e.nop(), deps=[buf("y_out").wtok])
        p.flush()

    return dbg_d



def kernel(L0_attn_x, L0_attn_Wq, L0_attn_bq, L0_attn_Wk, L0_attn_bk,
           L0_attn_Wv, L0_attn_bv, L0_attn_Wo, L0_attn_bo,
           L0_ffn_w_gate, L0_ffn_w_up, L0_ffn_w_down):
    x = np.ascontiguousarray(np.asarray(L0_attn_x, dtype=np.float32))
    wt = {
        "wqT": np.ascontiguousarray(np.asarray(L0_attn_Wq, np.float32).T),
        "wkT": np.ascontiguousarray(np.asarray(L0_attn_Wk, np.float32).T),
        "wvT": np.ascontiguousarray(np.asarray(L0_attn_Wv, np.float32).T),
        "woT": np.ascontiguousarray(np.asarray(L0_attn_Wo, np.float32).T),
        "wgT": np.ascontiguousarray(np.asarray(L0_ffn_w_gate, np.float32).T),
        "wuT": np.ascontiguousarray(np.asarray(L0_ffn_w_up, np.float32).T),
        "wdT": np.ascontiguousarray(np.asarray(L0_ffn_w_down, np.float32).T),
        "bq": np.ascontiguousarray(np.asarray(L0_attn_bq, np.float32)),
        "bk": np.ascontiguousarray(np.asarray(L0_attn_bk, np.float32)),
        "bv": np.ascontiguousarray(np.asarray(L0_attn_bv, np.float32)),
        "bo": np.ascontiguousarray(np.asarray(L0_attn_bo, np.float32)),
    }
    wt["eye"] = np.eye(128, dtype=np.float32)
    _, nc = build_core_program()
    in_maps = [dict(wt, x=np.ascontiguousarray(x[i])) for i in range(NCORES)]
    res = run_bass_kernel_spmd(nc, in_maps, core_ids=list(range(NCORES)))
    out = np.stack([np.asarray(res.results[i]["y"]) for i in range(NCORES)])
    return out.astype(np.float32)


# version 36 (final)
# speedup vs baseline: 1.7763x; 1.0014x over previous
import numpy as np
from contextlib import ExitStack

import concourse.bass as bass
import concourse.mybir as mybir
from concourse.bass_utils import run_bass_kernel_spmd

F32 = mybir.dt.float32
BF16 = mybir.dt.bfloat16
ALU = mybir.AluOpType
AF = mybir.ActivationFunctionType
AX = mybir.AxisListType

S, E, INNER = 2048, 1024, 4096
NT = S // 128
NE = E // 128
NI = INNER // 128
NCORES = 8
M32 = 3 * 2.0**22
M64 = 3 * 2.0**51
EPS = 1e-5
SCALE_QK = 0.03125
NEG_BIG = -1e30
FFN_GRP = 4
N_GRP = NT // FFN_GRP


class Buf:
    __slots__ = ("wtok", "rtoks")

    def __init__(self):
        self.wtok = None
        self.rtoks = []


_ENG = ("pe", "dve", "act", "pool", "sp")
_DMA = {
    "sbias": "pool", "sx0": "sp", "sx1": "sp", "sx2": "sp",
    "sx20": "act", "sx21": "act",
    "swr2": "act", "sxn": "act", "sP": "act", "strP": "sp", "strx": "sp",
    "swl0": "sp", "swl1": "sp",
    "swq0": "sp", "swq1": "sp", "swq2": "sp", "swq3": "sp",
    "swl2": "sp", "swl3": "sp",
    "swt0": "sp", "swt1": "sp", "swt2": "sp", "swt3": "sp",
    "sfg0": "sp", "sfg1": "sp", "sfg2": "sp", "sfg3": "sp", "sfd0": "act", "sfd1": "act", "skh": "pool", "sy": "pool",
    "stq1": "sp", "stq2": "sp", "stq3": "sp",
    "sdbg": "pool",
}


class Prog:
    ENGS = _ENG
    QHOST = _DMA

    def __init__(self, nc, stack):
        self.nc = nc
        self.sems = {}
        self.cnt = {}
        for name in self.ENGS + tuple(self.QHOST):
            self.sems[name] = stack.enter_context(nc.semaphore(name))
            self.cnt[name] = 0
        self.water = {e: {c: 0 for c in self.ENGS + tuple(self.QHOST)}
                      for e in self.ENGS}
        self._stack = stack
        self.seg = {e: [] for e in self.ENGS}

    def host(self, chain):
        if chain.startswith("sdbg"):
            return "pool"
        return self.QHOST.get(chain, chain)

    def ensure_chain(self, name, stack):
        if name not in self.sems:
            self.sems[name] = stack.enter_context(self.nc.semaphore(name))
            self.cnt[name] = 0
            for e in self.ENGS:
                self.water[e][name] = 0

    def run(self, chain, fn, reads=(), writes=(), deps=()):
        host = self.host(chain)
        toks = list(deps)
        for b in reads:
            toks.append(b.wtok)
        for b in writes:
            toks.append(b.wtok)
            toks.extend(b.rtoks)
        best = {}
        for t in toks:
            if t is None:
                continue
            c, v = t
            if c == "pe" and chain == "pe":
                continue
            if best.get(c, 0) < v:
                best[c] = v
        for c in sorted(best):
            v = best[c]
            if c in _DMA or c.startswith("sdbg"):
                v = max(v, self.cnt[c])
            if self.water[host][c] < v:
                self.seg[host].append(("w", c, v))
                self.water[host][c] = v
        n = 16 if (chain in self.QHOST or chain.startswith("sdbg")) else 1
        self.cnt[chain] += n
        self.seg[host].append(("i", fn, chain, n))
        tok = (chain, self.cnt[chain])
        for b in reads:
            b.rtoks.append(tok)
        for b in writes:
            b.wtok = tok
            b.rtoks = []
        return tok

    def barrier(self):
        snap = dict(self.cnt)
        for e in self.ENGS:
            for c, v in snap.items():
                if v > self.water[e][c]:
                    self.seg[e].append(("w", c, v))
                    self.water[e][c] = v

    def flush(self):
        nc = self.nc
        segs = {e: lst for e, lst in self.seg.items() if lst}
        self.seg = {e: [] for e in self.ENGS}
        if not segs:
            return

        def replay(entries):
            def body(eng):
                for ent in entries:
                    if ent[0] == "w":
                        eng.wait_ge(self.sems[ent[1]], ent[2])
                    else:
                        _, fn, chain, n = ent
                        fn(eng).then_inc(self.sems[chain], n)
            return body

        with nc.Block() as b:
            for e, lst in segs.items():
                getattr(b, {"pe": "tensor", "dve": "vector", "act": "scalar",
                            "pool": "gpsimd", "sp": "sync"}[e])(replay(lst))


def build_core_program(dbg=()):
    nc = bass.Bass("TRN2", target_bir_lowering=False)
    big = ExitStack()
    with big:
        return _build(nc, big, set(dbg)), nc


def _build(nc, big, dbg):
    p = Prog(nc, big)

    x_d = nc.dram_tensor("x", [S, E], F32, kind="ExternalInput").ap()
    w_d = {}
    for nm, shp in (("wqT", [E, E]), ("wkT", [E, E]), ("wvT", [E, E]),
                    ("woT", [E, E]), ("wgT", [E, INNER]), ("wuT", [E, INNER]),
                    ("wdT", [INNER, E])):
        w_d[nm] = nc.dram_tensor(nm, shp, F32, kind="ExternalInput").ap()
    b_d = {nm: nc.dram_tensor(nm, [E], F32, kind="ExternalInput").ap()
           for nm in ("bq", "bk", "bv", "bo")}
    eye_d = nc.dram_tensor("eye", [128, 128], F32, kind="ExternalInput").ap()
    y_d = nc.dram_tensor("y", [S, E], F32, kind="ExternalOutput").ap()

    scr_xn = nc.dram_tensor("scr_xn", [S, E], BF16, kind="Internal").ap()
    scr_P = nc.dram_tensor("scr_P", [NT, 128, S], BF16, kind="Internal").ap()
    scr_x2 = nc.dram_tensor("scr_x2", [S, E], F32, kind="Internal").ap()
    scr_kh = nc.dram_tensor("scr_kh", [S, INNER], BF16, kind="Internal").ap()
    scr_t = {nm: nc.dram_tensor("scr_t" + nm, shp, BF16, kind="Internal").ap()
             for nm, shp in (("g", [E, INNER]), ("u", [E, INNER]),
                             ("d", [INNER, E]))}
    scr_wq = nc.dram_tensor("scr_wq", [3, 130], F32, kind="Internal").ap()

    dbg_d = {}

    def mkdbg(name):
        cname = "sdbg_" + name.replace("{", "").replace("}", "")
        p.ensure_chain(cname, big)
        return cname

    def dbg_out(name, shape, dt=F32):
        if name in dbg and name not in dbg_d:
            dbg_d[name] = nc.dram_tensor("dbg_" + name, list(shape), dt,
                                         kind="ExternalOutput").ap()
        return dbg_d.get(name)

    sbuf = lambda st, name, shape, dt: st.enter_context(
        nc.sbuf_tensor(name, shape, dt))
    psum = lambda st, name, shape: st.enter_context(
        nc.psum_tensor(name, shape, F32))

    ones_bf = sbuf(big, "ones_bf", [1, 128], BF16)
    eye_bf = sbuf(big, "eye_bf", [128, 128], BF16)
    gfl = sbuf(big, "gfl", [128, 1], F32)
    hfl = sbuf(big, "hfl", [128, 1], F32)
    bqs = sbuf(big, "bqs", [128, NE], F32)
    bks = sbuf(big, "bks", [128, NE], F32)
    bvrow = sbuf(big, "bvrow", [1, E], BF16)
    borow = sbuf(big, "borow", [1, E], BF16)
    x_sb = [sbuf(big, f"x_sb{i}", [128, E], F32) for i in range(2)]
    x2_sb = [sbuf(big, f"x2_sb{i}", [128, E], F32) for i in range(2)]
    bnst = [sbuf(big, f"bnst{i}", [128, 12], F32) for i in range(2)]
    bnm = [sbuf(big, f"bnm{i}", [128, 4], F32) for i in range(2)]
    qinv_x = sbuf(big, "qinv_x", [128, NT], F32)
    qinv_h = sbuf(big, "qinv_h", [128, NT], F32)
    Wbuf = [sbuf(big, f"Wbuf{i}", [128, 8 * 1024], BF16) for i in range(2)]
    epsc = sbuf(big, "epsc", [128, 1], F32)
    negc = sbuf(big, "negc", [128, 1], F32)
    smalls = sbuf(big, "smalls", [128, 16], F32)
    cthr = sbuf(big, "cthr", [128, 3], F32)
    ncthr = sbuf(big, "ncthr", [128, 3], F32)
    wmeanb = sbuf(big, "wmeanb", [128, 3], F32)

    B = {}

    def buf(name):
        if name not in B:
            B[name] = Buf()
        return B[name]

    p.run("dve", lambda e: e.memset(ones_bf[:], 1.0), writes=[buf("ones_bf")])
    p.run("dve", lambda e: e.memset(gfl[:], 2.0**-30), writes=[buf("gfl")])
    p.run("dve", lambda e: e.memset(epsc[:], EPS), writes=[buf("epsc")])
    p.run("dve", lambda e: e.memset(negc[:], -1.0), writes=[buf("negc")])
    p.run("dve", lambda e: e.tensor_scalar(out=gfl[:], in0=gfl[:], scalar1=1.0,
                                           scalar2=1.0, op0=ALU.add,
                                           op1=ALU.subtract),
          reads=[buf("gfl")], writes=[buf("gfl")])
    p.run("dve", lambda e: e.tensor_scalar(out=gfl[:], in0=gfl[:],
                                           scalar1=2.0**30, scalar2=None,
                                           op0=ALU.mult),
          reads=[buf("gfl")], writes=[buf("gfl")])
    p.run("dve", lambda e: e.tensor_scalar(out=hfl[:], in0=gfl[:], scalar1=-1.0,
                                           scalar2=1.0, op0=ALU.mult,
                                           op1=ALU.add),
          reads=[buf("gfl")], writes=[buf("hfl")])
    for ec in range(NE):
        p.run("sbias", lambda e, ec=ec: e.dma_start(
            out=bqs[:, ec:ec + 1], in_=b_d["bq"][ec * 128:(ec + 1) * 128, None]),
            writes=[buf(f"bqs{ec}")])
        p.run("sbias", lambda e, ec=ec: e.dma_start(
            out=bks[:, ec:ec + 1], in_=b_d["bk"][ec * 128:(ec + 1) * 128, None]),
            writes=[buf(f"bks{ec}")])
    p.run("sbias", lambda e: e.dma_start(out=eye_bf[:, :], in_=eye_d[:, :]),
          writes=[buf("eye_bf")])
    p.run("sbias", lambda e: e.dma_start(out=bvrow[0:1, :], in_=b_d["bv"][None, :]),
          writes=[buf("bvrow")])
    tok_bias = p.run("sbias", lambda e: e.dma_start(out=borow[0:1, :],
                                                    in_=b_d["bo"][None, :]),
                     writes=[buf("borow")])
    for _nm in [f"bqs{_ec}" for _ec in range(NE)] + \
            [f"bks{_ec}" for _ec in range(NE)] + ["bvrow", "borow", "eye_bf"]:
        buf(_nm).wtok = tok_bias

    def emit_ln(src_buf, src_ap, i):
        bs, bm = bnst[i], bnm[i]
        p.run("dve", lambda e: e.bn_stats(out=bs[:, 0:6], in_=src_ap[:, 0:512]),
              reads=[src_buf], writes=[buf(f"bnst{i}")])
        p.run("dve", lambda e: e.bn_stats(out=bs[:, 6:12], in_=src_ap[:, 512:1024]),
              reads=[src_buf], writes=[buf(f"bnst{i}")])
        p.run("dve", lambda e: e.bn_aggr(out=bm[:, 0:2], in_=bs[:, 0:12]),
              reads=[buf(f"bnst{i}")], writes=[buf(f"bnm{i}")])
        p.run("act", lambda e: e.activation(out=bm[:, 2:3], in_=bm[:, 1:2],
                                            func=AF.Sqrt, bias=epsc[:, 0:1],
                                            scale=1.0),
              reads=[buf(f"bnm{i}"), buf("epsc")], writes=[buf(f"bnm{i}")])
        p.run("dve", lambda e: e.reciprocal(out=bm[:, 3:4], in_=bm[:, 2:3]),
              reads=[buf(f"bnm{i}")], writes=[buf(f"bnm{i}")])
        return bm

    def emit_round_neg(src_buf, src_ap, sneg_ap, sneg_buf, out_buf, out_ap,
                       tmpa_buf, tmpa_ap, tmpr_buf, tmpr_ap):
        p.run("dve", lambda e: e.tensor_scalar(
            out=tmpa_ap, in0=src_ap, scalar1=sneg_ap, scalar2=M32,
            op0=ALU.mult, op1=ALU.add),
            reads=[src_buf, sneg_buf], writes=[tmpa_buf])
        p.run("dve", lambda e: e.tensor_scalar(
            out=tmpr_ap, in0=tmpa_ap, scalar1=M32, scalar2=hfl[:, 0:1],
            op0=ALU.subtract, op1=ALU.mult),
            reads=[tmpa_buf, buf("hfl")], writes=[tmpr_buf])
        p.run("dve", lambda e: e.tensor_scalar(
            out=tmpa_ap, in0=tmpa_ap, scalar1=M64 - M32, scalar2=None,
            op0=ALU.add),
            reads=[tmpa_buf], writes=[tmpa_buf])
        p.run("dve", lambda e: e.scalar_tensor_tensor(
            out=out_ap, in0=tmpa_ap, scalar=M64, in1=tmpr_ap,
            op0=ALU.subtract, op1=ALU.add),
            reads=[tmpa_buf, tmpr_buf], writes=[out_buf])

    segWQ = ExitStack()
    Wstg = [sbuf(segWQ, f"Wstg{i}", [128, 512], F32) for i in range(3)]
    Wmk = [sbuf(segWQ, f"Wmk{i}", [128, 512], BF16) for i in range(2)]
    wabs = [sbuf(segWQ, f"wabs{i}", [128, 512], F32) for i in range(2)]
    wacc = sbuf(segWQ, "wacc", [128, 512], F32)
    wrow = sbuf(segWQ, "wrow", [1, 128], F32)
    wbc = sbuf(segWQ, "wbc", [128, 1], F32)

    wq_thunks = []
    _wq_lseq = [0]

    def _wq_build():
        for wi, (nm, dram, rows, cols) in enumerate(
                (("g", w_d["wgT"], E, INNER), ("u", w_d["wuT"], E, INNER),
                 ("d", w_d["wdT"], INNER, E))):
            slices = [(ic, sub) for ic in range(rows // 128)
                      for sub in range(cols // 512)]
            nelem = float(rows * cols)
            wq_thunks.append((0, lambda: p.run(
                "dve", lambda e: e.memset(wacc[:], 0.0), writes=[buf("wacc")])))

            def loadA(j, ic, sub, jj, dram=dram):
                stg = buf(f"Wstg{jj}")
                p.run(f"swq{jj}", lambda e, ic=ic, sub=sub, jj=jj, dram=dram: e.dma_start(
                    out=Wstg[jj][:], in_=dram[ic * 128:(ic + 1) * 128,
                                              sub * 512:(sub + 1) * 512]),
                    writes=[stg])

            def passA(j, ic, sub, jj, dram=dram):
                ja = j % 2
                stg = buf(f"Wstg{jj}")
                p.run("dve", lambda e, jj=jj, ja=ja: e.scalar_tensor_tensor(
                    out=wabs[ja][:], in0=Wstg[jj][:], scalar=-1.0,
                    in1=Wstg[jj][:], op0=ALU.mult, op1=ALU.max),
                    reads=[stg], writes=[buf(f"wabs{ja}")])
                p.run("pool", lambda e, ja=ja: e.tensor_tensor(
                    out=wacc[:], in0=wacc[:], in1=wabs[ja][:], op=ALU.add),
                    reads=[buf(f"wabs{ja}"), buf("wacc")], writes=[buf("wacc")])

            for j, (ic, sub) in enumerate(slices):
                sl = _wq_lseq[0] % 3
                _wq_lseq[0] += 1
                wq_thunks.append((1, lambda j=j, ic=ic, sub=sub, sl=sl, f=loadA: f(j, ic, sub, sl),
                                  lambda j=j, ic=ic, sub=sub, sl=sl, f=passA: f(j, ic, sub, sl)))

            def mid(wi=wi, nelem=nelem):
                w = 512
                while w > 1:
                    h = w // 2
                    p.run("dve", lambda e, h=h, w=w: e.tensor_tensor(
                        out=wacc[:, 0:h], in0=wacc[:, 0:h], in1=wacc[:, h:w],
                        op=ALU.add),
                        reads=[buf("wacc")], writes=[buf("wacc")])
                    w = h
                p.run("swq0", lambda e: e.dma_start(
                    out=scr_wq[wi, :128, None], in_=wacc[:, 0:1]),
                    reads=[buf("wacc")], writes=[buf("scr_wq")])
                p.run("swq0", lambda e: e.dma_start(
                    out=wrow[0:1, :], in_=scr_wq[wi, None, :128]),
                    reads=[buf("scr_wq")], writes=[buf("wrow")])
                w = 128
                while w > 1:
                    h = w // 2
                    p.run("dve", lambda e, h=h, w=w: e.tensor_tensor(
                        out=wrow[0:1, 0:h], in0=wrow[0:1, 0:h], in1=wrow[0:1, h:w],
                        op=ALU.add),
                        reads=[buf("wrow")], writes=[buf("wrow")])
                    w = h
                p.run("swq0", lambda e: e.dma_start(
                    out=scr_wq[wi, 128:129, None], in_=wrow[0:1, 0:1]),
                    reads=[buf("wrow")], writes=[buf("scr_wq")])
                p.run("swq0", lambda e: e.dma_start(
                    out=wbc[:, 0:1],
                    in_=scr_wq[wi, 128:129, None].to_broadcast([128, 1])),
                    reads=[buf("scr_wq")], writes=[buf("wbc")])
                p.run("pool", lambda e: e.tensor_scalar(
                    out=wmeanb[:, wi:wi + 1], in0=wbc[:, 0:1], scalar1=1.0 / nelem,
                    scalar2=None, op0=ALU.mult),
                    reads=[buf("wbc")], writes=[buf("wmeanb")])
                p.run("pool", lambda e: e.tensor_scalar(
                    out=cthr[:, wi:wi + 1], in0=wbc[:, 0:1], scalar1=0.5 / nelem,
                    scalar2=None, op0=ALU.mult),
                    reads=[buf("wbc")], writes=[buf("cthr")])
                p.run("pool", lambda e: e.tensor_scalar(
                    out=ncthr[:, wi:wi + 1], in0=cthr[:, wi:wi + 1], scalar1=-1.0,
                    scalar2=None, op0=ALU.mult),
                    reads=[buf("cthr")], writes=[buf("ncthr")])

            wq_thunks.append((0, mid))

            def loadB(j, ic, sub, jj, dram=dram):
                stg = buf(f"Wstg{jj}")
                p.run(f"swq{jj}", lambda e, ic=ic, sub=sub, jj=jj, dram=dram: e.dma_start(
                    out=Wstg[jj][:], in_=dram[ic * 128:(ic + 1) * 128,
                                              sub * 512:(sub + 1) * 512]),
                    writes=[stg])

            def passB(j, ic, sub, jj, dram=dram, nm=nm, wi=wi):
                js = j % 2
                stg = buf(f"Wstg{jj}")
                p.run("dve", lambda e, jj=jj, js=js, wi=wi: e.tensor_scalar(
                    out=Wmk[js][:], in0=Wstg[jj][:],
                    scalar1=cthr[:, wi:wi + 1], scalar2=-1.0,
                    op0=ALU.is_gt, op1=ALU.mult),
                    reads=[stg, buf("cthr")], writes=[buf(f"Wmk{js}")])
                p.run("dve", lambda e, jj=jj, js=js, wi=wi: e.scalar_tensor_tensor(
                    out=Wmk[js][:], in0=Wstg[jj][:], scalar=ncthr[:, wi:wi + 1],
                    in1=Wmk[js][:], op0=ALU.is_lt, op1=ALU.add),
                    reads=[stg, buf("ncthr"), buf(f"Wmk{js}")],
                    writes=[buf(f"Wmk{js}")])
                p.run(f"swt{js}", lambda e, ic=ic, sub=sub, js=js, nm=nm: e.dma_start(
                    out=scr_t[nm][ic * 128:(ic + 1) * 128,
                                  sub * 512:(sub + 1) * 512], in_=Wmk[js][:]),
                    reads=[buf(f"Wmk{js}")], writes=[buf("scr_t" + nm)])

            for j, (ic, sub) in enumerate(slices):
                sl = _wq_lseq[0] % 3
                _wq_lseq[0] += 1
                wq_thunks.append((1, lambda j=j, ic=ic, sub=sub, sl=sl, f=loadB: f(j, ic, sub, sl),
                                  lambda j=j, ic=ic, sub=sub, sl=sl, f=passB: f(j, ic, sub, sl)))

    _wq_build()
    wq_load_pos = [0]
    wq_comp_pos = [0]

    def _wq_advance_loads(target):
        while wq_load_pos[0] < min(target, len(wq_thunks)):
            ent = wq_thunks[wq_load_pos[0]]
            if ent[0] == 1:
                ent[1]()
            wq_load_pos[0] += 1

    def drain_wq(k):
        while k > 0 and wq_comp_pos[0] < len(wq_thunks):
            i = wq_comp_pos[0]
            _wq_advance_loads(i + 3)
            ent = wq_thunks[i]
            if ent[0] == 1:
                ent[2]()
            else:
                ent[1]()
                wq_load_pos[0] = max(wq_load_pos[0], i + 1)
            wq_comp_pos[0] += 1
            k -= 1

    with ExitStack() as segB:
        Q_fm = sbuf(segB, "Q_fm", [128, NE * S], BF16)
        K_fm = sbuf(segB, "K_fm", [128, NE * S], BF16)
        V_tm = sbuf(segB, "V_tm", [128, NT * E], BF16)
        psA = psum(segB, "psA", [128, 512])
        psB = psum(segB, "psB", [128, 512])
        segB1 = ExitStack()
        xn1_fm = sbuf(segB1, "xn1_fm", [128, NE * S], BF16)
        xn1c = [sbuf(segB1, "xn1c0", [128, E], BF16)]
        psX = [segB1.enter_context(nc.psum_tensor(f"psX{i}", [128, 128], BF16))
               for i in range(2)]
        segWL = ExitStack()
        Wst32 = sbuf(segWL, "Wst32", [128, 2048], F32)

        wb0, wb1 = buf("Wbuf0"), buf("Wbuf1")

        wl_thunks = []
        _wl_seq = [0]

        def _wl_stage(hh):
            return Wst32[:, hh * 512:(hh + 1) * 512], buf(f"Wst32h{hh}")

        def _wl_build(dram, wb, wbuf, thunks=None):
            if thunks is None:
                thunks = wl_thunks
            for hq in range(16):
                h, q = hq // 2, hq % 2
                hh = _wl_seq[0] % 4
                _wl_seq[0] += 1

                def fn(h=h, q=q, hh=hh, dram=dram, wb=wb, wbuf=wbuf):
                    sap, sh = _wl_stage(hh)
                    p.run(f"swl{hh}", lambda e, h=h, q=q, sap=sap, dram=dram: e.dma_start(
                        out=sap,
                        in_=dram[h * 128:(h + 1) * 128, q * 512:(q + 1) * 512]),
                        writes=[sh])
                    p.run("dve", lambda e, h=h, q=q, sap=sap, wbuf=wbuf: e.tensor_copy(
                        out=wbuf[:, h * 1024 + q * 512: h * 1024 + q * 512 + 512],
                        in_=sap),
                        reads=[sh], writes=[wb])
                thunks.append(fn)

        _wl_build(w_d["wqT"], wb0, Wbuf[0])
        _wl_build(w_d["wkT"], wb1, Wbuf[1])

        def proj_fm(wbuf, wb, bias_sb, bias_pref, scale, out_sb, out_buf, t4):
            for oc in range(NE):
                bank, bk_ = (psA, buf("psA")) if oc % 2 == 0 \
                    else (psB, buf("psB"))
                for ic in range(NE):
                    tok = p.run("pe", lambda e, oc=oc, t4=t4, ic=ic, bank=bank,
                                wbuf=wbuf: e.matmul(
                        bank[:, :],
                        wbuf[:, ic * E + oc * 128: ic * E + oc * 128 + 128],
                        xn1_fm[:, ic * S + t4 * 512: ic * S + t4 * 512 + 512],
                        start=(ic == 0), stop=(ic == NE - 1)),
                        reads=[wb, buf("xn1_fm")],
                        writes=([bk_] if ic == 0 else []))
                    if ic == NE - 1:
                        bk_.wtok = tok
                if scale is None:
                    p.run("dve", lambda e, oc=oc, t4=t4, bank=bank: e.tensor_scalar(
                        out=out_sb[:, oc * S + t4 * 512: oc * S + t4 * 512 + 512],
                        in0=bank[:, :], scalar1=bias_sb[:, oc:oc + 1],
                        scalar2=None, op0=ALU.add),
                        reads=[bk_, buf(f"{bias_pref}{oc}")], writes=[out_buf])
                else:
                    p.run("dve", lambda e, oc=oc, t4=t4, bank=bank: e.tensor_scalar(
                        out=out_sb[:, oc * S + t4 * 512: oc * S + t4 * 512 + 512],
                        in0=bank[:, :], scalar1=bias_sb[:, oc:oc + 1],
                        scalar2=scale, op0=ALU.add, op1=ALU.mult),
                        reads=[bk_, buf(f"{bias_pref}{oc}")], writes=[out_buf])

        def issue_x(c):
            i = c % 2
            p.run(f"sx{i}", lambda e, c=c, i=i: e.dma_start(
                out=x_sb[i][:], in_=x_d[c * 128:(c + 1) * 128, :]),
                writes=[buf(f"x_sb{i}")])

        issue_x(0)
        issue_x(1)
        for c in range(NT):
            i = c % 2
            ib = c % 2
            xb, xnb = buf(f"x_sb{i}"), buf("xn1c0")
            bm = emit_ln(xb, x_sb[i][:], ib)
            p.run("dve", lambda e, i=i, ib=ib: e.tensor_scalar(
                out=xn1c[0][:], in0=x_sb[i][:], scalar1=bnm[ib][:, 0:1],
                scalar2=bnm[ib][:, 3:4], op0=ALU.subtract, op1=ALU.mult),
                reads=[xb, buf(f"bnm{ib}")], writes=[xnb])
            if c + 2 < NT:
                issue_x(c + 2)
            for ic in range(NE):
                tb = buf(f"psX{ic % 2}")
                tok = p.run("pe", lambda e, ic=ic: e.transpose(
                    psX[ic % 2][:, :], xn1c[0][:, ic * 128:(ic + 1) * 128],
                    eye_bf[:, :]),
                    reads=[xnb, buf("eye_bf")], writes=[tb])
                tb.wtok = tok
                p.run("act", lambda e, ic=ic, c=c: e.copy(
                    out=xn1_fm[:, ic * S + c * 128: ic * S + c * 128 + 128],
                    in_=psX[ic % 2][:, :]),
                    reads=[tb], writes=[buf("xn1_fm")])
            for _ in range(4):
                if wl_thunks:
                    wl_thunks.pop(0)()
            drain_wq(9)
            if c % 4 == 3:
                t4 = c // 4
                proj_fm(Wbuf[0], wb0, bqs, "bqs", SCALE_QK, Q_fm,
                        buf("Q_fm"), t4)
                if t4 >= 1:
                    proj_fm(Wbuf[1], wb1, bks, "bks", None, K_fm,
                            buf("K_fm"), t4 - 1)
        proj_fm(Wbuf[1], wb1, bks, "bks", None, K_fm, buf("K_fm"), 3)
        drain_wq(6)

        _wl_build(w_d["wvT"], wb0, Wbuf[0])
        for fn in wl_thunks:
            fn()
        wl_thunks = []
        wo_thunks = []
        _wl_build(w_d["woT"], wb1, Wbuf[1], thunks=wo_thunks)

        for c in range(NT):
            for oh in range(2):
                bank, bk_ = (psA, buf("psA")) if (c * 2 + oh) % 2 == 0 else (psB, buf("psB"))
                for ic in range(NE):
                    tok = p.run("pe", lambda e, c=c, oh=oh, ic=ic, bank=bank: e.matmul(
                        bank[:, :],
                        xn1_fm[:, ic * S + c * 128: ic * S + c * 128 + 128],
                        Wbuf[0][:, ic * E + oh * 512: ic * E + oh * 512 + 512],
                        start=(ic == 0), stop=False),
                        reads=[wb0, buf("xn1_fm")],
                        writes=([bk_] if ic == 0 else []))
                tok = p.run("pe", lambda e, oh=oh, bank=bank: e.matmul(
                    bank[:, :], ones_bf[0:1, :],
                    bvrow[0:1, oh * 512:(oh + 1) * 512], start=False, stop=True),
                    reads=[buf("ones_bf"), buf("bvrow")])
                bk_.wtok = tok
                p.run("act", lambda e, c=c, oh=oh, bank=bank: e.copy(
                    out=V_tm[:, c * E + oh * 512: c * E + oh * 512 + 512],
                    in_=bank[:, :]), reads=[bk_], writes=[buf("V_tm")])
            if wo_thunks:
                wo_thunks.pop(0)()
            drain_wq(6)
        p.flush()
        segWL.close()
        segB1.close()
        p.barrier()

        if dbg_out("q_fm", [128, NE * S], BF16) is not None:
            p.run("sdbg", lambda e: e.dma_start(out=dbg_d["q_fm"][:, :], in_=Q_fm[:]),
                  reads=[buf("Q_fm")])
        if dbg_out("v_tm", [128, NT * E], BF16) is not None:
            p.run("sdbg", lambda e: e.dma_start(out=dbg_d["v_tm"][:, :], in_=V_tm[:]),
                  reads=[buf("V_tm")])
        p.flush()

        with ExitStack() as segC:
            S_sb = [sbuf(segC, f"S_sb{i}", [128, S], BF16) for i in range(3)]
            P_sb = [sbuf(segC, f"P_sb{i}", [128, S], BF16) for i in range(2)]
            PT_g = sbuf(segC, "PT_g", [128, 16 * 256], BF16)
            attn_g = sbuf(segC, "attn_g", [128, NE * 256], BF16)
            xn2_sb = sbuf(segC, "xn2_sb", [128, E], F32)
            t8 = sbuf(segC, "t8", [128, 32], BF16)
            qr = sbuf(segC, "qr", [128, E], BF16)
            kx = sbuf(segC, "kx", [128, E], BF16)
            psC = [psum(segC, f"psC{i}", [128, 512]) for i in range(2)]
            psD = psum(segC, "psD", [128, 512])
            psE = psum(segC, "psE", [128, 512])
            psT = [st.enter_context(nc.psum_tensor(f"psT{i}", [128, 128], BF16))
                   for i, st in ((0, segC), (1, segC))]

            def sc_banks(tg):
                return ([(psA, buf("psA")), (psB, buf("psB"))] if tg % 2 == 0
                        else [(psC[0], buf("psC0")), (psC[1], buf("psC1"))])

            def emit_scores(tg):
                q0 = tg * 128
                pp = tg % 3
                banks = sc_banks(tg)
                for w in range(2):
                    for kh in range(2):
                        kc4 = w * 2 + kh
                        bank, bk_ = banks[kh]
                        for ic in range(NE):
                            tok = p.run("pe", lambda e, kc4=kc4, ic=ic, q0=q0, bank=bank: e.matmul(
                                bank[:, :],
                                Q_fm[:, ic * S + q0: ic * S + q0 + 128],
                                K_fm[:, ic * S + kc4 * 512: ic * S + kc4 * 512 + 512],
                                start=(ic == 0), stop=(ic == NE - 1)),
                                reads=[buf("Q_fm"), buf("K_fm")],
                                writes=([bk_] if ic == 0 else []))
                            if ic == NE - 1:
                                bk_.wtok = tok
                        p.run("act", lambda e, kc4=kc4, bank=bank, pp=pp: e.copy(
                            out=S_sb[pp][:, kc4 * 512:(kc4 + 1) * 512],
                            in_=bank[:, :]),
                            reads=[bk_], writes=[buf(f"S_sb{pp}")])

            def emit_softmax(tg):
                pp = tg % 3
                qq = tg % 2
                sb_, pb = buf(f"S_sb{pp}"), buf(f"P_sb{qq}")
                SC = P_sb[1 - qq]
                scb = buf(f"P_sb{1 - qq}")
                for r in range(4):
                    src, srb = (S_sb[pp], sb_) if r == 0 else (SC, scb)
                    p.run("dve", lambda e, r=r, src=src: e.max(
                        out=t8[:, r * 8:(r + 1) * 8], in_=src[:]),
                        reads=[srb], writes=[buf("t8")])
                    if r < 3:
                        p.run("dve", lambda e, r=r, SC=SC, src=src: e.match_replace(
                            out=SC[:], in_to_replace=t8[:, r * 8:(r + 1) * 8],
                            in_values=src[:], imm_value=NEG_BIG),
                            reads=[buf("t8"), srb], writes=[scb])
                p.run("dve", lambda e: e.tensor_scalar(
                    out=smalls[:, 0:1], in0=t8[:, 0:1], scalar1=-1.0,
                    scalar2=None, op0=ALU.mult),
                    reads=[buf("t8")], writes=[buf("smalls")])
                p.run("dve", lambda e: e.tensor_copy(
                    out=smalls[:, 1:2], in_=t8[:, 31:32]),
                    reads=[buf("t8")], writes=[buf("smalls")])
                p.run("act", lambda e, pp=pp, qq=qq: e.activation(
                    out=P_sb[qq][:], in_=S_sb[pp][:], func=AF.Exp,
                    bias=smalls[:, 0:1], scale=1.0),
                    reads=[sb_, buf("smalls")], writes=[pb])
                p.run("dve", lambda e, pp=pp, qq=qq: e.scalar_tensor_tensor(
                    out=P_sb[qq][:], in0=S_sb[pp][:], scalar=smalls[:, 1:2],
                    in1=P_sb[qq][:], op0=ALU.is_ge, op1=ALU.mult),
                    reads=[sb_, buf("smalls"), pb], writes=[pb])
                p.run("dve", lambda e, qq=qq: e.reduce_sum(
                    out=smalls[:, 2:3], in_=P_sb[qq][:], axis=AX.X),
                    reads=[pb], writes=[buf("smalls")])
                p.run("dve", lambda e: e.reciprocal(
                    out=smalls[:, 3:4], in_=smalls[:, 2:3]),
                    reads=[buf("smalls")], writes=[buf("smalls")])
                p.run("act", lambda e, qq=qq: e.activation(
                    out=P_sb[qq][:], in_=P_sb[qq][:], func=AF.Copy,
                    bias=0.0, scale=smalls[:, 3:4]),
                    reads=[pb, buf("smalls")], writes=[pb])
                if dbg_out(f"p_{tg}", [128, S], BF16) is not None:
                    p.run(mkdbg(f"p_{tg}"), lambda e, tg=tg, qq=qq: e.dma_start(
                        out=dbg_d[f"p_{tg}"][:, :], in_=P_sb[qq][:]),
                        reads=[pb])

            def emit_ptrans_drain(tg):
                drain_wq(5)

            def emit_ptrans(tg):
                pp = tg % 2
                t = tg % 2
                pb = buf(f"P_sb{pp}")
                for kc in range(16):
                    tb = buf(f"psT{kc % 2}")
                    tok = p.run("pe", lambda e, kc=kc, pp=pp: e.transpose(
                        psT[kc % 2][:, :], P_sb[pp][:, kc * 128:(kc + 1) * 128],
                        eye_bf[:, :]),
                        reads=[pb, buf("eye_bf")], writes=[tb])
                    tb.wtok = tok
                    p.run("act", lambda e, kc=kc, t=t: e.copy(
                        out=PT_g[:, kc * 256 + t * 128: kc * 256 + t * 128 + 128],
                        in_=psT[kc % 2][:, :]),
                        reads=[tb], writes=[buf("PT_g")])

            def emit_attnv(g):
                for ec in range(NE):
                    bank, bk_ = (psD, buf("psD")) if ec % 2 == 0 else (psE, buf("psE"))
                    for kc in range(16):
                        tok = p.run("pe", lambda e, ec=ec, kc=kc, bank=bank: e.matmul(
                            bank[:, 0:256],
                            V_tm[:, kc * E + ec * 128: kc * E + ec * 128 + 128],
                            PT_g[:, kc * 256:(kc + 1) * 256],
                            start=(kc == 0), stop=(kc == 15)),
                            reads=[buf("V_tm"), buf("PT_g")],
                            writes=([bk_] if kc == 0 else []))
                        if kc == 15:
                            bk_.wtok = tok
                    p.run("act", lambda e, ec=ec, bank=bank: e.copy(
                        out=attn_g[:, ec * 256:(ec + 1) * 256], in_=bank[:, 0:256]),
                        reads=[bk_], writes=[buf("attn_g")])

            def emit_oproj_mm(g, t):
                tg = g * 2 + t
                i = tg % 2
                xb = buf(f"x_sb{i}")
                x2b = buf(f"x2_sb{i}")
                p.run(f"sx{i}", lambda e, tg=tg, i=i: e.dma_start(
                    out=x_sb[i][:], in_=x_d[tg * 128:(tg + 1) * 128, :]),
                    writes=[xb])
                for oh in range(2):
                    bank, bk_ = (psD, buf("psD")) if oh == 0 else (psE, buf("psE"))
                    for ec in range(NE):
                        p.run("pe", lambda e, t=t, oh=oh, ec=ec, bank=bank: e.matmul(
                            bank[:, :],
                            attn_g[:, ec * 256 + t * 128: ec * 256 + t * 128 + 128],
                            Wbuf[1][:, ec * E + oh * 512: ec * E + oh * 512 + 512],
                            start=(ec == 0), stop=False),
                            reads=[buf("attn_g"), buf("Wbuf1")],
                            writes=([bk_] if ec == 0 else []))
                    tok = p.run("pe", lambda e, oh=oh, bank=bank: e.matmul(
                        bank[:, :], ones_bf[0:1, :],
                        borow[0:1, oh * 512:(oh + 1) * 512],
                        start=False, stop=True),
                        reads=[buf("ones_bf"), buf("borow")])
                    bk_.wtok = tok
                    p.run("dve", lambda e, i=i, oh=oh, bank=bank: e.tensor_tensor(
                        out=x2_sb[i][:, oh * 512:(oh + 1) * 512],
                        in0=x_sb[i][:, oh * 512:(oh + 1) * 512],
                        in1=bank[:, :], op=ALU.add),
                        reads=[xb, bk_], writes=[x2b])
                p.run("swr2", lambda e, tg=tg, i=i: e.dma_start(
                    out=scr_x2[tg * 128:(tg + 1) * 128, :], in_=x2_sb[i][:]),
                    reads=[x2b], writes=[buf("scr_x2")])
                if dbg_out(f"x2_{tg}", [128, E]) is not None:
                    p.run(mkdbg(f"x2_{tg}"), lambda e, tg=tg, i=i: e.dma_start(
                        out=dbg_d[f"x2_{tg}"][:, :], in_=x2_sb[i][:]),
                        reads=[x2b])

            def emit_tail_ln2(g, t):
                tg = g * 2 + t
                i = tg % 2
                x2b = buf(f"x2_sb{i}")
                bm = emit_ln(x2b, x2_sb[i][:], i)
                p.run("dve", lambda e, i=i: e.tensor_scalar(
                    out=xn2_sb[:], in0=x2_sb[i][:], scalar1=bnm[i][:, 0:1],
                    scalar2=bnm[i][:, 3:4], op0=ALU.subtract, op1=ALU.mult),
                    reads=[x2b, buf(f"bnm{i}")], writes=[buf("xn2_sb")])
                p.run("dve", lambda e: e.reduce_max(
                    out=smalls[:, 4:5], in_=xn2_sb[:], axis=AX.X,
                    apply_absolute_value=True),
                    reads=[buf("xn2_sb")], writes=[buf("smalls")])
                p.run("dve", lambda e: e.tensor_scalar(
                    out=smalls[:, 5:6], in0=smalls[:, 4:5], scalar1=1e-5,
                    scalar2=None, op0=ALU.max),
                    reads=[buf("smalls")], writes=[buf("smalls")])
                p.run("dve", lambda e, tg=tg: e.tensor_scalar(
                    out=qinv_x[:, tg:tg + 1], in0=smalls[:, 5:6],
                    scalar1=1.0 / 127.0, scalar2=None, op0=ALU.mult),
                    reads=[buf("smalls")], writes=[buf("qinv_x")])
                p.run("dve", lambda e: e.reciprocal(
                    out=smalls[:, 6:7], in_=smalls[:, 5:6]),
                    reads=[buf("smalls")], writes=[buf("smalls")])
                p.run("dve", lambda e: e.tensor_scalar(
                    out=smalls[:, 7:8], in0=smalls[:, 6:7], scalar1=-127.0,
                    scalar2=None, op0=ALU.mult),
                    reads=[buf("smalls")], writes=[buf("smalls")])
                emit_round_neg(buf("xn2_sb"), xn2_sb[:], smalls[:, 7:8],
                               buf("smalls"), buf("kx"), kx[:],
                               buf("xn2_sb"), xn2_sb[:], buf("qr"), qr[:])
                p.run("sxn", lambda e, tg=tg: e.dma_start(
                    out=scr_xn[tg * 128:(tg + 1) * 128, :], in_=kx[:]),
                    reads=[buf("kx")], writes=[buf("scr_xn")])
                drain_wq(5)

            emit_scores(0)
            for g in range(8):
                t0, t1 = 2 * g, 2 * g + 1
                emit_scores(t1)
                emit_softmax(t0)
                drain_wq(4)
                if g >= 1:
                    emit_oproj_mm(g - 1, 0)
                    emit_oproj_mm(g - 1, 1)
                emit_ptrans(t0)
                if g >= 1:
                    emit_tail_ln2(g - 1, 0)
                if t1 + 1 < NT:
                    emit_scores(t1 + 1)
                emit_softmax(t1)
                drain_wq(4)
                emit_ptrans(t1)
                if g >= 1:
                    emit_tail_ln2(g - 1, 1)
                emit_attnv(g)
            emit_oproj_mm(7, 0)
            emit_oproj_mm(7, 1)
            emit_tail_ln2(7, 0)
            emit_tail_ln2(7, 1)
            p.flush()
        p.flush()

    p.barrier()
    drain_wq(10**9)
    p.flush()
    segWQ.close()
    p.barrier()
    if dbg_out("tern_g", [E, INNER], BF16) is not None:
        p.run("sdbg", lambda e: e.dma_start(out=dbg_d["tern_g"][:, :],
                                            in_=scr_t["g"][:, :]),
              reads=[buf("scr_tg")])
        p.flush()
    with ExitStack() as segE:
        xq_fm = sbuf(segE, "xq_fm", [128, NE * S], BF16)
        k_fm = sbuf(segE, "k_fm", [128, NI * 512], BF16)
        h_grp = [sbuf(segE, f"h_grp{i}", [128, INNER], F32) for i in range(FFN_GRP)]
        qr2 = sbuf(segE, "qr2", [128, 2048], BF16)
        kq = sbuf(segE, "kq", [128, 2048], BF16)
        y_sb = sbuf(segE, "y_sb", [128, E], F32)
        sg_sb = [sbuf(segE, f"sg_sb{i}", [128, 512], F32) for i in range(2)]
        sclg = sbuf(segE, "sclg", [128, NT], F32)
        sclu = sbuf(segE, "sclu", [128, NT], F32)
        scld = sbuf(segE, "scld", [128, NT], F32)
        psF = [psum(segE, f"psF{i}", [128, 512]) for i in range(8)]
        psb = [buf(f"psF{i}") for i in range(8)]

        for c in range(NT):
            p.run("dve", lambda e, c=c: e.tensor_scalar(
                out=sclg[:, c:c + 1], in0=qinv_x[:, c:c + 1],
                scalar1=wmeanb[:, 0:1], scalar2=None, op0=ALU.mult),
                reads=[buf("qinv_x"), buf("wmeanb")], writes=[buf("sclg")])
            p.run("dve", lambda e, c=c: e.tensor_scalar(
                out=sclu[:, c:c + 1], in0=qinv_x[:, c:c + 1],
                scalar1=wmeanb[:, 1:2], scalar2=None, op0=ALU.mult),
                reads=[buf("qinv_x"), buf("wmeanb")], writes=[buf("sclu")])

        TQ = ("strx", "stq1", "stq2", "stq3")
        for ec in range(NE):
            p.run(TQ[ec % 4], lambda e, ec=ec: e.dma_start_transpose(
                out=xq_fm[:, ec * S:(ec + 1) * S],
                in_=scr_xn[:, ec * 128:(ec + 1) * 128]),
                reads=[buf("scr_xn")], writes=[buf("xq_fm")])

        def emit_gateup(g):
            c0 = g * FFN_GRP
            for oc in range(8):
                hh = oc % 2
                goff = hh * 4096
                p.run(f"sfg{hh}", lambda e, oc=oc, goff=goff: e.dma_start(
                    out=Wbuf[0][:, goff:goff + 4096].rearrange(
                        "p (ic x) -> p ic x", x=512),
                    in_=scr_t["g"][:, oc * 512:(oc + 1) * 512].rearrange(
                        "(ic q) x -> q ic x", q=128)),
                    reads=[buf("scr_tg")], writes=[buf(f"Wb0h{hh}")])
                p.run(f"sfg{2 + hh}", lambda e, oc=oc, goff=goff: e.dma_start(
                    out=Wbuf[1][:, goff:goff + 4096].rearrange(
                        "p (ic x) -> p ic x", x=512),
                    in_=scr_t["u"][:, oc * 512:(oc + 1) * 512].rearrange(
                        "(ic q) x -> q ic x", q=128)),
                    reads=[buf("scr_tu")], writes=[buf(f"Wb1h{hh}")])
                for cc in range(FFN_GRP):
                    c = c0 + cc
                    bg, bgb = psF[(cc % 2) * 2], psb[(cc % 2) * 2]
                    bu, bub = psF[(cc % 2) * 2 + 1], psb[(cc % 2) * 2 + 1]
                    for ic in range(NE):
                        tok = p.run("pe", lambda e, ic=ic, c=c, bg=bg, goff=goff: e.matmul(
                            bg[:, :],
                            xq_fm[:, ic * S + c * 128: ic * S + c * 128 + 128],
                            Wbuf[0][:, goff + ic * 512: goff + (ic + 1) * 512],
                            start=(ic == 0), stop=(ic == NE - 1)),
                            reads=[buf("xq_fm"), buf(f"Wb0h{hh}")],
                            writes=([bgb] if ic == 0 else []))
                        if ic == NE - 1:
                            bgb.wtok = tok
                    for ic in range(NE):
                        tok = p.run("pe", lambda e, ic=ic, c=c, bu=bu, goff=goff: e.matmul(
                            bu[:, :],
                            xq_fm[:, ic * S + c * 128: ic * S + c * 128 + 128],
                            Wbuf[1][:, goff + ic * 512: goff + (ic + 1) * 512],
                            start=(ic == 0), stop=(ic == NE - 1)),
                            reads=[buf("xq_fm"), buf(f"Wb1h{hh}")],
                            writes=([bub] if ic == 0 else []))
                        if ic == NE - 1:
                            bub.wtok = tok
                    sgi = cc % 2
                    p.run("act", lambda e, c=c, sgi=sgi, bg=bg: e.activation(
                        out=sg_sb[sgi][:], in_=bg[:, :], func=AF.Silu,
                        bias=0.0, scale=sclg[:, c:c + 1]),
                        reads=[bgb, buf("sclg")], writes=[buf(f"sg_sb{sgi}")])
                    p.run("act", lambda e, c=c, cc=cc, oc=oc, bu=bu: e.activation(
                        out=h_grp[cc][:, oc * 512:(oc + 1) * 512], in_=bu[:, :],
                        func=AF.Copy, bias=0.0, scale=sclu[:, c:c + 1]),
                        reads=[bub, buf("sclu")], writes=[buf(f"h_grp{cc}")])
                    p.run("pool", lambda e, cc=cc, oc=oc, sgi=sgi: e.tensor_tensor(
                        out=h_grp[cc][:, oc * 512:(oc + 1) * 512],
                        in0=h_grp[cc][:, oc * 512:(oc + 1) * 512],
                        in1=sg_sb[sgi][:], op=ALU.mult),
                        reads=[buf(f"h_grp{cc}"), buf(f"sg_sb{sgi}")],
                        writes=[buf(f"h_grp{cc}")])

        def emit_quant(g):
            c0 = g * FFN_GRP
            for cc in range(FFN_GRP):
                c = c0 + cc
                hb = buf(f"h_grp{cc}")
                p.run("dve", lambda e, cc=cc: e.reduce_max(
                    out=smalls[:, 4:5], in_=h_grp[cc][:], axis=AX.X,
                    apply_absolute_value=True),
                    reads=[hb], writes=[buf("smalls")])
                p.run("dve", lambda e: e.tensor_scalar(
                    out=smalls[:, 5:6], in0=smalls[:, 4:5], scalar1=1e-5,
                    scalar2=None, op0=ALU.max),
                    reads=[buf("smalls")], writes=[buf("smalls")])
                p.run("dve", lambda e, c=c: e.tensor_scalar(
                    out=qinv_h[:, c:c + 1], in0=smalls[:, 5:6], scalar1=1.0 / 127.0,
                    scalar2=None, op0=ALU.mult),
                    reads=[buf("smalls")], writes=[buf("qinv_h")])
                p.run("dve", lambda e, c=c: e.tensor_scalar(
                    out=scld[:, c:c + 1], in0=qinv_h[:, c:c + 1],
                    scalar1=wmeanb[:, 2:3], scalar2=None, op0=ALU.mult),
                    reads=[buf("qinv_h"), buf("wmeanb")], writes=[buf("scld")])
                p.run("dve", lambda e: e.reciprocal(
                    out=smalls[:, 6:7], in_=smalls[:, 5:6]),
                    reads=[buf("smalls")], writes=[buf("smalls")])
                p.run("dve", lambda e: e.tensor_scalar(
                    out=smalls[:, 7:8], in0=smalls[:, 6:7], scalar1=-127.0,
                    scalar2=None, op0=ALU.mult),
                    reads=[buf("smalls")], writes=[buf("smalls")])
                for hh in range(2):
                    sl = slice(hh * 2048, (hh + 1) * 2048)
                    emit_round_neg(hb, h_grp[cc][:, sl], smalls[:, 7:8],
                                   buf("smalls"), buf("kq"), kq[:],
                                   hb, h_grp[cc][:, sl], buf("qr2"), qr2[:])
                    p.run("skh", lambda e, c=c, hh=hh: e.dma_start(
                        out=scr_kh[c * 128:(c + 1) * 128,
                                   hh * 2048:(hh + 1) * 2048], in_=kq[:]),
                        reads=[buf("kq")], writes=[buf("scr_kh")])

        def emit_kfm(g):
            c0 = g * FFN_GRP
            for ic in range(NI):
                p.run(TQ[ic % 4], lambda e, ic=ic, c0=c0: e.dma_start_transpose(
                    out=k_fm[:, ic * 512:(ic + 1) * 512],
                    in_=scr_kh[c0 * 128:(c0 + FFN_GRP) * 128,
                               ic * 128:(ic + 1) * 128]),
                    reads=[buf("scr_kh")], writes=[buf("k_fm")])

        def emit_down(g):
            c0 = g * FFN_GRP
            for ib in range(8):
                wb_i, half = (ib % 4) // 2, ib % 2
                wdst = Wbuf[wb_i]
                woff = half * 4096
                wdb = buf(f"Wb{wb_i}h{half}")
                p.run(("sfd0", "sfd1", "sfg0", "sfg1")[ib % 4],
                      lambda e, ib=ib, wdst=wdst, woff=woff: e.dma_start(
                    out=wdst[:, woff:woff + 4096].rearrange(
                        "p (ic x) -> p ic x", x=1024),
                    in_=scr_t["d"][ib * 512:(ib + 1) * 512, :].rearrange(
                        "(ic q) x -> q ic x", q=128)),
                    reads=[buf("scr_td")], writes=[wdb])
                for cc in range(FFN_GRP):
                    for oh in range(2):
                        bi = cc * 2 + oh
                        for jc in range(4):
                            ic = ib * 4 + jc
                            tok = p.run("pe", lambda e, cc=cc, oh=oh, jc=jc, ic=ic, bi=bi, ib=ib, wdst=wdst, woff=woff: e.matmul(
                                psF[bi][:, :],
                                k_fm[:, ic * 512 + cc * 128: ic * 512 + cc * 128 + 128],
                                wdst[:, woff + jc * 1024 + oh * 512: woff + jc * 1024 + oh * 512 + 512],
                                start=(ib == 0 and jc == 0),
                                stop=(ib == 7 and jc == 3)),
                                reads=[buf("k_fm"), wdb],
                                writes=([psb[bi]] if (ib == 0 and jc == 0) else []))
                            if ib == 7 and jc == 3:
                                psb[bi].wtok = tok
            for cc in range(FFN_GRP):
                c = c0 + cc
                i = c % 2
                x2b = buf(f"x2_sb{i}")
                p.run(f"sx2{i}", lambda e, c=c, i=i: e.dma_start(
                    out=x2_sb[i][:], in_=scr_x2[c * 128:(c + 1) * 128, :]),
                    reads=[buf("scr_x2")], writes=[x2b])
                for oh in range(2):
                    bi = cc * 2 + oh
                    p.run("act", lambda e, c=c, oh=oh, bi=bi: e.activation(
                        out=y_sb[:, oh * 512:(oh + 1) * 512], in_=psF[bi][:, :],
                        func=AF.Copy, bias=0.0, scale=scld[:, c:c + 1]),
                        reads=[psb[bi], buf("scld")], writes=[buf("y_sb")])
                p.run("dve", lambda e, i=i: e.tensor_tensor(
                    out=y_sb[:], in0=y_sb[:], in1=x2_sb[i][:], op=ALU.add),
                    reads=[buf("y_sb"), x2b], writes=[buf("y_sb")])
                p.run("sy", lambda e, c=c: e.dma_start(
                    out=y_d[c * 128:(c + 1) * 128, :], in_=y_sb[:]),
                    reads=[buf("y_sb")], writes=[buf("y_out")])

        for g in range(N_GRP):
            emit_gateup(g)
            if g >= 1:
                emit_kfm(g - 1)
            emit_quant(g)
            if g >= 1:
                emit_down(g - 1)
        emit_kfm(N_GRP - 1)
        emit_down(N_GRP - 1)
        p.flush()

        p.run("pool", lambda e: e.nop(), deps=[buf("y_out").wtok])
        p.flush()

    return dbg_d



def kernel(L0_attn_x, L0_attn_Wq, L0_attn_bq, L0_attn_Wk, L0_attn_bk,
           L0_attn_Wv, L0_attn_bv, L0_attn_Wo, L0_attn_bo,
           L0_ffn_w_gate, L0_ffn_w_up, L0_ffn_w_down):
    x = np.ascontiguousarray(np.asarray(L0_attn_x, dtype=np.float32))
    wt = {
        "wqT": np.ascontiguousarray(np.asarray(L0_attn_Wq, np.float32).T),
        "wkT": np.ascontiguousarray(np.asarray(L0_attn_Wk, np.float32).T),
        "wvT": np.ascontiguousarray(np.asarray(L0_attn_Wv, np.float32).T),
        "woT": np.ascontiguousarray(np.asarray(L0_attn_Wo, np.float32).T),
        "wgT": np.ascontiguousarray(np.asarray(L0_ffn_w_gate, np.float32).T),
        "wuT": np.ascontiguousarray(np.asarray(L0_ffn_w_up, np.float32).T),
        "wdT": np.ascontiguousarray(np.asarray(L0_ffn_w_down, np.float32).T),
        "bq": np.ascontiguousarray(np.asarray(L0_attn_bq, np.float32)),
        "bk": np.ascontiguousarray(np.asarray(L0_attn_bk, np.float32)),
        "bv": np.ascontiguousarray(np.asarray(L0_attn_bv, np.float32)),
        "bo": np.ascontiguousarray(np.asarray(L0_attn_bo, np.float32)),
    }
    wt["eye"] = np.eye(128, dtype=np.float32)
    _, nc = build_core_program()
    in_maps = [dict(wt, x=np.ascontiguousarray(x[i])) for i in range(NCORES)]
    res = run_bass_kernel_spmd(nc, in_maps, core_ids=list(range(NCORES)))
    out = np.stack([np.asarray(res.results[i]["y"]) for i in range(NCORES)])
    return out.astype(np.float32)
